# version 07
# baseline (speedup 1.0000x reference)
import contextlib
import numpy as np
import concourse.bass as bass
import concourse.mybir as mybir
from concourse.bass_utils import run_bass_kernel_spmd

F32 = mybir.dt.float32
BF16 = mybir.dt.bfloat16
AF = mybir.ActivationFunctionType
ALU = mybir.AluOpType
AX = mybir.AxisListType

NCORES = 8
TOK = 1024
HALO = 3
TOKH = TOK + HALO
D = 2048
DI = 4096
NH = 64
NST = 64
S = 64
NCH = TOK // S
DIP = 8384
E = 8

class Sem:
    def __init__(self, handle):
        self.h = handle
        self.n = 0


class K:
    ENGS = ("tensor", "vector", "scalar", "gpsimd", "sync")

    def __init__(self, nc):
        self.nc = nc
        self.q = {e: [] for e in self.ENGS}
        self.dirty = {e: set() for e in self.ENGS}

    @staticmethod
    def _tnames(aps):
        out = set()
        for a in aps:
            if a is None or isinstance(a, (int, float)):
                continue
            t = getattr(a, "tensor", None)
            if t is not None:
                out.add(t.name)
        return out

    def op(self, eng, fn, reads=(), writes=(), inc=None, inc_k=1, wait=None,
           dma_inc=None):
        import traceback
        tag = None
        for fr in reversed(traceback.extract_stack()[:-1]):
            if fr.filename.endswith("kernel.py") and fr.name not in (
                    "op", "dma", "mm", "trans", "act", "v", "vdrain"):
                tag = f"{fr.name}:{fr.lineno}"
                break
        rn = self._tnames(reads)
        wn = self._tnames(writes)
        need_drain = bool(self.dirty[eng] & (rn | wn))
        waits = []
        if wait:
            for w in wait:
                waits.append((w.h, w.n) if isinstance(w, Sem) else (w[0].h, w[1]))
        if inc is not None:
            inc.n += inc_k
        if dma_inc is not None:
            dma_inc.n += 16
        if need_drain:
            self.dirty[eng] = set()

        def emit(h, nd=need_drain, waits=waits, fn=fn, inc=inc, inc_k=inc_k,
                 dma_inc=dma_inc, tag=tag):
            if nd:
                h.drain()
            for sh, sv in waits:
                h.wait_ge(sh, sv)
            try:
                ins = fn(h)
            except Exception as e:
                raise RuntimeError(f"emit failed at record site {tag}: {e}") from e
            if ins is not None:
                if inc is not None:
                    ins.then_inc(inc.h, inc_k)
                if dma_inc is not None:
                    ins.then_inc(dma_inc.h, 16)

        self.q[eng].append(emit)
        self.dirty[eng] |= wn

    def dma(self, eng, out, in_, inc=None, wait=None):
        self.op(eng, lambda h: h.dma_start(out=out, in_=in_),
                reads=(in_,), writes=(out,), dma_inc=inc, wait=wait)

    def mm3(self, out, lh, ll, rh, rl, start, stop, inc=None, wait=None):
        self.mm(out, lh, rh, start=start, stop=False, wait=wait)
        self.mm(out, lh, rl, start=False, stop=False)
        self.mm(out, ll, rh, start=False, stop=stop, inc=inc)

    def mm(self, out, lhsT, rhs, start, stop, inc=None, wait=None):
        self.op("tensor",
                lambda h: h.matmul(out, lhsT, rhs, start=start, stop=stop),
                reads=(lhsT, rhs), writes=(out,), inc=inc, wait=wait)

    def trans(self, out, in_, ident, inc=None, wait=None):
        self.op("tensor", lambda h: h.transpose(out, in_, ident),
                reads=(in_, ident), writes=(out,), inc=inc, wait=wait)

    def act(self, out, in_, func, bias=0.0, scale=1.0, accum_out=None, inc=None,
            wait=None):
        self.op("scalar",
                lambda h: h.activation(out, in_, func, bias=bias, scale=scale,
                                       accum_out=accum_out),
                reads=(in_, None if isinstance(bias, float) else bias,
                       None if isinstance(scale, float) else scale),
                writes=(out, accum_out), inc=inc, wait=wait)

    def v(self, method, out, *ins, inc=None, wait=None, **kw):
        def fn(h, method=method, out=out, ins=ins, kw=kw):
            return getattr(h, method)(out, *ins, **kw)
        self.op("vector", fn, reads=ins, writes=(out,), inc=inc, wait=wait)

    def vdrain(self, inc):
        self.op("vector", lambda h: h.drain(), inc=inc)
        self.dirty["vector"] = set()

    def emit(self):
        with self.nc.Block() as block:
            @block.tensor
            def _(h):
                for f in self.q["tensor"]:
                    f(h)

            @block.vector
            def _(h):
                for f in self.q["vector"]:
                    f(h)

            @block.scalar
            def _(h):
                for f in self.q["scalar"]:
                    f(h)

            @block.gpsimd
            def _(h):
                for f in self.q["gpsimd"]:
                    f(h)

            @block.sync
            def _(h):
                for f in self.q["sync"]:
                    f(h)


def build_program(debug=(), stage=1):
    nc = bass.Bass("TRN2", target_bir_lowering=False, debug=False)

    def dram_in(name, shape):
        return nc.dram_tensor(name, shape, F32, kind="ExternalInput").ap()

    u = dram_in("u", [TOKH, D])
    w_in = dram_in("w_in", [D, DIP])
    conv_w = dram_in("conv_w", [128, 32, 4])
    conv_b = dram_in("conv_b", [128, 32])
    a_log = dram_in("a_log", [NH, 1])
    dt_bias = dram_in("dt_bias", [NH, 1])
    d_rep = dram_in("d_rep", [128, NH])
    norm_scale = dram_in("norm_scale", [128, 32])
    w_out = dram_in("w_out", [DI, D])
    w1t = dram_in("w1t", [D, D])
    w3t = dram_in("w3t", [D, D])
    w2t = dram_in("w2t", [D, D])
    wrt = dram_in("wrt", [128, 16 * E])
    ident_in = dram_in("ident", [128, 128])
    maskl_in = dram_in("maskl", [128, S])
    combmaskinv = dram_in("combmaskinv", [64, NCORES])
    csmask_in = dram_in("csmask", [NH, TOK])
    combmask = dram_in("combmask", [64, NCORES])
    core0 = dram_in("core0", [128, 1])

    out = nc.dram_tensor("out", [TOK, D], F32, kind="ExternalOutput").ap()

    def scratch(name, shape, dbg=False, dtype=F32):
        if dbg:
            return nc.dram_tensor(name, shape, dtype, kind="ExternalOutput").ap()
        return nc.dram_tensor(name, shape, dtype).ap()

    xn_sp = scratch("xn_sp", [16, 128, TOKH], "xn" in debug)
    xnh_sp = scratch("xnh_sp", [16, 128, TOKH], False, BF16)
    xnl_sp = scratch("xnl_sp", [16, 128, TOKH], False, BF16)
    z_sp = scratch("z_sp", [32, 128, TOK], "z" in debug)
    bcdt_sp = scratch("bcdt_sp", [4, 64, TOK], "bcdt" in debug)
    vfm_sp = scratch("vfm_sp", [2, 32, 128, 512], "v" in debug)
    vtok_sp = scratch("vtok_sp", [NCH, 64, DI], "vtok" in debug)
    m_sp = scratch("m_sp", [NCH, 64, DI], "m" in debug)
    ft_sp = scratch("ft_sp", [65, DI], "ft" in debug)
    p_sp = scratch("p_sp", [8, D], "p" in debug)
    mg_sp = scratch("mg_sp", [128, 64], "p" in debug)
    h2_sp = scratch("h2_sp", [16, 128, TOK], "h2" in debug)
    f_bnc = nc.dram_tensor("f_bnc", [64, DI], F32).ap()
    t_bnc = nc.dram_tensor("t_bnc", [64, 1], F32).ap()
    f_gat = nc.dram_tensor("f_gat", [NCORES, 64, DI], F32).ap()
    t_gat = nc.dram_tensor("t_gat", [NCORES, 64, 1], F32).ap()
    csrow = nc.dram_tensor("csrow", [2, 2, NH * S], F32).ap()
    totrow = nc.dram_tensor("totrow", [2, 2, NH], F32).ap()
    rrow = nc.dram_tensor("rrow", [2, 1, S], F32).ap()
    p_bnc = nc.dram_tensor("p_bnc", [8, D], F32).ap()
    p_tot = nc.dram_tensor("p_tot", [8, D], F32).ap()
    y_sp = scratch("y_sp", [NCH, 64, DI], "y" in debug)
    att_sp = scratch("att_sp", [128, DI], "att" in debug)
    vt2_sp = scratch("vt2_sp", [128, DI], "att" in debug)
    hin_sp = scratch("hin_sp", [64, DI], "att" in debug)
    sml_sp = scratch("sml_sp", [128, 1152], "att" in debug)
    hq_sp = scratch("hq_sp", [64, DI], "att" in debug)

    k = K(nc)
    sems = {}

    def sem(name):
        s = Sem(stackctx.enter_context(nc.semaphore(name)))
        sems[name] = s
        return s

    stackctx = contextlib.ExitStack()
    with stackctx:
        def sbuf(name, shape, dtype=F32):
            return stackctx.enter_context(nc.sbuf_tensor(name, shape, dtype))

        def psum(name):
            return stackctx.enter_context(nc.psum_tensor(name, [128, 512], F32))

        s_const = sem("const")
        s_u = [sem("u0"), sem("u1")]
        s_wb = [sem("wb0"), sem("wb1")]
        s_zs = [sem(f"zs{i}") for i in range(4)]
        s_mm = sem("mm")
        s_ve = sem("ve")
        s_se = sem("se")
        s_sp = sem("sp")
        s_gld = sem("gld")
        s_cc = sem("cc")
        s_vt = sem("vt")
        s_ft = sem("ft")
        s_pb = sem("pbs")
        s_c2 = sem("c2")
        s_c3 = sem("c3")
        s_c4 = sem("c4")
        s_c5 = sem("c5")
        s_m = sem("m")
        s_xnr = sem("xnr")

        ident = sbuf("identt", [128, 128])
        xn_fm = sbuf("xn_fm", [128, 16, TOKH])
        _xnb = xn_fm.ap().rearrange("p a b -> p (a b)").bitcast(BF16)
        xn_hi = _xnb[:, 0:16 * TOKH].rearrange("p (a b) -> p a b", a=16)
        xn_lo = _xnb[:, 16 * TOKH:32 * TOKH].rearrange("p (a b) -> p a b", a=16)
        B_fm = sbuf("B_fm", [64, TOK])
        C_fm = sbuf("C_fm", [64, TOK])
        dt_fm = sbuf("dt_fm", [64, TOK])
        la_fm = sbuf("la_fm", [64, TOK])
        cs_fm = sbuf("cs_fm", [64, TOK])
        csmask = sbuf("csmask_t", [64, TOK])
        acol = sbuf("acol", [64, 1])
        dtb = sbuf("dtb", [64, 1])
        convw = sbuf("convw_t", [128, 32, 4])
        convb = sbuf("convb_t", [128, 32])
        v_fm = sbuf("v_fm", [128, 32, 512])
        wband = sbuf("wband", [128, 2, 512])
        wbh = sbuf("wbh", [128, 2, 512], BF16)
        wbl = sbuf("wbl", [128, 2, 512], BF16)
        ustage = sbuf("ustage", [128, 2, D])
        lnm = sbuf("lnm", [128, 2])
        consts1 = sbuf("consts1", [128, 1])
        lntmp = sbuf("lntmp", [128, D])
        xtmp = sbuf("xtmp", [128, 2, 515])
        zstage = sbuf("zstage", [128, 4, 512])
        h_loc = ustage.ap()[0:64].rearrange("p a b -> p (a b)")[:, 0:DI]
        xnflat = xn_fm.ap()[0:64].rearrange("p a b -> p (a b)")
        xnflat128 = xn_fm.ap().rearrange("p a b -> p (a b)")
        vt_sb = xnflat[:, 0:DI]
        m_sb = xnflat[:, DI:2 * DI]
        b_tok = sbuf("b_tok", [64, 64])
        e2s = sbuf("e2s", [64, 64])
        tmp2_fm = sbuf("tmp2_fm", [64, 64])
        ecs_tok = sbuf("ecs_tok", [64, 64])
        drep = sbuf("drep", [64, 64])
        tcol = sbuf("tcol", [64, 1])

        pb = [psum(f"pb{i}") for i in range(6)]
        ptr = [psum("pt0"), psum("pt1")]

        k.dma("sync", ident[:], ident_in, inc=s_const)
        k.dma("sync", csmask[:], csmask_in, inc=s_const)
        k.dma("sync", acol[:], a_log, inc=s_const)
        k.dma("sync", dtb[:], dt_bias, inc=s_const)
        k.dma("sync", convw[:], conv_w, inc=s_const)
        k.dma("sync", convb[:], conv_b, inc=s_const)
        k.op("vector", lambda h: h.memset(consts1[:], 1.0),
             writes=(consts1[:],))
        k.vdrain(s_ve)
        k.act(acol[:], acol[:], AF.Exp, wait=[s_const], inc=s_se)

        umarks = []
        semarks = []
        pmarks = {}
        pmarks2 = {}
        tposed = 0
        for t in range(9):
            rows = 128 if t < 8 else 3
            r0 = t * 128
            ub = ustage[:, t % 2]
            w = None
            if t >= 2:
                w = [(s_mm, umarks[t - 2]), (s_se, semarks[t - 2]),
                     (s_ve, pmarks2.get((t - 1) * 16 - 1, 0))]
            k.dma("sync", ub[0:rows, :], u[r0:r0 + rows, :], inc=s_u[t % 2],
                  wait=w)
            k.v("reduce_sum", lnm[0:rows, 0:1], ub[0:rows, :], axis=AX.X,
                wait=[s_u[t % 2]])
            k.v("tensor_scalar_mul", lnm[0:rows, 0:1], lnm[0:rows, 0:1], 1.0 / D)
            k.v("tensor_single_scalar", ub[0:rows, :], ub[0:rows, :],
                lnm[0:rows, 0:1], ALU.subtract)
            k.vdrain(s_ve)
            k.act(lntmp[0:rows, :], ub[0:rows, :], AF.Square,
                  accum_out=lnm[0:rows, 1:2], wait=[s_ve], inc=s_se)
            k.v("tensor_scalar", lnm[0:rows, 1:2], lnm[0:rows, 1:2], 1.0 / D,
                1e-5, ALU.mult, op1=ALU.add, wait=[s_se])
            k.vdrain(s_ve)
            k.act(lnm[0:rows, 1:2], lnm[0:rows, 1:2], AF.Sqrt, wait=[s_ve],
                  inc=s_se)
            k.v("reciprocal", lnm[0:rows, 1:2], lnm[0:rows, 1:2], wait=[s_se])
            k.v("tensor_single_scalar", ub[0:rows, :], ub[0:rows, :],
                lnm[0:rows, 1:2], ALU.mult)
            k.vdrain(s_ve)
            for kd in range(16):
                pslot = ptr[tposed % 2]
                tw = [s_ve] if kd == 0 else []
                if tposed >= 2:
                    tw = tw + [(s_se, pmarks[tposed - 2]),
                               (s_ve, pmarks2[tposed - 2])]
                k.trans(pslot[:, 0:rows], ub[0:rows, kd * 128:(kd + 1) * 128],
                        ident[0:rows, 0:rows], wait=tw or None, inc=s_mm)
                k.act(xn_hi[:, kd, r0:r0 + rows], pslot[:, 0:rows], AF.Copy,
                      wait=[s_mm], inc=s_se)
                k.v("tensor_sub", xn_lo[:, kd, r0:r0 + rows], pslot[:, 0:rows],
                    xn_hi[:, kd, r0:r0 + rows], wait=[s_se])
                k.vdrain(s_ve)
                pmarks[tposed] = s_se.n
                pmarks2[tposed] = s_ve.n
                tposed += 1
            umarks.append(s_mm.n)
            semarks.append(s_se.n)
        xn_done = (s_ve, s_ve.n)
        for kd in range(16):
            k.dma("sync", xnh_sp[kd], xn_hi[:, kd, :], inc=s_sp, wait=[xn_done])
            k.dma("sync", xnl_sp[kd], xn_lo[:, kd, :], inc=s_sp)
        xnsp_mark = (s_sp, s_sp.n)

        zmov = [(3, 0), (515, 3)]
        zslot_marks = []
        spmarks = {}
        bank_free = [(s_se, s_se.n), (s_ve, s_ve.n)]
        wmarks = []

        def proj_pass(tiles, evac, xn_cols=zmov, ncols=512):
            nonlocal bank_free, wmarks
            nt = len(tiles)
            for kd in range(16):
                c0 = tiles[0][0]
                c1 = tiles[-1][0] + tiles[-1][1]
                slot = len(wmarks) % 2
                wb = wband[:, slot, 0:c1 - c0]
                lw = [(s_ve, spmarks.get(len(wmarks) - 2, 0))] \
                    if len(wmarks) >= 2 else None
                k.dma("sync", wb, w_in[kd * 128:(kd + 1) * 128, c0:c1],
                      inc=s_wb[slot], wait=lw)
                spw = [(s_wb[slot], s_wb[slot].n)]
                if len(wmarks) >= 2:
                    spw.append((s_mm, wmarks[-2]))
                k.act(wbh[:, slot, 0:c1 - c0], wb, AF.Copy, wait=spw, inc=s_se)
                k.v("tensor_sub", wbl[:, slot, 0:c1 - c0], wb,
                    wbh[:, slot, 0:c1 - c0], wait=[s_se])
                k.vdrain(s_ve)
                spmarks[len(wmarks)] = s_ve.n
                split_done = (s_ve, s_ve.n)
                for mi, (wc0, wm) in enumerate(tiles):
                    lh = wbh[:, slot, wc0 - c0:wc0 - c0 + wm]
                    ll = wbl[:, slot, wc0 - c0:wc0 - c0 + wm]
                    for (x0, pbase) in xn_cols:
                        mmw = []
                        if mi == 0 and (x0, pbase) == xn_cols[0]:
                            mmw.append(split_done)
                            if kd == 0:
                                mmw += bank_free
                        k.mm3(pb[pbase + mi][0:wm, 0:ncols], lh, ll,
                              xn_hi[:, kd, x0:x0 + ncols],
                              xn_lo[:, kd, x0:x0 + ncols],
                              start=(kd == 0), stop=(kd == 15),
                              wait=mmw or None,
                              inc=s_mm if (mi == nt - 1 and (x0, pbase) == xn_cols[-1]) else None)
                wmarks.append(s_mm.n)
            evac()

        for g0 in range(0, 32, 3):
            gt = [(m * 128, 128) for m in range(g0, min(g0 + 3, 32))]

            def z_evac(g0=g0, gt=gt):
                nonlocal bank_free
                for mi in range(len(gt)):
                    m = g0 + mi
                    for hi in range(2):
                        slot = len(zslot_marks) % 4
                        zw = [(s_mm, s_mm.n)]
                        if len(zslot_marks) >= 4:
                            zw.append((s_zs[slot], zslot_marks[-4]))
                        k.act(zstage[:, slot], pb[hi * 3 + mi][:, 0:512],
                              AF.Copy, wait=zw, inc=s_se)
                        k.dma("sync", z_sp[m, :, hi * 512:hi * 512 + 512],
                              zstage[:, slot], inc=s_zs[slot], wait=[s_se])
                        zslot_marks.append(s_zs[slot].n)
                bank_free = [(s_se, s_se.n)]
            proj_pass(gt, z_evac)

        def bcdt_evac():
            nonlocal bank_free
            for mi, (nm, dest) in enumerate((("B", B_fm), ("C", C_fm),
                                             ("dt", dt_fm))):
                for hi in range(2):
                    src = pb[hi * 3 + mi][0:64, 0:512]
                    dc = hi * 512
                    if nm in ("B", "C"):
                        k.v("tensor_relu", dest[:, dc:dc + 512], src,
                            wait=[(s_mm, s_mm.n)])
                        k.v("tensor_sub", lntmp[0:64, 0:512], src,
                            dest[:, dc:dc + 512])
                        k.vdrain(s_ve)
                        k.act(lntmp[0:64, 512:1024], lntmp[0:64, 0:512], AF.Exp,
                              wait=[s_ve], inc=s_se)
                        k.v("tensor_add", dest[:, dc:dc + 512],
                            dest[:, dc:dc + 512], lntmp[0:64, 512:1024],
                            wait=[s_se])
                        k.vdrain(s_ve)
                    else:
                        t0 = lntmp[0:64, 0:512]
                        t1 = lntmp[0:64, 512:1024]
                        k.v("tensor_scalar_add", t0, src, dtb[:, 0:1],
                            wait=[(s_mm, s_mm.n)])
                        k.v("tensor_scalar_mul", t1, t0, -1.0)
                        k.v("tensor_tensor", t1, t1, t0, ALU.min)
                        k.vdrain(s_ve)
                        k.act(t1, t1, AF.Exp, wait=[s_ve])
                        k.op("scalar", lambda h: h.drain())
                        k.act(t1, t1, AF.Ln, bias=1.0 if False else consts1[0:64, 0:1],
                              inc=s_se)
                        k.v("tensor_relu", dest[:, dc:dc + 512], t0, wait=[s_se])
                        k.v("tensor_add", dest[:, dc:dc + 512],
                            dest[:, dc:dc + 512], t1)
                        k.vdrain(s_ve)
            bank_free = [(s_se, s_se.n), (s_ve, s_ve.n)]
        proj_pass([(8192, 64), (8256, 64), (8320, 64)], bcdt_evac)

        k.v("tensor_scalar", la_fm[:], dt_fm[:], acol[:, 0:1], -1.0,
            ALU.mult, op1=ALU.mult, wait=[s_se, s_ve])
        k.op("vector", lambda h: h.tensor_tensor_scan(
            cs_fm[:], csmask[:], la_fm[:], 0.0, ALU.mult, ALU.add),
            reads=(csmask[:], la_fm[:]), writes=(cs_fm[:],))
        k.vdrain(s_ve)
        if "bcdt" in debug:
            k.dma("sync", bcdt_sp[0], B_fm[:], inc=s_sp, wait=[s_ve])
            k.dma("sync", bcdt_sp[1], C_fm[:], inc=s_sp)
            k.dma("sync", bcdt_sp[2], dt_fm[:], inc=s_sp)
            k.dma("sync", bcdt_sp[3], cs_fm[:], inc=s_sp)

        xt_marks = {}
        ptm = {}
        mqm = {}
        mmvt = {}
        ttm = {}
        ttm2 = {}
        ptm2 = {}
        for half in range(2):
            xn0 = half * 512
            if half == 1:
                xw = [(s_ve, s_ve.n), (s_se, s_se.n), (s_mm, s_mm.n),
                      (s_vt, s_vt.n), (s_m, s_m.n)]
                for kd in range(16):
                    k.dma("sync", xn_hi[:, kd, :], xnh_sp[kd], inc=s_xnr,
                          wait=xw if kd == 0 else None)
                    k.dma("sync", xn_lo[:, kd, :], xnl_sp[kd], inc=s_xnr)
            vfree = None
            if half == 1 and "v" in debug:
                vfree = (s_sp, s_sp.n)
            for g0 in range(0, 32, 3):
                gt = [(4096 + m * 128, 128) for m in range(g0, min(g0 + 3, 32))]
                nt = len(gt)
                for kd in range(16):
                    c0 = gt[0][0]
                    c1 = gt[-1][0] + gt[-1][1]
                    slot = len(wmarks) % 2
                    wb = wband[:, slot, 0:c1 - c0]
                    lw = [(s_ve, spmarks.get(len(wmarks) - 2, 0))] \
                        if len(wmarks) >= 2 else None
                    k.dma("sync", wb, w_in[kd * 128:(kd + 1) * 128, c0:c1],
                          inc=s_wb[slot], wait=lw)
                    spw = [(s_wb[slot], s_wb[slot].n)]
                    if len(wmarks) >= 2:
                        spw.append((s_mm, wmarks[-2]))
                    k.act(wbh[:, slot, 0:c1 - c0], wb, AF.Copy, wait=spw,
                          inc=s_se)
                    k.v("tensor_sub", wbl[:, slot, 0:c1 - c0], wb,
                        wbh[:, slot, 0:c1 - c0], wait=[s_se])
                    k.vdrain(s_ve)
                    spmarks[len(wmarks)] = s_ve.n
                    split_done = (s_ve, s_ve.n)
                    for mi in range(nt):
                        wc0, wm = gt[mi]
                        lhsT = wb[:, wc0 - c0:wc0 - c0 + wm]
                        lh = wbh[:, slot, wc0 - c0:wc0 - c0 + wm]
                        ll = wbl[:, slot, wc0 - c0:wc0 - c0 + wm]
                        mmw = []
                        if mi == 0:
                            mmw.append(split_done)
                            if kd == 0:
                                mmw += bank_free + [(s_xnr, s_xnr.n)]
                        k.mm3(pb[mi][:, 0:512], lh, ll,
                              xn_hi[:, kd, xn0:xn0 + 512],
                              xn_lo[:, kd, xn0:xn0 + 512],
                              start=(kd == 0), stop=(kd == 15), wait=mmw or None)
                        k.mm3(pb[3 + mi][:, 0:3], lh, ll,
                              xn_hi[:, kd, xn0 + 512:xn0 + 515],
                              xn_lo[:, kd, xn0 + 512:xn0 + 515],
                              start=(kd == 0), stop=(kd == 15),
                              inc=s_mm if (mi == nt - 1) else None)
                    wmarks.append(s_mm.n)
                for mi in range(nt):
                    m = g0 + mi
                    xt = xtmp[:, m % 2]
                    xw = [(s_mm, s_mm.n)]
                    if (m - 2) in xt_marks:
                        xw.append((s_ve, xt_marks[m - 2]))
                    k.act(xt[:, 0:512], pb[mi][:, 0:512], AF.Copy, wait=xw)
                    k.act(xt[:, 512:515], pb[3 + mi][:, 0:3], AF.Copy, inc=s_se)
                    cw = convw[:, m]
                    vdst = v_fm[:, m, :]
                    vw = [s_se]
                    if vfree is not None:
                        vw.append(vfree)
                    k.v("tensor_scalar_mul", vdst, xt[:, 0:512], cw[:, 0:1],
                        wait=vw)
                    for kk in range(1, 4):
                        k.v("scalar_tensor_tensor", vdst, xt[:, kk:kk + 512],
                            cw[:, kk:kk + 1], vdst, ALU.mult, ALU.add)
                    k.vdrain(s_ve)
                    xt_marks[m] = s_ve.n
                    k.act(vdst, vdst, AF.Silu, bias=convb[:, m:m + 1],
                          wait=[s_ve], inc=s_se)
                bank_free = [(s_se, s_se.n)]
                if "v" in debug:
                    for mi in range(nt):
                        m = g0 + mi
                        k.dma("sync", vfm_sp[half, m], v_fm[:, m, :],
                              inc=s_sp, wait=[(s_se, s_se.n)])

            if half == 0:
                k.op("vector", lambda h: h.memset(h_loc, 0.0),
                     writes=(h_loc,), wait=[(s_mm, s_mm.n), (s_se, s_se.n)])
                k.vdrain(s_ve)
            vconsumed = []
            for c in range(8):
                gc = half * 8 + c
                csl = slice(gc * 64, gc * 64 + 64)
                for m in range(32):
                    pslot = ptr[m % 2]
                    tw = [(s_se, s_se.n)] if m < 2 else [(s_se, ptm[m - 2])]
                    if m == 0:
                        tw = tw + [(s_mm, mmvt.get("x", 0))]
                    ew_extra = [xnsp_mark] if (m == 0 and c == 0 and half == 0) else []
                    k.trans(pslot[0:64, 0:128], v_fm[:, m, c * 64:(c + 1) * 64],
                            ident[:], wait=tw, inc=s_mm)
                    ew = [s_mm] if m > 0 else [s_mm, (s_ve, s_ve.n)]
                    k.act(vt_sb[:, m * 128:(m + 1) * 128], pslot[0:64, 0:128],
                          AF.Copy, wait=ew + ew_extra, inc=s_se)
                    ptm[m] = s_se.n
                vt_done = (s_se, s_se.n)
                vconsumed.append(s_mm.n)
                k.dma("sync", vtok_sp[gc], vt_sb, inc=s_vt, wait=[vt_done])
                k.v("tensor_scalar", tmp2_fm[:], cs_fm[:, csl],
                    cs_fm[:, gc * 64 + 63:gc * 64 + 64], -1.0, ALU.subtract,
                    op1=ALU.mult, wait=[vt_done])
                k.vdrain(s_ve)
                k.trans(ptr[0][0:64, 0:64], tmp2_fm[:], ident[0:64, 0:64],
                        wait=[s_ve], inc=s_mm)
                k.trans(ptr[1][0:64, 0:64], B_fm[:, csl], ident[0:64, 0:64],
                        inc=s_mm)
                k.trans(ptr[0][0:64, 64:128], cs_fm[:, csl], ident[0:64, 0:64],
                        inc=s_mm)
                k.act(e2s[:], ptr[0][0:64, 0:64], AF.Exp, wait=[s_mm])
                k.act(b_tok[:], ptr[1][0:64, 0:64], AF.Copy)
                
                k.act(ecs_tok[:], ptr[0][0:64, 64:128], AF.Exp, inc=s_se)
                ptm[0] = s_se.n
                ptm[1] = s_se.n
                k.v("tensor_mul", drep[:], e2s[:], ecs_tok[:],
                    wait=[s_se])
                k.v("tensor_tensor",
                    vt_sb.rearrange("t (h p) -> t h p", h=64),
                    vt_sb.rearrange("t (h p) -> t h p", h=64),
                    e2s[:, :, None].broadcast_to([64, 64, 64]), ALU.mult,
                    wait=[s_vt])
                k.vdrain(s_ve)
                for q in range(8):
                    k.mm(pb[q % 6][0:64, 0:512], b_tok[:],
                         vt_sb[:, q * 512:(q + 1) * 512], start=True,
                         stop=True,
                         wait=[s_ve] if q == 0 else ([(s_se, mqm[q - 6])] if q >= 6 else None),
                         inc=s_mm)
                    macw = [s_mm]
                    if q == 0:
                        macw.append((s_m, s_m.n))
                    k.act(m_sb[:, q * 512:(q + 1) * 512], pb[q % 6][0:64, 0:512],
                          AF.Copy, wait=macw, inc=s_se)
                    mqm[q] = s_se.n
                mmvt["x"] = s_mm.n
                m_done = (s_se, s_se.n)
                k.dma("sync", m_sp[gc], m_sb, inc=s_m, wait=[m_done])
                k.v("tensor_tensor",
                    h_loc.rearrange("n (h p) -> n h p", h=64),
                    h_loc.rearrange("n (h p) -> n h p", h=64),
                    drep[:, :, None].broadcast_to([64, 64, 64]), ALU.mult,
                    wait=[m_done])
                k.v("tensor_add", h_loc, h_loc, m_sb[:])
                k.vdrain(s_ve)
                mspill_scan = [(s_m, s_m.n), (s_ve, s_ve.n)]
            bank_free = bank_free + [(s_se, s_se.n), (s_ve, s_ve.n)]

        k.v("reduce_sum", tcol[:], la_fm[:], axis=AX.X, wait=[(s_se, s_se.n)])
        k.vdrain(s_ve)
        k.act(tcol[:], tcol[:], AF.Exp, wait=[s_ve], inc=s_se)
        k.dma("sync", f_bnc, h_loc, inc=s_ft, wait=[(s_ve, s_ve.n)])
        k.dma("sync", t_bnc, tcol[:], inc=s_ft, wait=[(s_se, s_se.n)])
        if "ft" in debug:
            k.dma("sync", ft_sp[0:64, :], h_loc, inc=s_sp,
                  wait=[(s_ve, s_ve.n)])
            k.dma("sync", ft_sp[64:65, 0:64], tcol[:, 0:1].rearrange("a b -> b a") if False else tcol[0:64, 0:1],
                  inc=s_sp, wait=[(s_se, s_se.n)])
        cc_ready = (s_ft, s_ft.n)
        k.op("gpsimd", lambda h: h.collective_compute(
            "AllGather", ALU.bypass, replica_groups=[list(range(NCORES))],
            ins=[f_bnc.opt()], outs=[f_gat.opt()]), wait=[cc_ready], inc=s_cc)
        k.op("gpsimd", lambda h: h.collective_compute(
            "AllGather", ALU.bypass, replica_groups=[list(range(NCORES))],
            ins=[t_bnc.opt()], outs=[t_gat.opt()]), wait=None, inc=s_cc)

        wrt_sb = sbuf("wrt_sb", [128, 16 * E])
        wrt_h = sbuf("wrt_h", [128, 16 * E], BF16)
        wrt_l = sbuf("wrt_l", [128, 16 * E], BF16)
        mg_sb = sbuf("mg_sb", [128, 8, E])
        lgt = sbuf("lgt", [128, E + 4])
        p_part = zstage.ap()[0:8].rearrange("p a b -> p (a b)")[:, 0:D]
        k.dma("sync", wrt_sb[:], wrt, inc=s_c2)
        xw = [(s_ve, s_ve.n), (s_se, s_se.n), (s_mm, s_mm.n),
              (s_vt, s_vt.n), (s_m, s_m.n)]
        for kd in range(16):
            k.dma("sync", xn_hi[:, kd, :], xnh_sp[kd], inc=s_xnr,
                  wait=xw if kd == 0 else None)
            k.dma("sync", xn_lo[:, kd, :], xnl_sp[kd], inc=s_xnr)
        k.act(wrt_h[:], wrt_sb[:], AF.Copy, wait=[(s_c2, s_c2.n)], inc=s_se)
        k.v("tensor_sub", wrt_l[:], wrt_sb[:], wrt_h[:], wait=[s_se])
        k.vdrain(s_ve)
        wrt_split = (s_ve, s_ve.n)
        for tt in range(8):
            tc = 3 + tt * 128
            for kd in range(16):
                k.mm3(pb[0][0:128, 0:E],
                      xn_hi[:, kd, tc:tc + 128], xn_lo[:, kd, tc:tc + 128],
                      wrt_h[:, kd * E:(kd + 1) * E],
                      wrt_l[:, kd * E:(kd + 1) * E],
                      start=(kd == 0), stop=(kd == 15),
                      wait=([(s_xnr, s_xnr.n), wrt_split] + bank_free +
                            [(s_se, s_se.n)]) if (tt == 0 and kd == 0) else
                           ([(s_se, ttm[tt - 1])] if kd == 0 and tt > 0 else None),
                      inc=s_mm if kd == 15 else None)
            lcw = [s_mm] + ([(s_ve, ttm2[tt - 1])] if tt > 0 else [])
            k.act(lgt[:, 0:E], pb[0][:, 0:E], AF.Copy, wait=lcw, inc=s_se)
            k.v("reduce_max", lgt[:, E:E + 1], lgt[:, 0:E], axis=AX.X,
                wait=[s_se])
            k.v("tensor_single_scalar", mg_sb[:, tt], lgt[:, 0:E],
                lgt[:, E:E + 1], ALU.is_equal)
            k.vdrain(s_ve)
            k.v("scalar_tensor_tensor", lgt[:, 0:E], mg_sb[:, tt], -1e30,
                lgt[:, 0:E], ALU.mult, ALU.add)
            k.v("reduce_max", lgt[:, E + 1:E + 2], lgt[:, 0:E], axis=AX.X)
            k.vdrain(s_ve)
            k.v("tensor_single_scalar", lgt[:, 0:E], lgt[:, 0:E],
                lgt[:, E + 1:E + 2], ALU.is_equal)
            k.v("tensor_sub", lgt[:, E + 2:E + 3], lgt[:, E:E + 1],
                lgt[:, E + 1:E + 2])
            k.vdrain(s_ve)
            k.act(lgt[:, E + 2:E + 3], lgt[:, E + 2:E + 3], AF.Sigmoid,
                  wait=[s_ve], inc=s_se)
            k.v("tensor_scalar", lgt[:, E + 3:E + 4], lgt[:, E + 2:E + 3],
                -1.0, 1.0, ALU.mult, op1=ALU.add, wait=[s_se])
            k.vdrain(s_ve)
            k.v("tensor_scalar", mg_sb[:, tt], mg_sb[:, tt],
                lgt[:, E + 2:E + 3], None, ALU.mult, wait=[s_ve])
            k.v("scalar_tensor_tensor", mg_sb[:, tt], lgt[:, 0:E],
                lgt[:, E + 3:E + 4], mg_sb[:, tt], ALU.mult, ALU.add)
            k.vdrain(s_ve)
            ttm[tt] = s_se.n
            ttm2[tt] = s_ve.n
        mg_done = (s_ve, s_ve.n)
        if "p" in debug:
            k.dma("sync", mg_sp,
                  mg_sb.ap().rearrange("p a b -> p (a b)"), inc=s_sp,
                  wait=[mg_done])

        h1v = v_fm.ap().rearrange("p a b -> p (a b)")
        _vb = h1v.bitcast(BF16)
        h1h = _vb[:, 0:16384]
        h1l = _vb[:, 16384:32768]
        _xb = xnflat128.bitcast(BF16)
        h2h = _xb[:, 0:16384]
        h2l = _xb[:, 16384:32768]
        h2v = xnflat128
        ftmp = sbuf("ftmp", [128, 2, 512])

        def ffn_fm(wsrc, movs, dest_fn, first_wait):
            for g0 in range(0, 16, 3):
                gt = list(range(g0, min(g0 + 3, 16)))
                nt = len(gt)
                for kd in range(16):
                    slot = len(wmarks) % 2
                    wb = wband[:, slot, 0:nt * 128]
                    lw = [(s_ve, spmarks.get(len(wmarks) - 2, 0))] \
                        if len(wmarks) >= 2 else None
                    k.dma("sync", wb,
                          wsrc[kd * 128:(kd + 1) * 128,
                               g0 * 128:g0 * 128 + nt * 128],
                          inc=s_wb[slot], wait=lw)
                    spw = [(s_wb[slot], s_wb[slot].n)]
                    if len(wmarks) >= 2:
                        spw.append((s_mm, wmarks[-2]))
                    k.act(wbh[:, slot, 0:nt * 128], wb, AF.Copy, wait=spw,
                          inc=s_se)
                    k.v("tensor_sub", wbl[:, slot, 0:nt * 128], wb,
                        wbh[:, slot, 0:nt * 128], wait=[s_se])
                    k.vdrain(s_ve)
                    spmarks[len(wmarks)] = s_ve.n
                    split_done = (s_ve, s_ve.n)
                    for mi in range(nt):
                        lh = wbh[:, slot, mi * 128:(mi + 1) * 128]
                        ll = wbl[:, slot, mi * 128:(mi + 1) * 128]
                        for hi in range(2):
                            mmw = []
                            if mi == 0 and hi == 0:
                                mmw.append(split_done)
                                if kd == 0:
                                    mmw += bank_free2[0]
                                    if g0 == 0:
                                        mmw += first_wait
                            mh, ml = movs(kd, hi)
                            k.mm3(pb[hi * 3 + mi][:, 0:512], lh, ll, mh, ml,
                                  start=(kd == 0), stop=(kd == 15),
                                  wait=mmw or None,
                                  inc=s_mm if (mi == nt - 1 and hi == 1) else None)
                    wmarks.append(s_mm.n)
                for mi in range(nt):
                    m = g0 + mi
                    for hi in range(2):
                        dest_fn(m, hi, pb[hi * 3 + mi][:, 0:512])
                bank_free2[0] = [(s_se, s_se.n), (s_ve, s_ve.n)]

        def silu_split_dest(hh_v, hl_v):
            def dest(m, hi, psum_ap, hh_v=hh_v, hl_v=hl_v):
                ft = ftmp[:, (m * 2 + hi) % 2]
                o0 = m * 1024 + hi * 512
                k.act(ft, psum_ap, AF.Silu,
                      wait=[(s_mm, s_mm.n), (s_ve, s_ve.n)], inc=s_se)
                k.act(hh_v[:, o0:o0 + 512], ft, AF.Copy, inc=s_se)
                k.v("tensor_sub", hl_v[:, o0:o0 + 512], ft,
                    hh_v[:, o0:o0 + 512], wait=[s_se])
                k.vdrain(s_ve)
            return dest

        def copy_dest(m, hi, psum_ap):
            o0 = m * 1024 + hi * 512
            k.act(h1v[:, o0:o0 + 512], psum_ap, AF.Copy,
                  wait=[(s_mm, s_mm.n)], inc=s_se)

        bank_free2 = [bank_free + [mg_done, (s_se, s_se.n)]]
        ffn_fm(w1t, lambda kd, hi: (xn_hi[:, kd, 3 + hi * 512:3 + hi * 512 + 512],
                                    xn_lo[:, kd, 3 + hi * 512:3 + hi * 512 + 512]),
               silu_split_dest(h1h, h1l), [])
        ffn_fm(w3t, lambda kd, hi: (h1h[:, kd * 1024 + hi * 512:kd * 1024 + hi * 512 + 512],
                                    h1l[:, kd * 1024 + hi * 512:kd * 1024 + hi * 512 + 512]),
               silu_split_dest(h2h, h2l), [(s_mm, s_mm.n)])
        if "h2" in debug:
            for m in range(16):
                k.dma("sync", h2_sp[m],
                      h2h[:, m * 1024:(m + 1) * 1024], inc=s_sp,
                      wait=[(s_ve, s_ve.n)])
        ffn_fm(w2t, lambda kd, hi: (h2h[:, kd * 1024 + hi * 512:kd * 1024 + hi * 512 + 512],
                                    h2l[:, kd * 1024 + hi * 512:kd * 1024 + hi * 512 + 512]),
               copy_dest, [(s_mm, s_mm.n)])
        htok = lntmp.ap()
        hw0 = [(s_se, s_se.n), (s_ve, s_ve.n)]
        for tt in range(8):
            for dtile in range(16):
                pslot = ptr[dtile % 2]
                tw = ([(s_se, ptm2[(tt, dtile - 2)])] if dtile >= 2
                      else [(s_se, ptm2.get((tt - 1, 14 + dtile), s_se.n))])
                if tt == 0 and dtile < 2:
                    tw = [(s_se, s_se.n)]
                if tt == 0 and dtile == 0:
                    tw += hw0
                k.trans(pslot[:, 0:128],
                        h1v[:, dtile * 1024 + tt * 128:dtile * 1024 + tt * 128 + 128],
                        ident[:], wait=tw, inc=s_mm)
                k.act(htok[:, dtile * 128:(dtile + 1) * 128], pslot[:, 0:128],
                      AF.Copy, wait=[s_mm], inc=s_se)
                ptm2[(tt, dtile)] = s_se.n
            for q in range(4):
                k.mm(pb[q][0:8, 0:512], mg_sb[:, tt],
                     htok[:, q * 512:(q + 1) * 512],
                     start=(tt == 0), stop=(tt == 7),
                     wait=[(s_se, s_se.n)] if q == 0 else None,
                     inc=s_mm if q == 3 else None)
        for q in range(4):
            k.act(p_part[:, q * 512:(q + 1) * 512], pb[q][0:8, 0:512], AF.Copy,
                  wait=[s_mm] + ([(sz, sz.n) for sz in s_zs] if q == 0 else []),
                  inc=s_se)
        if "p" in debug:
            k.dma("sync", p_sp, p_part, inc=s_sp, wait=[(s_se, s_se.n)])

        k.dma("sync", p_bnc, p_part, inc=s_pb, wait=[(s_se, s_se.n)])
        pcc_ready = (s_pb, s_pb.n)
        k.op("gpsimd", lambda h: h.collective_compute(
            "AllReduce", ALU.add, replica_groups=[list(range(NCORES))],
            ins=[p_bnc.opt()], outs=[p_tot.opt()]), wait=[pcc_ready], inc=s_cc)

        hbuf = h_loc
        SPc = 3 * DI + 2048
        trep = xnflat[:, SPc + 768:SPc + 832]
        tm_t = xnflat[:, SPc + 832:SPc + 896]
        cmask = mg_sb.ap()[0:64].rearrange("p a b -> p (a b)")[:, 32:40]
        cmaski = mg_sb.ap()[0:64].rearrange("p a b -> p (a b)")[:, 40:48]
        fbufs = [xnflat[:, 0:DI], xnflat[:, DI:2 * DI]]
        s_f = [sem("f0"), sem("f1")]
        cmw = [(s_mm, s_mm.n), (s_se, s_se.n), (s_ve, s_ve.n)]
        k.dma("sync", cmask, combmask, inc=s_c3, wait=cmw)
        k.dma("sync", cmaski, combmaskinv, inc=s_c3)
        k.op("vector", lambda h: h.memset(hbuf, 0.0), writes=(hbuf,),
             wait=[(s_mm, s_mm.n), (s_se, s_se.n), (s_sp, s_sp.n),
                   (s_ft, s_ft.n)])
        k.vdrain(s_ve)
        fjm = {}
        for j in range(NCORES - 1):
            fb = fbufs[j % 2]
            fw = [(s_cc, 2), (s_mm, s_mm.n), (s_se, s_se.n)]
            if j >= 2:
                fw.append((s_ve, fjm[j - 2]))
            k.dma("sync", fb, f_gat[j], inc=s_f[j % 2], wait=fw)
            k.dma("sync", trep,
                  t_gat[j].rearrange("h one -> one h").broadcast_to([64, 64]),
                  inc=s_f[j % 2],
                  wait=[(s_ve, fjm[j - 1])] if j >= 1 else None)
            k.v("tensor_scalar", tm_t, trep, cmask[:, j:j + 1],
                cmaski[:, j:j + 1], ALU.mult, op1=ALU.add,
                wait=[(s_f[j % 2], s_f[j % 2].n), (s_c3, s_c3.n)])
            k.vdrain(s_ve)
            k.v("tensor_tensor", hbuf.rearrange("n (h p) -> n h p", h=64),
                hbuf.rearrange("n (h p) -> n h p", h=64),
                tm_t[:, :, None].broadcast_to([64, 64, 64]), ALU.mult)
            k.v("scalar_tensor_tensor", hbuf, fb, cmask[:, j:j + 1], hbuf,
                ALU.mult, ALU.add)
            k.vdrain(s_ve)
            fjm[j] = s_ve.n
        hin_done = (s_ve, s_ve.n)

        vtk2 = xnflat128[:, 0:DI]
        mrgY = xnflat128[:, DI:2 * DI]
        mrgA = xnflat[:, DI:2 * DI]
        att2 = xnflat128[:, 2 * DI:3 * DI]
        zb2 = xnflat128[:, 3 * DI:4 * DI]
        wob = [xnflat128[:, 0:2048], xnflat128[:, 2048:4096]]
        _mb = xnflat128[:, DI:2 * DI].bitcast(BF16)
        wobh = [_mb[:, 0:2048], _mb[:, 2048:4096]]
        wobl = [_mb[:, 4096:6144], _mb[:, 6144:8192]]
        _gb = v_fm.ap().rearrange("p a b -> p (a b)").bitcast(BF16)
        gyh = _gb[:, 0:16384].rearrange("p (a b) -> p a b", a=32)
        gyl = _gb[:, 16384:32768].rearrange("p (a b) -> p a b", a=32)
        gtmp = sbuf("gtmp", [128, 2, 128])
        lnf = lntmp.ap()
        tmpq = lnf[:, 0:512]
        masklv = lnf[:, 512:576]
        gm2 = lnf[:, 576:640]
        drepA = lnf[0:64, 640:704]
        drepB = lnf[0:64, 1088:1152]
        cstk2 = lnf[:, 704:768]
        ecs2 = lnf[:, 768:832]
        drpDv = lnf[:, 896:960]
        nsc = mg_sb.ap().rearrange("p a b -> p (a b)")[:, 0:32]
        rcol = mg_sb.ap().rearrange("p a b -> p (a b)")[:, 48:50]
        smw = [(s_mm, s_mm.n), (s_se, s_se.n), (s_ve, s_ve.n)]
        k.dma("sync", nsc, norm_scale, inc=s_c4, wait=smw)
        s_vl = sem("vl")
        s_ml = sem("ml")
        s_zl = sem("zl")
        s_cb = [sem("cb0"), sem("cb1")]
        s_cbr = sem("cbr")
        s_dr = sem("dr")
        s_ur = sem("ur")
        s_ot = sem("ot")
        marks = {"amm": {}, "yev": {}, "scan": {}, "tp": {}, "gy": {},
                 "zl": {}, "sq": {}, "sm": {}, "ml": {}}
        gy_fm = v_fm
        for half in range(2):
            mlw = [(s_ve, s_ve.n), (s_se, s_se.n)] if half == 0 else \
                  [(s_ot, s_ot.n), (s_ve, s_ve.n)]
            k.dma("sync", masklv, maskl_in, inc=s_c5, wait=mlw)
            k.dma("sync", drpDv, d_rep, inc=s_c5)
            mklm = (s_c5, s_c5.n)
            for p in range(4):
                ga = half * 8 + 2 * p
                gb = ga + 1
                sla = slice(ga * 64, ga * 64 + 64)
                slb = slice(gb * 64, gb * 64 + 64)
                slab = slice(ga * 64, ga * 64 + 128)
                pp = half * 4 + p
                if pp == 0:
                    vw = [hin_done, (s_mm, s_mm.n), (s_cb[0], s_cb[0].n)]
                elif p == 0:
                    vw = [(s_mm, s_mm.n), (s_ve, s_ve.n)]
                else:
                    vw = [(s_mm, marks["amm"][pp - 1]),
                          (s_ve, marks["yev"][pp - 1]),
                          (s_se, marks["sq"].get(pp - 1, 0))]
                k.dma("sync", vtk2,
                      vtok_sp.rearrange("c s d -> (c s) d")[ga * 64:ga * 64 + 128, :],
                      inc=s_vl, wait=vw)
                mw = [hin_done] if pp == 0 else \
                     [(s_ve, marks["scan"][pp - 1]), (s_mm, marks["tp"].get(pp - 1, 0)),
                      (s_se, marks["sq"].get(pp - 1, 0))]
                k.dma("sync", mrgA, m_sp[ga], inc=s_ml, wait=mw)
                zw = [(s_se, marks["zl"][pp - 1]),
                      (s_ve, marks["gy"][(pp - 1, 31)])] if pp > 0 else \
                     [(s_mm, s_mm.n), hin_done]
                k.dma("gpsimd", zb2.rearrange("p (m t) -> p m t", m=32),
                      z_sp[:, :, ga * 64:ga * 64 + 128].rearrange("m p t -> p m t"),
                      inc=s_zl, wait=zw)
                cw = [(s_cb[p % 2], s_cb[p % 2].n)]
                k.dma("gpsimd", csrow[p % 2, 0], cs_fm[:, sla],
                      inc=s_cb[p % 2], wait=cw)
                k.dma("gpsimd", csrow[p % 2, 1], cs_fm[:, slb],
                      inc=s_cb[p % 2])
                k.dma("gpsimd", totrow[p % 2, 0],
                      cs_fm[:, ga * 64 + 63:ga * 64 + 64], inc=s_cb[p % 2])
                k.dma("gpsimd", totrow[p % 2, 1],
                      cs_fm[:, gb * 64 + 63:gb * 64 + 64], inc=s_cb[p % 2])
                rw = [(s_cb[p % 2], s_cb[p % 2].n)]
                if pp > 0:
                    rw.append((s_mm, marks["amm"][pp - 1]))
                k.dma("gpsimd", att2[0:64, :],
                      csrow[p % 2, 0:1].rearrange('a b -> (a b)')[None, :].broadcast_to([64, NH * S]) if False else csrow[p % 2, 0:1, :].rearrange('a f -> a f').broadcast_to([64, NH * S]),
                      inc=s_cbr, wait=rw)
                k.dma("gpsimd", att2[64:128, :],
                      csrow[p % 2, 1:2, :].broadcast_to([64, NH * S]), inc=s_cbr)
                csb_ld = (s_cbr, s_cbr.n)
                sw = [(s_ve, s_ve.n)]
                daw = [(s_cb[p % 2], s_cb[p % 2].n),
                       (s_ve, marks["scan"].get(pp - 1, 0))]
                if p == 0:
                    daw += [(s_ot, s_ot.n), (s_ve, s_ve.n), (s_ur, s_ur.n)]
                k.dma("gpsimd", drepA,
                      totrow[p % 2, 0:1, :].broadcast_to([64, NH]),
                      inc=s_dr, wait=daw)
                k.dma("gpsimd", drepB,
                      totrow[p % 2, 1:2, :].broadcast_to([64, NH]), inc=s_dr)
                tpw = sw + [(s_se, marks["sm"].get(pp - 1, 0))]
                k.trans(ptr[1][:, 0:64], cs_fm[:, slab], ident[0:64, 0:64],
                        wait=tpw, inc=s_mm)
                k.mm(pb[0][0:64, 0:64], B_fm[:, sla], C_fm[:, sla],
                     start=True, stop=True)
                k.mm(pb[2][64:128, 0:64], B_fm[:, slb], C_fm[:, slb],
                     start=True, stop=True, inc=s_mm)
                k.act(drepA, drepA, AF.Exp,
                      wait=[s_mm, (s_dr, s_dr.n), (s_ve, s_ve.n)])
                k.act(drepB, drepB, AF.Exp)
                k.act(cstk2[:], ptr[1][:, 0:64], AF.Copy)
                k.act(ecs2[:], ptr[1][:, 0:64], AF.Exp)
                k.act(gm2[0:64, :], pb[0][0:64, 0:64], AF.Copy)
                k.act(gm2[64:128, :], pb[2][64:128, 0:64], AF.Copy, inc=s_se)
                marks["sm"][pp] = s_se.n
                a3 = att2.rearrange("t (h s) -> t h s", h=NH)
                k.v("tensor_tensor", a3, a3,
                    masklv[:, None, :].broadcast_to([128, NH, S]), ALU.add,
                    wait=[csb_ld, mklm])
                k.v("tensor_tensor", a3, a3,
                    cstk2[:, :, None].broadcast_to([128, NH, S]), ALU.subtract,
                    wait=[(s_se, marks["sm"][pp])])
                k.vdrain(s_ve)
                k.act(att2, att2, AF.Exp, wait=[s_ve], inc=s_se)
                k.v("tensor_tensor", a3, a3,
                    gm2[:, None, :].broadcast_to([128, NH, S]), ALU.mult,
                    wait=[s_se])
                k.vdrain(s_ve)
                att_done = (s_ve, s_ve.n)
                if "att" in debug and pp == 0:
                    k.dma("sync", att_sp, att2, inc=s_sp, wait=[att_done])
                    k.dma("sync", vt2_sp, vtk2, inc=s_sp,
                          wait=[(s_vl, s_vl.n)])
                    k.dma("sync", hin_sp, hbuf, inc=s_sp, wait=[hin_done])
                    k.dma("sync", sml_sp, lnf[:, 0:1152], inc=s_sp,
                          wait=[(s_se, s_se.n), (s_ve, s_ve.n)])
                for q in range(4):
                    qs = q * 1024
                    for bh in range(2):
                        qq = qs + bh * 512
                        w0 = None
                        if q == 0 and bh == 0:
                            w0 = [att_done, (s_vl, s_vl.n), (s_ml, s_ml.n),
                                  (s_se, marks["sm"][pp])]
                            if pp > 0:
                                w0 = w0 + [(s_ve, marks["yev"][pp - 1])]
                        elif bh == 0 and q > 0:
                            w0 = [(s_ve, marks["yev"][(pp, q - 1)])]
                        k.mm(pb[bh][0:64, 0:512], C_fm[:, sla],
                             hbuf[:, qq:qq + 512], start=True, stop=True,
                             wait=w0)
                        for hh in range(8):
                            hid = q * 16 + bh * 8 + hh
                            k.mm(pb[4 + bh][0:64, hh * 64:hh * 64 + 64],
                                 att2[0:64, hid * 64:hid * 64 + 64],
                                 vtk2[0:64, hid * 64:hid * 64 + 64],
                                 start=(hh == 0), stop=(hh == 7))
                            k.mm(ptr[bh][64:128, hh * 64:hh * 64 + 64],
                                 att2[64:128, hid * 64:hid * 64 + 64],
                                 vtk2[64:128, hid * 64:hid * 64 + 64],
                                 start=(hh == 0), stop=(hh == 7),
                                 inc=s_mm if hh == 7 else None)
                    interE = s_mm.n
                    k.v("tensor_tensor",
                        hbuf[:, qs:qs + 1024].rearrange("n (h p) -> n h p", h=16),
                        hbuf[:, qs:qs + 1024].rearrange("n (h p) -> n h p", h=16),
                        drepA[:, q * 16:q * 16 + 16][:, :, None]
                        .broadcast_to([64, 16, 64]), ALU.mult,
                        wait=[(s_mm, interE), (s_ml, s_ml.n),
                              (s_se, marks["sm"][pp])])
                    k.v("tensor_add", hbuf[:, qs:qs + 1024],
                        hbuf[:, qs:qs + 1024], mrgA[:, qs:qs + 1024])
                    k.vdrain(s_ve)
                    scanE = s_ve.n
                    k.dma("sync", mrgA[:, qs:qs + 1024],
                          m_sp[gb, :, qs:qs + 1024], inc=s_ml,
                          wait=[(s_ve, scanE)])
                    for bh in range(2):
                        qq = qs + bh * 512
                        k.mm(pb[2 + bh][64:128, 0:512], C_fm[:, slb],
                             hbuf[:, qq:qq + 512], start=True, stop=True,
                             wait=[(s_ve, scanE)] if bh == 0 else None,
                             inc=s_mm if bh == 1 else None)
                    k.v("tensor_tensor",
                        hbuf[:, qs:qs + 1024].rearrange("n (h p) -> n h p", h=16),
                        hbuf[:, qs:qs + 1024].rearrange("n (h p) -> n h p", h=16),
                        drepB[:, q * 16:q * 16 + 16][:, :, None]
                        .broadcast_to([64, 16, 64]), ALU.mult,
                        wait=[(s_mm, s_mm.n), (s_ml, s_ml.n)])
                    k.v("tensor_add", hbuf[:, qs:qs + 1024],
                        hbuf[:, qs:qs + 1024], mrgA[:, qs:qs + 1024])
                    k.vdrain(s_ve)
                    marks["scan"][pp] = s_ve.n
                    for bh in range(2):
                        qq = qs + bh * 512
                        hsl = slice(q * 16 + bh * 8, q * 16 + bh * 8 + 8)
                        for half_rows, pi, pa in ((slice(0, 64), pb[bh], pb[4 + bh]),
                                                  (slice(64, 128), pb[2 + bh], ptr[bh])):
                            dst = mrgY[half_rows, qq:qq + 512]
                            tq = tmpq[half_rows, :]
                            k.v("tensor_tensor",
                                dst.rearrange("t (h p) -> t h p", h=8),
                                pi[half_rows, 0:512].rearrange("t (h p) -> t h p", h=8),
                                ecs2[half_rows, hsl][:, :, None]
                                .broadcast_to([64, 8, 64]), ALU.mult,
                                wait=[(s_mm, s_mm.n), (s_ve, marks["scan"][pp]),
                                      (s_c5, s_c5.n)])
                            k.v("tensor_add", dst, dst, pa[half_rows, 0:512])
                            k.v("tensor_tensor",
                                tq.rearrange("t (h p) -> t h p", h=8),
                                vtk2[half_rows, qq:qq + 512]
                                .rearrange("t (h p) -> t h p", h=8),
                                drpDv[half_rows, hsl][:, :, None]
                                .broadcast_to([64, 8, 64]), ALU.mult)
                            k.v("tensor_add", dst, dst, tq)
                            k.vdrain(s_ve)
                    marks["yev"][(pp, q)] = s_ve.n
                marks["amm"][pp] = s_mm.n
                marks["yev"][pp] = s_ve.n
                marks["ml"][pp] = s_ml.n
                if "y" in debug:
                    k.dma("sync", y_sp[ga], mrgY[0:64, :], inc=s_sp,
                          wait=[(s_ve, s_ve.n)])
                    k.dma("sync", y_sp[gb], mrgY[64:128, :], inc=s_sp)
                k.act(vtk2, mrgY, AF.Square, accum_out=rcol[:, 0:1],
                      wait=[(s_ve, marks["yev"][pp]), (s_se, s_se.n),
                            (s_cbr, s_cbr.n)], inc=s_se)
                marks["sq"][pp] = s_se.n
                k.v("tensor_scalar", rcol[:, 0:1], rcol[:, 0:1], 1.0 / DI,
                    1e-6, ALU.mult, op1=ALU.add, wait=[s_se])
                k.vdrain(s_ve)
                k.act(rcol[:, 0:1], rcol[:, 0:1], AF.Sqrt, wait=[s_ve],
                      inc=s_se)
                k.v("reciprocal", rcol[:, 0:1], rcol[:, 0:1], wait=[s_se])
                k.vdrain(s_ve)
                k.v("tensor_scalar", mrgY, mrgY, rcol[:, 0:1], None, ALU.mult)
                k.vdrain(s_ve)
                yrms = (s_ve, s_ve.n)
                k.act(zb2, zb2, AF.Silu, wait=[(s_zl, s_zl.n)], inc=s_se)
                marks["zl"][pp] = s_se.n
                for m in range(32):
                    pslot = pb[m % 2]
                    tw = [yrms, (s_se, marks["zl"][pp])]
                    if (pp, m - 2) in marks["gy"]:
                        tw.append((s_ve, marks["gy"][(pp, m - 2)]))
                    k.trans(pslot[:, 0:128], mrgY[:, m * 128:(m + 1) * 128],
                            ident[:], wait=tw, inc=s_mm)
                    k.v("scalar_tensor_tensor",
                        gy_fm[:, m, 2 * p * 64:2 * p * 64 + 128],
                        zb2.rearrange("x (m t) -> x m t", m=32)[:, m],
                        nsc[:, m:m + 1], pslot[:, 0:128], ALU.mult,
                        ALU.mult, wait=[s_mm, (s_c4, s_c4.n)])
                    k.vdrain(s_ve)
                    marks["gy"][(pp, m)] = s_ve.n
                marks["tp"][pp] = s_mm.n
            gy_done = (s_ve, s_ve.n)
            p6km = []
            p6_evac = gy_done
            for tpair in range(2):
                for kd in range(32):
                    slot = kd % 2
                    if len(p6km) >= 2:
                        bw = [(s_mm, p6km[-2])]
                    else:
                        bw = [gy_done, (s_vl, s_vl.n), (s_se, s_se.n),
                              (s_mm, s_mm.n)]
                    k.dma("sync", wob[slot], w_out[kd * 128:(kd + 1) * 128, :],
                          inc=s_wb[slot], wait=bw)
                    for tp in range(2):
                        tt = tpair * 2 + tp
                        for q in range(4):
                            pk = (pb + ptr)[tp * 4 + q]
                            mmw = None
                            if tp == 0 and q == 0:
                                mmw = [(s_wb[slot], s_wb[slot].n)]
                                if kd == 0:
                                    mmw.append(p6_evac)
                            k.mm(pk[:, 0:512],
                                 gy_fm[:, kd, tt * 128:(tt + 1) * 128],
                                 wob[slot][:, q * 512:(q + 1) * 512],
                                 start=(kd == 0), stop=(kd == 31),
                                 wait=mmw,
                                 inc=s_mm if (tp == 1 and q == 3) else None)
                    p6km.append(s_mm.n)
                for tp in range(2):
                    tt = tpair * 2 + tp
                    t0 = half * 512 + tt * 128
                    uw = [(s_ot, s_ot.n)]
                    if tpair == 0 and tp == 0:
                        uw += [(s_ve, s_ve.n), (s_se, s_se.n),
                               (s_dr, s_dr.n), (s_cbr, s_cbr.n),
                               (s_c5, s_c5.n)]
                    k.dma("sync", lntmp[:], u[3 + t0:3 + t0 + 128, :],
                          inc=s_ur, wait=uw)
                    for q in range(4):
                        pk = (pb + ptr)[tp * 4 + q]
                        k.v("tensor_add", lntmp[:, q * 512:(q + 1) * 512],
                            lntmp[:, q * 512:(q + 1) * 512], pk[:, 0:512],
                            wait=[(s_mm, p6km[-1]), (s_ur, s_ur.n)]
                            if q == 0 else None)
                    k.vdrain(s_ve)
                    k.dma("sync", out[t0:t0 + 128, :], lntmp[:], inc=s_ot,
                          wait=[(s_ve, s_ve.n)])
                p6_evac = (s_ve, s_ve.n)
        t8 = lntmp.ap()[0:8, :]
        ptot_sb = ustage.ap()[0:8].rearrange("p a b -> p (a b)")[:, 0:D]
        c0v = mg_sb.ap()[0:8].rearrange("p a b -> p (a b)")[:, 50:51]
        k.dma("sync", t8, out[0:8, :], inc=s_ur,
              wait=[(s_ot, s_ot.n), (s_cc, 3)])
        k.dma("sync", ptot_sb, p_tot, inc=s_ur,
              wait=[(s_ve, s_ve.n), (s_mm, s_mm.n)])
        k.dma("sync", c0v, core0[0:8, 0:1], inc=s_ur)
        k.v("scalar_tensor_tensor", t8, ptot_sb, c0v, t8,
            ALU.mult, ALU.add, wait=[(s_ur, s_ur.n)])
        k.vdrain(s_ve)
        k.dma("sync", out[0:8, :], t8, inc=s_ot, wait=[(s_ve, s_ve.n)])
        k.op("sync", lambda h: h.nop(),
             wait=[s_ot, s_sp, s_vt, s_m, s_xnr, s_cc, s_vl, s_ml, s_zl, s_ft, s_pb, s_c2, s_c3, s_c4, s_c5, s_dr,
                   s_cbr, s_ur, s_gld] + s_zs + s_u + s_wb + s_f + s_cb +
                  [s_const])

        k.emit()
    return nc


def _prep_inputs(L0_attn_u, L0_attn_in_proj_w, L0_attn_conv_weight,
                 L0_attn_conv_bias, L0_attn_A_log, L0_attn_dt_bias,
                 L0_attn_D_param, L0_attn_norm_scale, L0_attn_out_proj_w,
                 L0_ffn_w1, L0_ffn_w3, L0_ffn_w2, L0_ffn_wr):
    u = np.ascontiguousarray(np.asarray(L0_attn_u, np.float32).reshape(8192, D))
    upad = np.concatenate([np.zeros((HALO, D), np.float32), u], 0)
    csmask = np.ones((NH, TOK), np.float32)
    csmask[:, ::S] = 0.0
    shared = {
        "w_in": np.ascontiguousarray(L0_attn_in_proj_w, np.float32),
        "conv_w": np.ascontiguousarray(np.transpose(
            np.asarray(L0_attn_conv_weight, np.float32).reshape(4, 32, 128),
            (2, 1, 0))),
        "conv_b": np.ascontiguousarray(
            np.asarray(L0_attn_conv_bias, np.float32).reshape(32, 128).T),
        "a_log": np.asarray(L0_attn_A_log, np.float32).reshape(NH, 1),
        "dt_bias": np.asarray(L0_attn_dt_bias, np.float32).reshape(NH, 1),
        "d_rep": np.tile(np.asarray(L0_attn_D_param, np.float32).reshape(1, NH),
                         (128, 1)),
        "norm_scale": np.ascontiguousarray(
            np.asarray(L0_attn_norm_scale, np.float32).reshape(32, 128).T),
        "w_out": np.ascontiguousarray(L0_attn_out_proj_w, np.float32),
        "w1t": np.ascontiguousarray(np.asarray(L0_ffn_w1, np.float32).T),
        "w3t": np.ascontiguousarray(np.asarray(L0_ffn_w3, np.float32).T),
        "w2t": np.ascontiguousarray(np.asarray(L0_ffn_w2, np.float32).T),
        "wrt": np.ascontiguousarray(np.asarray(L0_ffn_wr, np.float32).T
                                    .reshape(16, 128, E).transpose(1, 0, 2)
                                    .reshape(128, 16 * E)),
        "ident": np.eye(128, dtype=np.float32),
        "csmask": csmask,
        "maskl": np.tile(np.where(
            np.arange(S)[None, :] >= np.arange(S)[:, None],
            0.0, -1e30).astype(np.float32), (2, 1)),
    }
    in_maps = []
    for c in range(NCORES):
        m = dict(shared)
        m["u"] = np.ascontiguousarray(upad[c * TOK:c * TOK + TOKH, :])
        cmb = np.zeros((64, NCORES), np.float32)
        cmb[:, :c] = 1.0
        m["combmask"] = cmb
        m["combmaskinv"] = (1.0 - cmb)
        m["core0"] = np.full((128, 1), 1.0 if c == 0 else 0.0, np.float32)
        in_maps.append(m)
    return in_maps


def kernel(**inputs):
    in_maps = _prep_inputs(**inputs)
    nc = build_program()
    res = run_bass_kernel_spmd(nc, in_maps, core_ids=list(range(NCORES)))
    return np.concatenate([r["out"] for r in res.results], 0).reshape(1, 8192, D)


# version 08
# speedup vs baseline: 1.1791x; 1.1791x over previous
import contextlib
import numpy as np
import concourse.bass as bass
import concourse.mybir as mybir
from concourse.bass_utils import run_bass_kernel_spmd

F32 = mybir.dt.float32
AF = mybir.ActivationFunctionType
ALU = mybir.AluOpType
AX = mybir.AxisListType

NCORES = 8
TOK = 1024
HALO = 3
TOKH = TOK + HALO
D = 2048
DI = 4096
NH = 64
NST = 64
S = 64
NCH = TOK // S
DIP = 8384
E = 8

class Sem:
    def __init__(self, handle):
        self.h = handle
        self.n = 0


class K:
    ENGS = ("tensor", "vector", "scalar", "gpsimd", "sync")

    def __init__(self, nc):
        self.nc = nc
        self.q = {e: [] for e in self.ENGS}
        self.dirty = {e: set() for e in self.ENGS}

    @staticmethod
    def _tnames(aps):
        out = set()
        for a in aps:
            if a is None or isinstance(a, (int, float)):
                continue
            t = getattr(a, "tensor", None)
            if t is not None:
                out.add(t.name)
        return out

    def op(self, eng, fn, reads=(), writes=(), inc=None, inc_k=1, wait=None,
           dma_inc=None):
        import traceback
        tag = None
        for fr in reversed(traceback.extract_stack()[:-1]):
            if fr.filename.endswith("kernel.py") and fr.name not in (
                    "op", "dma", "mm", "trans", "act", "v", "vdrain"):
                tag = f"{fr.name}:{fr.lineno}"
                break
        rn = self._tnames(reads)
        wn = self._tnames(writes)
        need_drain = bool(self.dirty[eng] & (rn | wn))
        waits = []
        if wait:
            for w in wait:
                waits.append((w.h, w.n) if isinstance(w, Sem) else (w[0].h, w[1]))
        if inc is not None:
            inc.n += inc_k
        if dma_inc is not None:
            dma_inc.n += 16
        if need_drain:
            self.dirty[eng] = set()

        def emit(h, nd=need_drain, waits=waits, fn=fn, inc=inc, inc_k=inc_k,
                 dma_inc=dma_inc, tag=tag):
            if nd:
                h.drain()
            for sh, sv in waits:
                h.wait_ge(sh, sv)
            try:
                ins = fn(h)
            except Exception as e:
                raise RuntimeError(f"emit failed at record site {tag}: {e}") from e
            if ins is not None:
                if inc is not None:
                    ins.then_inc(inc.h, inc_k)
                if dma_inc is not None:
                    ins.then_inc(dma_inc.h, 16)

        self.q[eng].append(emit)
        self.dirty[eng] |= wn

    def dma(self, eng, out, in_, inc=None, wait=None):
        self.op(eng, lambda h: h.dma_start(out=out, in_=in_),
                reads=(in_,), writes=(out,), dma_inc=inc, wait=wait)

    def mm(self, out, lhsT, rhs, start, stop, inc=None, wait=None):
        self.op("tensor",
                lambda h: h.matmul(out, lhsT, rhs, start=start, stop=stop),
                reads=(lhsT, rhs), writes=(out,), inc=inc, wait=wait)

    def trans(self, out, in_, ident, inc=None, wait=None):
        self.op("tensor", lambda h: h.transpose(out, in_, ident),
                reads=(in_, ident), writes=(out,), inc=inc, wait=wait)

    def act(self, out, in_, func, bias=0.0, scale=1.0, accum_out=None, inc=None,
            wait=None):
        self.op("scalar",
                lambda h: h.activation(out, in_, func, bias=bias, scale=scale,
                                       accum_out=accum_out),
                reads=(in_, None if isinstance(bias, float) else bias,
                       None if isinstance(scale, float) else scale),
                writes=(out, accum_out), inc=inc, wait=wait)

    def v(self, method, out, *ins, inc=None, wait=None, **kw):
        def fn(h, method=method, out=out, ins=ins, kw=kw):
            return getattr(h, method)(out, *ins, **kw)
        self.op("vector", fn, reads=ins, writes=(out,), inc=inc, wait=wait)

    def vdrain(self, inc):
        self.op("vector", lambda h: h.drain(), inc=inc)
        self.dirty["vector"] = set()

    def emit(self):
        with self.nc.Block() as block:
            @block.tensor
            def _(h):
                for f in self.q["tensor"]:
                    f(h)

            @block.vector
            def _(h):
                for f in self.q["vector"]:
                    f(h)

            @block.scalar
            def _(h):
                for f in self.q["scalar"]:
                    f(h)

            @block.gpsimd
            def _(h):
                for f in self.q["gpsimd"]:
                    f(h)

            @block.sync
            def _(h):
                for f in self.q["sync"]:
                    f(h)


def build_program(debug=(), stage=1):
    nc = bass.Bass("TRN2", target_bir_lowering=False, debug=False)

    def dram_in(name, shape):
        return nc.dram_tensor(name, shape, F32, kind="ExternalInput").ap()

    u = dram_in("u", [TOKH, D])
    w_in = dram_in("w_in", [D, DIP])
    conv_w = dram_in("conv_w", [128, 32, 4])
    conv_b = dram_in("conv_b", [128, 32])
    a_log = dram_in("a_log", [NH, 1])
    dt_bias = dram_in("dt_bias", [NH, 1])
    d_rep = dram_in("d_rep", [128, NH])
    norm_scale = dram_in("norm_scale", [128, 32])
    w_out = dram_in("w_out", [DI, D])
    w1t = dram_in("w1t", [D, D])
    w3t = dram_in("w3t", [D, D])
    w2t = dram_in("w2t", [D, D])
    wrt = dram_in("wrt", [128, 16 * E])
    ident_in = dram_in("ident", [128, 128])
    maskl_in = dram_in("maskl", [128, S])
    combmaskinv = dram_in("combmaskinv", [64, NCORES])
    csmask_in = dram_in("csmask", [NH, TOK])
    combmask = dram_in("combmask", [64, NCORES])
    core0 = dram_in("core0", [128, 1])

    out = nc.dram_tensor("out", [TOK, D], F32, kind="ExternalOutput").ap()

    def scratch(name, shape, dbg=False):
        kind = "ExternalOutput" if dbg else "Internal"
        if dbg:
            return nc.dram_tensor(name, shape, F32, kind="ExternalOutput").ap()
        return nc.dram_tensor(name, shape, F32).ap()

    xn_sp = scratch("xn_sp", [16, 128, TOKH], "xn" in debug)
    z_sp = scratch("z_sp", [32, 128, TOK], "z" in debug)
    bcdt_sp = scratch("bcdt_sp", [4, 64, TOK], "bcdt" in debug)
    vfm_sp = scratch("vfm_sp", [2, 32, 128, 512], "v" in debug)
    vtok_sp = scratch("vtok_sp", [NCH, 64, DI], "vtok" in debug)
    m_sp = scratch("m_sp", [NCH, 64, DI], "m" in debug)
    ft_sp = scratch("ft_sp", [65, DI], "ft" in debug)
    p_sp = scratch("p_sp", [8, D], "p" in debug)
    mg_sp = scratch("mg_sp", [128, 64], "p" in debug)
    h2_sp = scratch("h2_sp", [16, 128, TOK], "h2" in debug)
    f_bnc = nc.dram_tensor("f_bnc", [64, DI], F32).ap()
    t_bnc = nc.dram_tensor("t_bnc", [64, 1], F32).ap()
    f_gat = nc.dram_tensor("f_gat", [NCORES, 64, DI], F32).ap()
    t_gat = nc.dram_tensor("t_gat", [NCORES, 64, 1], F32).ap()
    csrow = nc.dram_tensor("csrow", [2, 2, NH * S], F32).ap()
    totrow = nc.dram_tensor("totrow", [2, 2, NH], F32).ap()
    rrow = nc.dram_tensor("rrow", [2, 1, S], F32).ap()
    p_bnc = nc.dram_tensor("p_bnc", [8, D], F32).ap()
    p_tot = nc.dram_tensor("p_tot", [8, D], F32).ap()
    y_sp = scratch("y_sp", [NCH, 64, DI], "y" in debug)
    att_sp = scratch("att_sp", [128, DI], "att" in debug)
    vt2_sp = scratch("vt2_sp", [128, DI], "att" in debug)
    hin_sp = scratch("hin_sp", [64, DI], "att" in debug)
    sml_sp = scratch("sml_sp", [128, 1152], "att" in debug)
    hq_sp = scratch("hq_sp", [64, DI], "att" in debug)

    k = K(nc)
    sems = {}

    def sem(name):
        s = Sem(stackctx.enter_context(nc.semaphore(name)))
        sems[name] = s
        return s

    stackctx = contextlib.ExitStack()
    with stackctx:
        def sbuf(name, shape, dtype=F32):
            return stackctx.enter_context(nc.sbuf_tensor(name, shape, dtype))

        def psum(name):
            return stackctx.enter_context(nc.psum_tensor(name, [128, 512], F32))

        s_const = sem("const")
        s_u = [sem("u0"), sem("u1")]
        s_wb = [sem("wb0"), sem("wb1")]
        s_zs = [sem(f"zs{i}") for i in range(4)]
        s_mm = sem("mm")
        s_ve = sem("ve")
        s_se = sem("se")
        s_sp = sem("sp")
        s_gld = sem("gld")
        s_cc = sem("cc")
        s_vt = sem("vt")
        s_ft = sem("ft")
        s_pb = sem("pbs")
        s_c2 = sem("c2")
        s_c3 = sem("c3")
        s_c4 = sem("c4")
        s_c5 = sem("c5")
        s_m = sem("m")
        s_xnr = sem("xnr")

        ident = sbuf("identt", [128, 128])
        xn_fm = sbuf("xn_fm", [128, 16, TOKH])
        B_fm = sbuf("B_fm", [64, TOK])
        C_fm = sbuf("C_fm", [64, TOK])
        dt_fm = sbuf("dt_fm", [64, TOK])
        la_fm = sbuf("la_fm", [64, TOK])
        cs_fm = sbuf("cs_fm", [64, TOK])
        csmask = sbuf("csmask_t", [64, TOK])
        acol = sbuf("acol", [64, 1])
        dtb = sbuf("dtb", [64, 1])
        convw = sbuf("convw_t", [128, 32, 4])
        convb = sbuf("convb_t", [128, 32])
        v_fm = sbuf("v_fm", [128, 32, 512])
        wband = sbuf("wband", [128, 2, 512])
        ustage = sbuf("ustage", [128, 2, D])
        lnm = sbuf("lnm", [128, 2])
        consts1 = sbuf("consts1", [128, 1])
        lntmp = sbuf("lntmp", [128, D])
        xtmp = sbuf("xtmp", [128, 2, 515])
        zstage = sbuf("zstage", [128, 4, 512])
        h_loc = ustage.ap()[0:64].rearrange("p a b -> p (a b)")[:, 0:DI]
        xnflat = xn_fm.ap()[0:64].rearrange("p a b -> p (a b)")
        xnflat128 = xn_fm.ap().rearrange("p a b -> p (a b)")
        vt_sb = xnflat[:, 0:DI]
        m_sb = xnflat[:, DI:2 * DI]
        b_tok = sbuf("b_tok", [64, 64])
        e2s = sbuf("e2s", [64, NCH, 64])
        tmp2_fm = sbuf("tmp2_fm", [64, 64])
        cs_tok = sbuf("cs_tok", [64, NCH, 64])
        ecs_tok = sbuf("ecs_tok", [64, NCH, 64])
        drep = sbuf("drep", [64, 64])
        tcol = sbuf("tcol", [64, 1])

        pb = [psum(f"pb{i}") for i in range(6)]
        ptr = [psum("pt0"), psum("pt1")]

        k.dma("sync", ident[:], ident_in, inc=s_const)
        k.dma("sync", csmask[:], csmask_in, inc=s_const)
        k.dma("sync", acol[:], a_log, inc=s_const)
        k.dma("sync", dtb[:], dt_bias, inc=s_const)
        k.dma("sync", convw[:], conv_w, inc=s_const)
        k.dma("sync", convb[:], conv_b, inc=s_const)
        k.op("vector", lambda h: h.memset(consts1[:], 1.0),
             writes=(consts1[:],))
        k.vdrain(s_ve)
        k.act(acol[:], acol[:], AF.Exp, wait=[s_const], inc=s_se)

        umarks = []
        semarks = []
        pmarks = {}
        tposed = 0
        for t in range(9):
            rows = 128 if t < 8 else 3
            r0 = t * 128
            ub = ustage[:, t % 2]
            w = None
            if t >= 2:
                w = [(s_mm, umarks[t - 2]), (s_se, semarks[t - 2])]
            k.dma("sync", ub[0:rows, :], u[r0:r0 + rows, :], inc=s_u[t % 2],
                  wait=w)
            k.v("reduce_sum", lnm[0:rows, 0:1], ub[0:rows, :], axis=AX.X,
                wait=[s_u[t % 2]])
            k.v("tensor_scalar_mul", lnm[0:rows, 0:1], lnm[0:rows, 0:1], 1.0 / D)
            k.v("tensor_single_scalar", ub[0:rows, :], ub[0:rows, :],
                lnm[0:rows, 0:1], ALU.subtract)
            k.vdrain(s_ve)
            k.act(lntmp[0:rows, :], ub[0:rows, :], AF.Square,
                  accum_out=lnm[0:rows, 1:2], wait=[s_ve], inc=s_se)
            k.v("tensor_scalar", lnm[0:rows, 1:2], lnm[0:rows, 1:2], 1.0 / D,
                1e-5, ALU.mult, op1=ALU.add, wait=[s_se])
            k.vdrain(s_ve)
            k.act(lnm[0:rows, 1:2], lnm[0:rows, 1:2], AF.Sqrt, wait=[s_ve],
                  inc=s_se)
            k.v("reciprocal", lnm[0:rows, 1:2], lnm[0:rows, 1:2], wait=[s_se])
            k.v("tensor_single_scalar", ub[0:rows, :], ub[0:rows, :],
                lnm[0:rows, 1:2], ALU.mult)
            k.vdrain(s_ve)
            for kd in range(16):
                pslot = ptr[tposed % 2]
                tw = [s_ve] if kd == 0 else []
                if tposed >= 2:
                    tw = tw + [(s_se, pmarks[tposed - 2])]
                k.trans(pslot[:, 0:rows], ub[0:rows, kd * 128:(kd + 1) * 128],
                        ident[0:rows, 0:rows], wait=tw or None, inc=s_mm)
                k.act(xn_fm[:, kd, r0:r0 + rows], pslot[:, 0:rows], AF.Copy,
                      wait=[s_mm], inc=s_se)
                pmarks[tposed] = s_se.n
                tposed += 1
            umarks.append(s_mm.n)
            semarks.append(s_se.n)
        xn_done = (s_se, s_se.n)
        for kd in range(16):
            k.dma("sync", xn_sp[kd], xn_fm[:, kd, :], inc=s_sp, wait=[xn_done])
        xnsp_mark = (s_sp, s_sp.n)

        zmov = [(3, 0), (515, 3)]
        zslot_marks = []
        bank_free = [(s_se, s_se.n), (s_ve, s_ve.n)]
        wmarks = []

        def proj_pass(tiles, evac, xn_cols=zmov, ncols=512):
            nonlocal bank_free, wmarks
            nt = len(tiles)
            for kd in range(16):
                c0 = tiles[0][0]
                c1 = tiles[-1][0] + tiles[-1][1]
                slot = len(wmarks) % 2
                wb = wband[:, slot, 0:c1 - c0]
                lw = None
                if len(wmarks) >= 2:
                    lw = [(s_mm, wmarks[-2])]
                k.dma("sync", wb, w_in[kd * 128:(kd + 1) * 128, c0:c1],
                      inc=s_wb[slot], wait=lw)
                for mi, (wc0, wm) in enumerate(tiles):
                    lhsT = wb[:, wc0 - c0:wc0 - c0 + wm]
                    for (x0, pbase) in xn_cols:
                        mmw = []
                        if mi == 0 and (x0, pbase) == xn_cols[0]:
                            mmw.append((s_wb[slot], s_wb[slot].n))
                            if kd == 0:
                                mmw += bank_free
                        k.mm(pb[pbase + mi][0:wm, 0:ncols], lhsT,
                             xn_fm[:, kd, x0:x0 + ncols],
                             start=(kd == 0), stop=(kd == 15),
                             wait=mmw or None,
                             inc=s_mm if (mi == nt - 1 and (x0, pbase) == xn_cols[-1]) else None)
                wmarks.append(s_mm.n)
            evac()

        for g0 in range(0, 32, 3):
            gt = [(m * 128, 128) for m in range(g0, min(g0 + 3, 32))]

            def z_evac(g0=g0, gt=gt):
                nonlocal bank_free
                for mi in range(len(gt)):
                    m = g0 + mi
                    for hi in range(2):
                        slot = len(zslot_marks) % 4
                        zw = [(s_mm, s_mm.n)]
                        if len(zslot_marks) >= 4:
                            zw.append((s_zs[slot], zslot_marks[-4]))
                        k.act(zstage[:, slot], pb[hi * 3 + mi][:, 0:512],
                              AF.Copy, wait=zw, inc=s_se)
                        k.dma("sync", z_sp[m, :, hi * 512:hi * 512 + 512],
                              zstage[:, slot], inc=s_zs[slot], wait=[s_se])
                        zslot_marks.append(s_zs[slot].n)
                bank_free = [(s_se, s_se.n)]
            proj_pass(gt, z_evac)

        def bcdt_evac():
            nonlocal bank_free
            for mi, (nm, dest) in enumerate((("B", B_fm), ("C", C_fm),
                                             ("dt", dt_fm))):
                for hi in range(2):
                    src = pb[hi * 3 + mi][0:64, 0:512]
                    dc = hi * 512
                    if nm in ("B", "C"):
                        k.v("tensor_relu", dest[:, dc:dc + 512], src,
                            wait=[(s_mm, s_mm.n)])
                        k.v("tensor_sub", lntmp[0:64, 0:512], src,
                            dest[:, dc:dc + 512])
                        k.vdrain(s_ve)
                        k.act(lntmp[0:64, 512:1024], lntmp[0:64, 0:512], AF.Exp,
                              wait=[s_ve], inc=s_se)
                        k.v("tensor_add", dest[:, dc:dc + 512],
                            dest[:, dc:dc + 512], lntmp[0:64, 512:1024],
                            wait=[s_se])
                        k.vdrain(s_ve)
                    else:
                        t0 = lntmp[0:64, 0:512]
                        t1 = lntmp[0:64, 512:1024]
                        k.v("tensor_scalar_add", t0, src, dtb[:, 0:1],
                            wait=[(s_mm, s_mm.n)])
                        k.v("tensor_scalar_mul", t1, t0, -1.0)
                        k.v("tensor_tensor", t1, t1, t0, ALU.min)
                        k.vdrain(s_ve)
                        k.act(t1, t1, AF.Exp, wait=[s_ve])
                        k.op("scalar", lambda h: h.drain())
                        k.act(t1, t1, AF.Ln, bias=1.0 if False else consts1[0:64, 0:1],
                              inc=s_se)
                        k.v("tensor_relu", dest[:, dc:dc + 512], t0, wait=[s_se])
                        k.v("tensor_add", dest[:, dc:dc + 512],
                            dest[:, dc:dc + 512], t1)
                        k.vdrain(s_ve)
            bank_free = [(s_se, s_se.n), (s_ve, s_ve.n)]
        proj_pass([(8192, 64), (8256, 64), (8320, 64)], bcdt_evac)

        k.v("tensor_scalar", la_fm[:], dt_fm[:], acol[:, 0:1], -1.0,
            ALU.mult, op1=ALU.mult, wait=[s_se, s_ve])
        k.op("vector", lambda h: h.tensor_tensor_scan(
            cs_fm[:], csmask[:], la_fm[:], 0.0, ALU.mult, ALU.add),
            reads=(csmask[:], la_fm[:]), writes=(cs_fm[:],))
        k.vdrain(s_ve)
        if "bcdt" in debug:
            k.dma("sync", bcdt_sp[0], B_fm[:], inc=s_sp, wait=[s_ve])
            k.dma("sync", bcdt_sp[1], C_fm[:], inc=s_sp)
            k.dma("sync", bcdt_sp[2], dt_fm[:], inc=s_sp)
            k.dma("sync", bcdt_sp[3], cs_fm[:], inc=s_sp)

        xt_marks = {}
        ptm = {}
        mqm = {}
        mmvt = {}
        ttm = {}
        ttm2 = {}
        ptm2 = {}
        for half in range(2):
            xn0 = half * 512
            if half == 1:
                xw = [(s_ve, s_ve.n), (s_se, s_se.n), (s_mm, s_mm.n),
                      (s_vt, s_vt.n), (s_m, s_m.n)]
                for kd in range(16):
                    k.dma("sync", xn_fm[:, kd, :], xn_sp[kd], inc=s_xnr,
                          wait=xw if kd == 0 else None)
            vfree = None
            if half == 1 and "v" in debug:
                vfree = (s_sp, s_sp.n)
            for g0 in range(0, 32, 3):
                gt = [(4096 + m * 128, 128) for m in range(g0, min(g0 + 3, 32))]
                nt = len(gt)
                for kd in range(16):
                    c0 = gt[0][0]
                    c1 = gt[-1][0] + gt[-1][1]
                    slot = len(wmarks) % 2
                    wb = wband[:, slot, 0:c1 - c0]
                    lw = [(s_mm, wmarks[-2])] if len(wmarks) >= 2 else None
                    k.dma("sync", wb, w_in[kd * 128:(kd + 1) * 128, c0:c1],
                          inc=s_wb[slot], wait=lw)
                    for mi in range(nt):
                        wc0, wm = gt[mi]
                        lhsT = wb[:, wc0 - c0:wc0 - c0 + wm]
                        mmw = []
                        if mi == 0:
                            mmw.append((s_wb[slot], s_wb[slot].n))
                            if kd == 0:
                                mmw += bank_free + [(s_xnr, s_xnr.n)]
                        k.mm(pb[mi][:, 0:512], lhsT,
                             xn_fm[:, kd, xn0:xn0 + 512],
                             start=(kd == 0), stop=(kd == 15), wait=mmw or None)
                        k.mm(pb[3 + mi][:, 0:3], lhsT,
                             xn_fm[:, kd, xn0 + 512:xn0 + 515],
                             start=(kd == 0), stop=(kd == 15),
                             inc=s_mm if (mi == nt - 1) else None)
                    wmarks.append(s_mm.n)
                for mi in range(nt):
                    m = g0 + mi
                    xt = xtmp[:, m % 2]
                    xw = [(s_mm, s_mm.n)]
                    if (m - 2) in xt_marks:
                        xw.append((s_ve, xt_marks[m - 2]))
                    k.act(xt[:, 0:512], pb[mi][:, 0:512], AF.Copy, wait=xw)
                    k.act(xt[:, 512:515], pb[3 + mi][:, 0:3], AF.Copy, inc=s_se)
                    cw = convw[:, m]
                    vdst = v_fm[:, m, :]
                    vw = [s_se]
                    if vfree is not None:
                        vw.append(vfree)
                    k.v("tensor_scalar_mul", vdst, xt[:, 0:512], cw[:, 0:1],
                        wait=vw)
                    for kk in range(1, 4):
                        k.v("scalar_tensor_tensor", vdst, xt[:, kk:kk + 512],
                            cw[:, kk:kk + 1], vdst, ALU.mult, ALU.add)
                    k.vdrain(s_ve)
                    xt_marks[m] = s_ve.n
                    k.act(vdst, vdst, AF.Silu, bias=convb[:, m:m + 1],
                          wait=[s_ve], inc=s_se)
                bank_free = [(s_se, s_se.n)]
                if "v" in debug:
                    for mi in range(nt):
                        m = g0 + mi
                        k.dma("sync", vfm_sp[half, m], v_fm[:, m, :],
                              inc=s_sp, wait=[(s_se, s_se.n)])

            if half == 0:
                k.op("vector", lambda h: h.memset(h_loc, 0.0),
                     writes=(h_loc,), wait=[(s_mm, s_mm.n), (s_se, s_se.n)])
                k.vdrain(s_ve)
            vconsumed = []
            for c in range(8):
                gc = half * 8 + c
                csl = slice(gc * 64, gc * 64 + 64)
                for m in range(32):
                    pslot = ptr[m % 2]
                    tw = [(s_se, s_se.n)] if m < 2 else [(s_se, ptm[m - 2])]
                    if m == 0:
                        tw = tw + [(s_mm, mmvt.get("x", 0))]
                    ew_extra = [xnsp_mark] if (m == 0 and c == 0 and half == 0) else []
                    k.trans(pslot[0:64, 0:128], v_fm[:, m, c * 64:(c + 1) * 64],
                            ident[:], wait=tw, inc=s_mm)
                    ew = [s_mm] if m > 0 else [s_mm, (s_ve, s_ve.n)]
                    k.act(vt_sb[:, m * 128:(m + 1) * 128], pslot[0:64, 0:128],
                          AF.Copy, wait=ew + ew_extra, inc=s_se)
                    ptm[m] = s_se.n
                vt_done = (s_se, s_se.n)
                vconsumed.append(s_mm.n)
                k.dma("sync", vtok_sp[gc], vt_sb, inc=s_vt, wait=[vt_done])
                k.v("tensor_scalar", tmp2_fm[:], cs_fm[:, csl],
                    cs_fm[:, gc * 64 + 63:gc * 64 + 64], -1.0, ALU.subtract,
                    op1=ALU.mult, wait=[vt_done])
                k.vdrain(s_ve)
                k.trans(ptr[0][0:64, 0:64], tmp2_fm[:], ident[0:64, 0:64],
                        wait=[s_ve], inc=s_mm)
                k.trans(ptr[1][0:64, 0:64], B_fm[:, csl], ident[0:64, 0:64],
                        inc=s_mm)
                k.trans(ptr[0][0:64, 64:128], cs_fm[:, csl], ident[0:64, 0:64],
                        inc=s_mm)
                k.act(e2s[:, gc], ptr[0][0:64, 0:64], AF.Exp, wait=[s_mm])
                k.act(b_tok[:], ptr[1][0:64, 0:64], AF.Copy)
                k.act(cs_tok[:, gc], ptr[0][0:64, 64:128], AF.Copy)
                k.act(ecs_tok[:, gc], ptr[0][0:64, 64:128], AF.Exp, inc=s_se)
                ptm[0] = s_se.n
                ptm[1] = s_se.n
                k.v("tensor_mul", drep[:], e2s[:, gc], ecs_tok[:, gc],
                    wait=[s_se])
                k.v("tensor_tensor",
                    vt_sb.rearrange("t (h p) -> t h p", h=64),
                    vt_sb.rearrange("t (h p) -> t h p", h=64),
                    e2s[:, gc][:, :, None].broadcast_to([64, 64, 64]), ALU.mult,
                    wait=[s_vt])
                k.vdrain(s_ve)
                for q in range(8):
                    k.mm(pb[q % 6][0:64, 0:512], b_tok[:],
                         vt_sb[:, q * 512:(q + 1) * 512], start=True,
                         stop=True,
                         wait=[s_ve] if q == 0 else ([(s_se, mqm[q - 6])] if q >= 6 else None),
                         inc=s_mm)
                    macw = [s_mm]
                    if q == 0:
                        macw.append((s_m, s_m.n))
                    k.act(m_sb[:, q * 512:(q + 1) * 512], pb[q % 6][0:64, 0:512],
                          AF.Copy, wait=macw, inc=s_se)
                    mqm[q] = s_se.n
                mmvt["x"] = s_mm.n
                m_done = (s_se, s_se.n)
                k.dma("sync", m_sp[gc], m_sb, inc=s_m, wait=[m_done])
                k.v("tensor_tensor",
                    h_loc.rearrange("n (h p) -> n h p", h=64),
                    h_loc.rearrange("n (h p) -> n h p", h=64),
                    drep[:, :, None].broadcast_to([64, 64, 64]), ALU.mult,
                    wait=[m_done])
                k.v("tensor_add", h_loc, h_loc, m_sb[:])
                k.vdrain(s_ve)
                mspill_scan = [(s_m, s_m.n), (s_ve, s_ve.n)]
            bank_free = bank_free + [(s_se, s_se.n), (s_ve, s_ve.n)]

        k.v("reduce_sum", tcol[:], la_fm[:], axis=AX.X, wait=[(s_se, s_se.n)])
        k.vdrain(s_ve)
        k.act(tcol[:], tcol[:], AF.Exp, wait=[s_ve], inc=s_se)
        k.dma("sync", f_bnc, h_loc, inc=s_ft, wait=[(s_ve, s_ve.n)])
        k.dma("sync", t_bnc, tcol[:], inc=s_ft, wait=[(s_se, s_se.n)])
        if "ft" in debug:
            k.dma("sync", ft_sp[0:64, :], h_loc, inc=s_sp,
                  wait=[(s_ve, s_ve.n)])
            k.dma("sync", ft_sp[64:65, 0:64], tcol[:, 0:1].rearrange("a b -> b a") if False else tcol[0:64, 0:1],
                  inc=s_sp, wait=[(s_se, s_se.n)])
        cc_ready = (s_ft, s_ft.n)
        k.op("gpsimd", lambda h: h.collective_compute(
            "AllGather", ALU.bypass, replica_groups=[list(range(NCORES))],
            ins=[f_bnc.opt()], outs=[f_gat.opt()]), wait=[cc_ready], inc=s_cc)
        k.op("gpsimd", lambda h: h.collective_compute(
            "AllGather", ALU.bypass, replica_groups=[list(range(NCORES))],
            ins=[t_bnc.opt()], outs=[t_gat.opt()]), wait=None, inc=s_cc)

        wrt_sb = sbuf("wrt_sb", [128, 16 * E])
        mg_sb = sbuf("mg_sb", [128, 8, E])
        lgt = sbuf("lgt", [128, E + 4])
        p_part = zstage.ap()[0:8].rearrange("p a b -> p (a b)")[:, 0:D]
        k.dma("sync", wrt_sb[:], wrt, inc=s_c2)
        xw = [(s_ve, s_ve.n), (s_se, s_se.n), (s_mm, s_mm.n),
              (s_vt, s_vt.n), (s_m, s_m.n)]
        for kd in range(16):
            k.dma("sync", xn_fm[:, kd, :], xn_sp[kd], inc=s_xnr,
                  wait=xw if kd == 0 else None)
        for tt in range(8):
            tc = 3 + tt * 128
            for kd in range(16):
                k.mm(pb[0][0:128, 0:E], xn_fm[:, kd, tc:tc + 128],
                     wrt_sb[:, kd * E:(kd + 1) * E],
                     start=(kd == 0), stop=(kd == 15),
                     wait=([(s_xnr, s_xnr.n), (s_c2, s_c2.n)] + bank_free +
                           [(s_se, s_se.n)]) if (tt == 0 and kd == 0) else
                          ([(s_se, ttm[tt - 1])] if kd == 0 and tt > 0 else None),
                     inc=s_mm if kd == 15 else None)
            lcw = [s_mm] + ([(s_ve, ttm2[tt - 1])] if tt > 0 else [])
            k.act(lgt[:, 0:E], pb[0][:, 0:E], AF.Copy, wait=lcw, inc=s_se)
            k.v("reduce_max", lgt[:, E:E + 1], lgt[:, 0:E], axis=AX.X,
                wait=[s_se])
            k.v("tensor_single_scalar", mg_sb[:, tt], lgt[:, 0:E],
                lgt[:, E:E + 1], ALU.is_equal)
            k.vdrain(s_ve)
            k.v("scalar_tensor_tensor", lgt[:, 0:E], mg_sb[:, tt], -1e30,
                lgt[:, 0:E], ALU.mult, ALU.add)
            k.v("reduce_max", lgt[:, E + 1:E + 2], lgt[:, 0:E], axis=AX.X)
            k.vdrain(s_ve)
            k.v("tensor_single_scalar", lgt[:, 0:E], lgt[:, 0:E],
                lgt[:, E + 1:E + 2], ALU.is_equal)
            k.v("tensor_sub", lgt[:, E + 2:E + 3], lgt[:, E:E + 1],
                lgt[:, E + 1:E + 2])
            k.vdrain(s_ve)
            k.act(lgt[:, E + 2:E + 3], lgt[:, E + 2:E + 3], AF.Sigmoid,
                  wait=[s_ve], inc=s_se)
            k.v("tensor_scalar", lgt[:, E + 3:E + 4], lgt[:, E + 2:E + 3],
                -1.0, 1.0, ALU.mult, op1=ALU.add, wait=[s_se])
            k.vdrain(s_ve)
            k.v("tensor_scalar", mg_sb[:, tt], mg_sb[:, tt],
                lgt[:, E + 2:E + 3], None, ALU.mult, wait=[s_ve])
            k.v("scalar_tensor_tensor", mg_sb[:, tt], lgt[:, 0:E],
                lgt[:, E + 3:E + 4], mg_sb[:, tt], ALU.mult, ALU.add)
            k.vdrain(s_ve)
            ttm[tt] = s_se.n
            ttm2[tt] = s_ve.n
        mg_done = (s_ve, s_ve.n)
        if "p" in debug:
            k.dma("sync", mg_sp,
                  mg_sb.ap().rearrange("p a b -> p (a b)"), inc=s_sp,
                  wait=[mg_done])

        h1v = v_fm.ap().rearrange("p a b -> p (a b)")
        h2v = xnflat128
        def ffn_fm(wsrc, mov, dest_fn, func, first_wait):
            for g0 in range(0, 16, 3):
                gt = list(range(g0, min(g0 + 3, 16)))
                nt = len(gt)
                for kd in range(16):
                    slot = len(wmarks) % 2
                    wb = wband[:, slot, 0:nt * 128]
                    lw = [(s_mm, wmarks[-2])] if len(wmarks) >= 2 else None
                    k.dma("sync", wb,
                          wsrc[kd * 128:(kd + 1) * 128,
                               g0 * 128:g0 * 128 + nt * 128],
                          inc=s_wb[slot], wait=lw)
                    for mi in range(nt):
                        lhsT = wb[:, mi * 128:(mi + 1) * 128]
                        for hi in range(2):
                            mmw = []
                            if mi == 0 and hi == 0:
                                mmw.append((s_wb[slot], s_wb[slot].n))
                                if kd == 0:
                                    mmw += bank_free2[0]
                                    if g0 == 0:
                                        mmw += first_wait
                            k.mm(pb[hi * 3 + mi][:, 0:512], lhsT,
                                 mov(kd, hi), start=(kd == 0), stop=(kd == 15),
                                 wait=mmw or None,
                                 inc=s_mm if (mi == nt - 1 and hi == 1) else None)
                    wmarks.append(s_mm.n)
                for mi in range(nt):
                    m = g0 + mi
                    for hi in range(2):
                        k.act(dest_fn(m, hi), pb[hi * 3 + mi][:, 0:512], func,
                              wait=[(s_mm, s_mm.n)], inc=s_se)
                bank_free2[0] = [(s_se, s_se.n)]

        bank_free2 = [bank_free + [mg_done, (s_se, s_se.n)]]
        ffn_fm(w1t, lambda kd, hi: xn_fm[:, kd, 3 + hi * 512:3 + hi * 512 + 512],
               lambda m, hi: h1v[:, m * 1024 + hi * 512:m * 1024 + hi * 512 + 512],
               AF.Silu, [])
        ffn_fm(w3t, lambda kd, hi: h1v[:, kd * 1024 + hi * 512:kd * 1024 + hi * 512 + 512],
               lambda m, hi: h2v[:, m * 1024 + hi * 512:m * 1024 + hi * 512 + 512],
               AF.Silu, [(s_mm, s_mm.n)])
        if "h2" in debug:
            for m in range(16):
                k.dma("sync", h2_sp[m],
                      h2v[:, m * 1024:(m + 1) * 1024], inc=s_sp,
                      wait=[(s_se, s_se.n)])
        ffn_fm(w2t, lambda kd, hi: h2v[:, kd * 1024 + hi * 512:kd * 1024 + hi * 512 + 512],
               lambda m, hi: h1v[:, m * 1024 + hi * 512:m * 1024 + hi * 512 + 512],
               AF.Copy, [(s_mm, s_mm.n)])
        htok = lntmp.ap()
        hw0 = [(s_se, s_se.n), (s_ve, s_ve.n)]
        for tt in range(8):
            for dtile in range(16):
                pslot = ptr[dtile % 2]
                tw = ([(s_se, ptm2[(tt, dtile - 2)])] if dtile >= 2
                      else [(s_se, ptm2.get((tt - 1, 14 + dtile), s_se.n))])
                if tt == 0 and dtile < 2:
                    tw = [(s_se, s_se.n)]
                if tt == 0 and dtile == 0:
                    tw += hw0
                k.trans(pslot[:, 0:128],
                        h1v[:, dtile * 1024 + tt * 128:dtile * 1024 + tt * 128 + 128],
                        ident[:], wait=tw, inc=s_mm)
                k.act(htok[:, dtile * 128:(dtile + 1) * 128], pslot[:, 0:128],
                      AF.Copy, wait=[s_mm], inc=s_se)
                ptm2[(tt, dtile)] = s_se.n
            for q in range(4):
                k.mm(pb[q][0:8, 0:512], mg_sb[:, tt],
                     htok[:, q * 512:(q + 1) * 512],
                     start=(tt == 0), stop=(tt == 7),
                     wait=[(s_se, s_se.n)] if q == 0 else None,
                     inc=s_mm if q == 3 else None)
        for q in range(4):
            k.act(p_part[:, q * 512:(q + 1) * 512], pb[q][0:8, 0:512], AF.Copy,
                  wait=[s_mm] + ([(sz, sz.n) for sz in s_zs] if q == 0 else []),
                  inc=s_se)
        if "p" in debug:
            k.dma("sync", p_sp, p_part, inc=s_sp, wait=[(s_se, s_se.n)])

        k.dma("sync", p_bnc, p_part, inc=s_pb, wait=[(s_se, s_se.n)])
        pcc_ready = (s_pb, s_pb.n)
        k.op("gpsimd", lambda h: h.collective_compute(
            "AllReduce", ALU.add, replica_groups=[list(range(NCORES))],
            ins=[p_bnc.opt()], outs=[p_tot.opt()]), wait=[pcc_ready], inc=s_cc)

        hbuf = h_loc
        SPc = 3 * DI + 2048
        trep = xnflat[:, SPc + 768:SPc + 832]
        tm_t = xnflat[:, SPc + 832:SPc + 896]
        cmask = mg_sb.ap()[0:64].rearrange("p a b -> p (a b)")[:, 32:40]
        cmaski = mg_sb.ap()[0:64].rearrange("p a b -> p (a b)")[:, 40:48]
        fbufs = [xnflat[:, 0:DI], xnflat[:, DI:2 * DI]]
        s_f = [sem("f0"), sem("f1")]
        cmw = [(s_mm, s_mm.n), (s_se, s_se.n), (s_ve, s_ve.n)]
        k.dma("sync", cmask, combmask, inc=s_c3, wait=cmw)
        k.dma("sync", cmaski, combmaskinv, inc=s_c3)
        k.op("vector", lambda h: h.memset(hbuf, 0.0), writes=(hbuf,),
             wait=[(s_mm, s_mm.n), (s_se, s_se.n), (s_sp, s_sp.n),
                   (s_ft, s_ft.n)])
        k.vdrain(s_ve)
        fjm = {}
        for j in range(NCORES - 1):
            fb = fbufs[j % 2]
            fw = [(s_cc, 2), (s_mm, s_mm.n), (s_se, s_se.n)]
            if j >= 2:
                fw.append((s_ve, fjm[j - 2]))
            k.dma("sync", fb, f_gat[j], inc=s_f[j % 2], wait=fw)
            k.dma("sync", trep,
                  t_gat[j].rearrange("h one -> one h").broadcast_to([64, 64]),
                  inc=s_f[j % 2],
                  wait=[(s_ve, fjm[j - 1])] if j >= 1 else None)
            k.v("tensor_scalar", tm_t, trep, cmask[:, j:j + 1],
                cmaski[:, j:j + 1], ALU.mult, op1=ALU.add,
                wait=[(s_f[j % 2], s_f[j % 2].n), (s_c3, s_c3.n)])
            k.vdrain(s_ve)
            k.v("tensor_tensor", hbuf.rearrange("n (h p) -> n h p", h=64),
                hbuf.rearrange("n (h p) -> n h p", h=64),
                tm_t[:, :, None].broadcast_to([64, 64, 64]), ALU.mult)
            k.v("scalar_tensor_tensor", hbuf, fb, cmask[:, j:j + 1], hbuf,
                ALU.mult, ALU.add)
            k.vdrain(s_ve)
            fjm[j] = s_ve.n
        hin_done = (s_ve, s_ve.n)

        vtk2 = xnflat128[:, 0:DI]
        mrgY = xnflat128[:, DI:2 * DI]
        mrgA = xnflat[:, DI:2 * DI]
        att2 = xnflat128[:, 2 * DI:3 * DI]
        zb2 = xnflat128[:, 3 * DI:4 * DI]
        wob = [xnflat128[:, 0:2048], xnflat128[:, 2048:4096]]
        lnf = lntmp.ap()
        tmpq = lnf[:, 0:512]
        masklv = lnf[:, 512:576]
        gm2 = lnf[:, 576:640]
        drepA = lnf[0:64, 640:704]
        drepB = lnf[0:64, 1088:1152]
        cstk2 = lnf[:, 704:768]
        ecs2 = lnf[:, 768:832]
        drpDv = lnf[:, 896:960]
        nsc = mg_sb.ap().rearrange("p a b -> p (a b)")[:, 0:32]
        rcol = mg_sb.ap().rearrange("p a b -> p (a b)")[:, 48:50]
        smw = [(s_mm, s_mm.n), (s_se, s_se.n), (s_ve, s_ve.n)]
        k.dma("sync", nsc, norm_scale, inc=s_c4, wait=smw)
        s_vl = sem("vl")
        s_ml = sem("ml")
        s_zl = sem("zl")
        s_cb = [sem("cb0"), sem("cb1")]
        s_cbr = sem("cbr")
        s_dr = sem("dr")
        s_ur = sem("ur")
        s_ot = sem("ot")
        marks = {"amm": {}, "yev": {}, "scan": {}, "tp": {}, "gy": {},
                 "zl": {}, "sq": {}, "sm": {}, "ml": {}}
        gy_fm = v_fm
        for half in range(2):
            mlw = [(s_ve, s_ve.n), (s_se, s_se.n)] if half == 0 else \
                  [(s_ot, s_ot.n), (s_ve, s_ve.n)]
            k.dma("sync", masklv, maskl_in, inc=s_c5, wait=mlw)
            k.dma("sync", drpDv, d_rep, inc=s_c5)
            mklm = (s_c5, s_c5.n)
            for p in range(4):
                ga = half * 8 + 2 * p
                gb = ga + 1
                sla = slice(ga * 64, ga * 64 + 64)
                slb = slice(gb * 64, gb * 64 + 64)
                slab = slice(ga * 64, ga * 64 + 128)
                pp = half * 4 + p
                if pp == 0:
                    vw = [hin_done, (s_mm, s_mm.n), (s_cb[0], s_cb[0].n)]
                elif p == 0:
                    vw = [(s_mm, s_mm.n), (s_ve, s_ve.n)]
                else:
                    vw = [(s_mm, marks["amm"][pp - 1]),
                          (s_ve, marks["yev"][pp - 1]),
                          (s_se, marks["sq"].get(pp - 1, 0))]
                k.dma("sync", vtk2,
                      vtok_sp.rearrange("c s d -> (c s) d")[ga * 64:ga * 64 + 128, :],
                      inc=s_vl, wait=vw)
                mw = [hin_done] if pp == 0 else \
                     [(s_ve, marks["scan"][pp - 1]), (s_mm, marks["tp"].get(pp - 1, 0)),
                      (s_se, marks["sq"].get(pp - 1, 0))]
                k.dma("sync", mrgA, m_sp[ga], inc=s_ml, wait=mw)
                zw = [(s_se, marks["zl"][pp - 1]),
                      (s_ve, marks["gy"][(pp - 1, 31)])] if pp > 0 else \
                     [(s_mm, s_mm.n), hin_done]
                k.dma("gpsimd", zb2.rearrange("p (m t) -> p m t", m=32),
                      z_sp[:, :, ga * 64:ga * 64 + 128].rearrange("m p t -> p m t"),
                      inc=s_zl, wait=zw)
                cw = [(s_cb[p % 2], s_cb[p % 2].n)]
                k.dma("gpsimd", csrow[p % 2, 0], cs_fm[:, sla],
                      inc=s_cb[p % 2], wait=cw)
                k.dma("gpsimd", csrow[p % 2, 1], cs_fm[:, slb],
                      inc=s_cb[p % 2])
                k.dma("gpsimd", totrow[p % 2, 0],
                      cs_fm[:, ga * 64 + 63:ga * 64 + 64], inc=s_cb[p % 2])
                k.dma("gpsimd", totrow[p % 2, 1],
                      cs_fm[:, gb * 64 + 63:gb * 64 + 64], inc=s_cb[p % 2])
                rw = [(s_cb[p % 2], s_cb[p % 2].n)]
                if pp > 0:
                    rw.append((s_mm, marks["amm"][pp - 1]))
                    rw.append((s_ve, marks["yev"][pp - 1]))
                k.dma("gpsimd", att2[0:64, :],
                      csrow[p % 2, 0:1].rearrange('a b -> (a b)')[None, :].broadcast_to([64, NH * S]) if False else csrow[p % 2, 0:1, :].rearrange('a f -> a f').broadcast_to([64, NH * S]),
                      inc=s_cbr, wait=rw)
                k.dma("gpsimd", att2[64:128, :],
                      csrow[p % 2, 1:2, :].broadcast_to([64, NH * S]), inc=s_cbr)
                csb_ld = (s_cbr, s_cbr.n)
                sw = [(s_ve, s_ve.n)]
                daw = [(s_cb[p % 2], s_cb[p % 2].n),
                       (s_ve, marks["scan"].get(pp - 1, 0))]
                if p == 0:
                    daw += [(s_ot, s_ot.n), (s_ve, s_ve.n), (s_ur, s_ur.n)]
                k.dma("gpsimd", drepA,
                      totrow[p % 2, 0:1, :].broadcast_to([64, NH]),
                      inc=s_dr, wait=daw)
                k.dma("gpsimd", drepB,
                      totrow[p % 2, 1:2, :].broadcast_to([64, NH]), inc=s_dr)
                tpw = sw + [(s_se, marks["sm"].get(pp - 1, 0))]
                k.trans(ptr[1][:, 0:64], cs_fm[:, slab], ident[0:64, 0:64],
                        wait=tpw, inc=s_mm)
                k.mm(pb[0][0:64, 0:64], B_fm[:, sla], C_fm[:, sla],
                     start=True, stop=True)
                k.mm(pb[2][64:128, 0:64], B_fm[:, slb], C_fm[:, slb],
                     start=True, stop=True, inc=s_mm)
                k.act(drepA, drepA, AF.Exp,
                      wait=[s_mm, (s_dr, s_dr.n), (s_ve, s_ve.n)])
                k.act(drepB, drepB, AF.Exp)
                k.act(cstk2[:], ptr[1][:, 0:64], AF.Copy)
                k.act(ecs2[:], ptr[1][:, 0:64], AF.Exp)
                k.act(gm2[0:64, :], pb[0][0:64, 0:64], AF.Copy)
                k.act(gm2[64:128, :], pb[2][64:128, 0:64], AF.Copy, inc=s_se)
                marks["sm"][pp] = s_se.n
                a3 = att2.rearrange("t (h s) -> t h s", h=NH)
                k.v("tensor_tensor", a3, a3,
                    masklv[:, None, :].broadcast_to([128, NH, S]), ALU.add,
                    wait=[csb_ld, mklm])
                k.v("tensor_tensor", a3, a3,
                    cstk2[:, :, None].broadcast_to([128, NH, S]), ALU.subtract,
                    wait=[(s_se, marks["sm"][pp])])
                k.vdrain(s_ve)
                k.act(att2, att2, AF.Exp, wait=[s_ve], inc=s_se)
                k.v("tensor_tensor", a3, a3,
                    gm2[:, None, :].broadcast_to([128, NH, S]), ALU.mult,
                    wait=[s_se])
                k.vdrain(s_ve)
                att_done = (s_ve, s_ve.n)
                if "att" in debug and pp == 0:
                    k.dma("sync", att_sp, att2, inc=s_sp, wait=[att_done])
                    k.dma("sync", vt2_sp, vtk2, inc=s_sp,
                          wait=[(s_vl, s_vl.n)])
                    k.dma("sync", hin_sp, hbuf, inc=s_sp, wait=[hin_done])
                    k.dma("sync", sml_sp, lnf[:, 0:1152], inc=s_sp,
                          wait=[(s_se, s_se.n), (s_ve, s_ve.n)])
                for q in range(4):
                    qs = q * 1024
                    for bh in range(2):
                        qq = qs + bh * 512
                        w0 = None
                        if q == 0 and bh == 0:
                            w0 = [att_done, (s_vl, s_vl.n), (s_ml, s_ml.n),
                                  (s_se, marks["sm"][pp])]
                            if pp > 0:
                                w0 = w0 + [(s_ve, marks["yev"][pp - 1])]
                        elif bh == 0 and q > 0:
                            w0 = [(s_ve, marks["yev"][(pp, q - 1)])]
                        k.mm(pb[bh][0:64, 0:512], C_fm[:, sla],
                             hbuf[:, qq:qq + 512], start=True, stop=True,
                             wait=w0)
                        for hh in range(8):
                            hid = q * 16 + bh * 8 + hh
                            k.mm(pb[4 + bh][0:64, hh * 64:hh * 64 + 64],
                                 att2[0:64, hid * 64:hid * 64 + 64],
                                 vtk2[0:64, hid * 64:hid * 64 + 64],
                                 start=(hh == 0), stop=(hh == 7))
                            k.mm(ptr[bh][64:128, hh * 64:hh * 64 + 64],
                                 att2[64:128, hid * 64:hid * 64 + 64],
                                 vtk2[64:128, hid * 64:hid * 64 + 64],
                                 start=(hh == 0), stop=(hh == 7),
                                 inc=s_mm if hh == 7 else None)
                    interE = s_mm.n
                    k.v("tensor_tensor",
                        hbuf[:, qs:qs + 1024].rearrange("n (h p) -> n h p", h=16),
                        hbuf[:, qs:qs + 1024].rearrange("n (h p) -> n h p", h=16),
                        drepA[:, q * 16:q * 16 + 16][:, :, None]
                        .broadcast_to([64, 16, 64]), ALU.mult,
                        wait=[(s_mm, interE), (s_ml, s_ml.n),
                              (s_se, marks["sm"][pp])])
                    k.v("tensor_add", hbuf[:, qs:qs + 1024],
                        hbuf[:, qs:qs + 1024], mrgA[:, qs:qs + 1024])
                    k.vdrain(s_ve)
                    scanE = s_ve.n
                    k.dma("sync", mrgA[:, qs:qs + 1024],
                          m_sp[gb, :, qs:qs + 1024], inc=s_ml,
                          wait=[(s_ve, scanE)])
                    for bh in range(2):
                        qq = qs + bh * 512
                        k.mm(pb[2 + bh][64:128, 0:512], C_fm[:, slb],
                             hbuf[:, qq:qq + 512], start=True, stop=True,
                             wait=[(s_ve, scanE)] if bh == 0 else None,
                             inc=s_mm if bh == 1 else None)
                    k.v("tensor_tensor",
                        hbuf[:, qs:qs + 1024].rearrange("n (h p) -> n h p", h=16),
                        hbuf[:, qs:qs + 1024].rearrange("n (h p) -> n h p", h=16),
                        drepB[:, q * 16:q * 16 + 16][:, :, None]
                        .broadcast_to([64, 16, 64]), ALU.mult,
                        wait=[(s_mm, s_mm.n), (s_ml, s_ml.n)])
                    k.v("tensor_add", hbuf[:, qs:qs + 1024],
                        hbuf[:, qs:qs + 1024], mrgA[:, qs:qs + 1024])
                    k.vdrain(s_ve)
                    marks["scan"][pp] = s_ve.n
                    for bh in range(2):
                        qq = qs + bh * 512
                        hsl = slice(q * 16 + bh * 8, q * 16 + bh * 8 + 8)
                        for half_rows, pi, pa in ((slice(0, 64), pb[bh], pb[4 + bh]),
                                                  (slice(64, 128), pb[2 + bh], ptr[bh])):
                            dst = mrgY[half_rows, qq:qq + 512]
                            tq = tmpq[half_rows, :]
                            k.v("tensor_tensor",
                                dst.rearrange("t (h p) -> t h p", h=8),
                                pi[half_rows, 0:512].rearrange("t (h p) -> t h p", h=8),
                                ecs2[half_rows, hsl][:, :, None]
                                .broadcast_to([64, 8, 64]), ALU.mult,
                                wait=[(s_mm, s_mm.n), (s_ve, marks["scan"][pp]),
                                      (s_c5, s_c5.n)])
                            k.v("tensor_add", dst, dst, pa[half_rows, 0:512])
                            k.vdrain(s_ve)
                    marks["yev"][(pp, q)] = s_ve.n
                k.v("tensor_tensor",
                    att2.rearrange("t (h p) -> t h p", h=64),
                    vtk2.rearrange("t (h p) -> t h p", h=64),
                    drpDv[:, :, None].broadcast_to([128, 64, 64]), ALU.mult,
                    wait=[(s_mm, s_mm.n), (s_ve, marks["yev"][(pp, 3)]),
                          (s_c5, s_c5.n)])
                k.v("tensor_add", mrgY, mrgY, att2)
                k.vdrain(s_ve)
                marks["amm"][pp] = s_mm.n
                marks["yev"][pp] = s_ve.n
                marks["ml"][pp] = s_ml.n
                if "y" in debug:
                    k.dma("sync", y_sp[ga], mrgY[0:64, :], inc=s_sp,
                          wait=[(s_ve, s_ve.n)])
                    k.dma("sync", y_sp[gb], mrgY[64:128, :], inc=s_sp)
                k.act(vtk2, mrgY, AF.Square, accum_out=rcol[:, 0:1],
                      wait=[(s_ve, marks["yev"][pp]), (s_se, s_se.n),
                            (s_cbr, s_cbr.n)], inc=s_se)
                marks["sq"][pp] = s_se.n
                k.v("tensor_scalar", rcol[:, 0:1], rcol[:, 0:1], 1.0 / DI,
                    1e-6, ALU.mult, op1=ALU.add, wait=[s_se])
                k.vdrain(s_ve)
                k.act(rcol[:, 0:1], rcol[:, 0:1], AF.Sqrt, wait=[s_ve],
                      inc=s_se)
                k.v("reciprocal", rcol[:, 0:1], rcol[:, 0:1], wait=[s_se])
                k.vdrain(s_ve)
                k.v("tensor_scalar", mrgY, mrgY, rcol[:, 0:1], None, ALU.mult)
                k.vdrain(s_ve)
                yrms = (s_ve, s_ve.n)
                k.act(zb2, zb2, AF.Silu, wait=[(s_zl, s_zl.n)], inc=s_se)
                marks["zl"][pp] = s_se.n
                for m in range(32):
                    pslot = pb[m % 2]
                    tw = [yrms, (s_se, marks["zl"][pp])]
                    if (pp, m - 2) in marks["gy"]:
                        tw.append((s_ve, marks["gy"][(pp, m - 2)]))
                    k.trans(pslot[:, 0:128], mrgY[:, m * 128:(m + 1) * 128],
                            ident[:], wait=tw, inc=s_mm)
                    k.v("scalar_tensor_tensor",
                        gy_fm[:, m, 2 * p * 64:2 * p * 64 + 128],
                        zb2.rearrange("x (m t) -> x m t", m=32)[:, m],
                        nsc[:, m:m + 1], pslot[:, 0:128], ALU.mult,
                        ALU.mult, wait=[s_mm, (s_c4, s_c4.n)])
                    k.vdrain(s_ve)
                    marks["gy"][(pp, m)] = s_ve.n
                marks["tp"][pp] = s_mm.n
            gy_done = (s_ve, s_ve.n)
            p6km = []
            p6_evac = gy_done
            for tpair in range(2):
                for kd in range(32):
                    slot = kd % 2
                    if len(p6km) >= 2:
                        bw = [(s_mm, p6km[-2])]
                    elif half == 0 and tpair == 0:
                        bw = [gy_done, (s_vl, s_vl.n)]
                    else:
                        bw = [gy_done]
                    k.dma("sync", wob[slot], w_out[kd * 128:(kd + 1) * 128, :],
                          inc=s_wb[slot], wait=bw)
                    for tp in range(2):
                        tt = tpair * 2 + tp
                        for q in range(4):
                            pk = (pb + ptr)[tp * 4 + q]
                            mmw = None
                            if tp == 0 and q == 0:
                                mmw = [(s_wb[slot], s_wb[slot].n)]
                                if kd == 0:
                                    mmw.append(p6_evac)
                            k.mm(pk[:, 0:512],
                                 gy_fm[:, kd, tt * 128:(tt + 1) * 128],
                                 wob[slot][:, q * 512:(q + 1) * 512],
                                 start=(kd == 0), stop=(kd == 31),
                                 wait=mmw,
                                 inc=s_mm if (tp == 1 and q == 3) else None)
                    p6km.append(s_mm.n)
                for tp in range(2):
                    tt = tpair * 2 + tp
                    t0 = half * 512 + tt * 128
                    uw = [(s_ot, s_ot.n)]
                    if tpair == 0 and tp == 0:
                        uw += [(s_ve, s_ve.n), (s_se, s_se.n),
                               (s_dr, s_dr.n), (s_cbr, s_cbr.n),
                               (s_c5, s_c5.n)]
                    k.dma("sync", lntmp[:], u[3 + t0:3 + t0 + 128, :],
                          inc=s_ur, wait=uw)
                    for q in range(4):
                        pk = (pb + ptr)[tp * 4 + q]
                        k.v("tensor_add", lntmp[:, q * 512:(q + 1) * 512],
                            lntmp[:, q * 512:(q + 1) * 512], pk[:, 0:512],
                            wait=[(s_mm, p6km[-1]), (s_ur, s_ur.n)]
                            if q == 0 else None)
                    k.vdrain(s_ve)
                    k.dma("sync", out[t0:t0 + 128, :], lntmp[:], inc=s_ot,
                          wait=[(s_ve, s_ve.n)])
                p6_evac = (s_ve, s_ve.n)
        t8 = lntmp.ap()[0:8, :]
        ptot_sb = ustage.ap()[0:8].rearrange("p a b -> p (a b)")[:, 0:D]
        c0v = mg_sb.ap()[0:8].rearrange("p a b -> p (a b)")[:, 50:51]
        k.dma("sync", t8, out[0:8, :], inc=s_ur,
              wait=[(s_ot, s_ot.n), (s_cc, 3)])
        k.dma("sync", ptot_sb, p_tot, inc=s_ur,
              wait=[(s_ve, s_ve.n), (s_mm, s_mm.n)])
        k.dma("sync", c0v, core0[0:8, 0:1], inc=s_ur)
        k.v("scalar_tensor_tensor", t8, ptot_sb, c0v, t8,
            ALU.mult, ALU.add, wait=[(s_ur, s_ur.n)])
        k.vdrain(s_ve)
        k.dma("sync", out[0:8, :], t8, inc=s_ot, wait=[(s_ve, s_ve.n)])
        k.op("sync", lambda h: h.nop(),
             wait=[s_ot, s_sp, s_vt, s_m, s_xnr, s_cc, s_vl, s_ml, s_zl, s_ft, s_pb, s_c2, s_c3, s_c4, s_c5, s_dr,
                   s_cbr, s_ur, s_gld] + s_zs + s_u + s_wb + s_f + s_cb +
                  [s_const])

        k.emit()
    return nc


def _prep_inputs(L0_attn_u, L0_attn_in_proj_w, L0_attn_conv_weight,
                 L0_attn_conv_bias, L0_attn_A_log, L0_attn_dt_bias,
                 L0_attn_D_param, L0_attn_norm_scale, L0_attn_out_proj_w,
                 L0_ffn_w1, L0_ffn_w3, L0_ffn_w2, L0_ffn_wr):
    u = np.ascontiguousarray(np.asarray(L0_attn_u, np.float32).reshape(8192, D))
    upad = np.concatenate([np.zeros((HALO, D), np.float32), u], 0)
    csmask = np.ones((NH, TOK), np.float32)
    csmask[:, ::S] = 0.0
    shared = {
        "w_in": np.ascontiguousarray(L0_attn_in_proj_w, np.float32),
        "conv_w": np.ascontiguousarray(np.transpose(
            np.asarray(L0_attn_conv_weight, np.float32).reshape(4, 32, 128),
            (2, 1, 0))),
        "conv_b": np.ascontiguousarray(
            np.asarray(L0_attn_conv_bias, np.float32).reshape(32, 128).T),
        "a_log": np.asarray(L0_attn_A_log, np.float32).reshape(NH, 1),
        "dt_bias": np.asarray(L0_attn_dt_bias, np.float32).reshape(NH, 1),
        "d_rep": np.tile(np.asarray(L0_attn_D_param, np.float32).reshape(1, NH),
                         (128, 1)),
        "norm_scale": np.ascontiguousarray(
            np.asarray(L0_attn_norm_scale, np.float32).reshape(32, 128).T),
        "w_out": np.ascontiguousarray(L0_attn_out_proj_w, np.float32),
        "w1t": np.ascontiguousarray(np.asarray(L0_ffn_w1, np.float32).T),
        "w3t": np.ascontiguousarray(np.asarray(L0_ffn_w3, np.float32).T),
        "w2t": np.ascontiguousarray(np.asarray(L0_ffn_w2, np.float32).T),
        "wrt": np.ascontiguousarray(np.asarray(L0_ffn_wr, np.float32).T
                                    .reshape(16, 128, E).transpose(1, 0, 2)
                                    .reshape(128, 16 * E)),
        "ident": np.eye(128, dtype=np.float32),
        "csmask": csmask,
        "maskl": np.tile(np.where(
            np.arange(S)[None, :] >= np.arange(S)[:, None],
            0.0, -1e30).astype(np.float32), (2, 1)),
    }
    in_maps = []
    for c in range(NCORES):
        m = dict(shared)
        m["u"] = np.ascontiguousarray(upad[c * TOK:c * TOK + TOKH, :])
        cmb = np.zeros((64, NCORES), np.float32)
        cmb[:, :c] = 1.0
        m["combmask"] = cmb
        m["combmaskinv"] = (1.0 - cmb)
        m["core0"] = np.full((128, 1), 1.0 if c == 0 else 0.0, np.float32)
        in_maps.append(m)
    return in_maps


def kernel(**inputs):
    in_maps = _prep_inputs(**inputs)
    nc = build_program()
    res = run_bass_kernel_spmd(nc, in_maps, core_ids=list(range(NCORES)))
    return np.concatenate([r["out"] for r in res.results], 0).reshape(1, 8192, D)


# version 09
# speedup vs baseline: 1.2047x; 1.0216x over previous
import contextlib
import numpy as np
import concourse.bass as bass
import concourse.mybir as mybir
from concourse.bass_utils import run_bass_kernel_spmd

F32 = mybir.dt.float32
AF = mybir.ActivationFunctionType
ALU = mybir.AluOpType
AX = mybir.AxisListType

NCORES = 8
TOK = 1024
HALO = 3
TOKH = TOK + HALO
D = 2048
DI = 4096
NH = 64
NST = 64
S = 64
NCH = TOK // S
DIP = 8384
E = 8

class Sem:
    def __init__(self, handle):
        self.h = handle
        self.n = 0


class K:
    ENGS = ("tensor", "vector", "scalar", "gpsimd", "sync")

    def __init__(self, nc):
        self.nc = nc
        self.q = {e: [] for e in self.ENGS}
        self.dirty = {e: set() for e in self.ENGS}

    @staticmethod
    def _tnames(aps):
        out = set()
        for a in aps:
            if a is None or isinstance(a, (int, float)):
                continue
            t = getattr(a, "tensor", None)
            if t is not None:
                out.add(t.name)
        return out

    def op(self, eng, fn, reads=(), writes=(), inc=None, inc_k=1, wait=None,
           dma_inc=None):
        import traceback
        tag = None
        for fr in reversed(traceback.extract_stack()[:-1]):
            if fr.filename.endswith("kernel.py") and fr.name not in (
                    "op", "dma", "mm", "trans", "act", "v", "vdrain"):
                tag = f"{fr.name}:{fr.lineno}"
                break
        rn = self._tnames(reads)
        wn = self._tnames(writes)
        need_drain = bool(self.dirty[eng] & (rn | wn))
        waits = []
        if wait:
            for w in wait:
                waits.append((w.h, w.n) if isinstance(w, Sem) else (w[0].h, w[1]))
        if inc is not None:
            inc.n += inc_k
        if dma_inc is not None:
            dma_inc.n += 16
        if need_drain:
            self.dirty[eng] = set()

        def emit(h, nd=need_drain, waits=waits, fn=fn, inc=inc, inc_k=inc_k,
                 dma_inc=dma_inc, tag=tag):
            if nd:
                h.drain()
            for sh, sv in waits:
                h.wait_ge(sh, sv)
            try:
                ins = fn(h)
            except Exception as e:
                raise RuntimeError(f"emit failed at record site {tag}: {e}") from e
            if ins is not None:
                if inc is not None:
                    ins.then_inc(inc.h, inc_k)
                if dma_inc is not None:
                    ins.then_inc(dma_inc.h, 16)

        self.q[eng].append(emit)
        self.dirty[eng] |= wn

    def dma(self, eng, out, in_, inc=None, wait=None):
        self.op(eng, lambda h: h.dma_start(out=out, in_=in_),
                reads=(in_,), writes=(out,), dma_inc=inc, wait=wait)

    def mm(self, out, lhsT, rhs, start, stop, inc=None, wait=None):
        self.op("tensor",
                lambda h: h.matmul(out, lhsT, rhs, start=start, stop=stop),
                reads=(lhsT, rhs), writes=(out,), inc=inc, wait=wait)

    def trans(self, out, in_, ident, inc=None, wait=None):
        self.op("tensor", lambda h: h.transpose(out, in_, ident),
                reads=(in_, ident), writes=(out,), inc=inc, wait=wait)

    def act(self, out, in_, func, bias=0.0, scale=1.0, accum_out=None, inc=None,
            wait=None):
        self.op("scalar",
                lambda h: h.activation(out, in_, func, bias=bias, scale=scale,
                                       accum_out=accum_out),
                reads=(in_, None if isinstance(bias, float) else bias,
                       None if isinstance(scale, float) else scale),
                writes=(out, accum_out), inc=inc, wait=wait)

    def v(self, method, out, *ins, inc=None, wait=None, **kw):
        def fn(h, method=method, out=out, ins=ins, kw=kw):
            return getattr(h, method)(out, *ins, **kw)
        self.op("vector", fn, reads=ins, writes=(out,), inc=inc, wait=wait)

    def vdrain(self, inc):
        self.op("vector", lambda h: h.drain(), inc=inc)
        self.dirty["vector"] = set()

    def emit(self):
        with self.nc.Block() as block:
            @block.tensor
            def _(h):
                for f in self.q["tensor"]:
                    f(h)

            @block.vector
            def _(h):
                for f in self.q["vector"]:
                    f(h)

            @block.scalar
            def _(h):
                for f in self.q["scalar"]:
                    f(h)

            @block.gpsimd
            def _(h):
                for f in self.q["gpsimd"]:
                    f(h)

            @block.sync
            def _(h):
                for f in self.q["sync"]:
                    f(h)


def build_program(debug=(), stage=1):
    nc = bass.Bass("TRN2", target_bir_lowering=False, debug=False)

    def dram_in(name, shape):
        return nc.dram_tensor(name, shape, F32, kind="ExternalInput").ap()

    u = dram_in("u", [TOKH, D])
    w_in = dram_in("w_in", [D, DIP])
    conv_w = dram_in("conv_w", [128, 32, 4])
    conv_b = dram_in("conv_b", [128, 32])
    a_log = dram_in("a_log", [NH, 1])
    dt_bias = dram_in("dt_bias", [NH, 1])
    d_rep = dram_in("d_rep", [128, NH])
    norm_scale = dram_in("norm_scale", [128, 32])
    w_out = dram_in("w_out", [DI, D])
    w1t = dram_in("w1t", [D, D])
    w3t = dram_in("w3t", [D, D])
    w2t = dram_in("w2t", [D, D])
    wrt = dram_in("wrt", [128, 16 * E])
    ident_in = dram_in("ident", [128, 128])
    maskl_in = dram_in("maskl", [128, S])
    combmaskinv = dram_in("combmaskinv", [64, NCORES])
    csmask_in = dram_in("csmask", [NH, TOK])
    combmask = dram_in("combmask", [64, NCORES])
    core0 = dram_in("core0", [128, 1])

    out = nc.dram_tensor("out", [TOK, D], F32, kind="ExternalOutput").ap()

    def scratch(name, shape, dbg=False):
        kind = "ExternalOutput" if dbg else "Internal"
        if dbg:
            return nc.dram_tensor(name, shape, F32, kind="ExternalOutput").ap()
        return nc.dram_tensor(name, shape, F32).ap()

    xn_sp = scratch("xn_sp", [16, 128, TOKH], "xn" in debug)
    z_sp = scratch("z_sp", [32, 128, TOK], "z" in debug)
    bcdt_sp = scratch("bcdt_sp", [4, 64, TOK], "bcdt" in debug)
    vfm_sp = scratch("vfm_sp", [2, 32, 128, 512], "v" in debug)
    vtok_sp = scratch("vtok_sp", [NCH, 64, DI], "vtok" in debug)
    m_sp = scratch("m_sp", [NCH, 64, DI], "m" in debug)
    ft_sp = scratch("ft_sp", [65, DI], "ft" in debug)
    p_sp = scratch("p_sp", [8, D], "p" in debug)
    mg_sp = scratch("mg_sp", [128, 64], "p" in debug)
    h2_sp = scratch("h2_sp", [16, 128, TOK], "h2" in debug)
    f_bnc = nc.dram_tensor("f_bnc", [64, DI], F32).ap()
    t_bnc = nc.dram_tensor("t_bnc", [64, 1], F32).ap()
    f_gat = nc.dram_tensor("f_gat", [NCORES, 64, DI], F32).ap()
    t_gat = nc.dram_tensor("t_gat", [NCORES, 64, 1], F32).ap()
    csrow = nc.dram_tensor("csrow", [2, 2, NH * S], F32).ap()
    totrow = nc.dram_tensor("totrow", [2, 2, NH], F32).ap()
    rrow = nc.dram_tensor("rrow", [2, 1, S], F32).ap()
    p_bnc = nc.dram_tensor("p_bnc", [8, D], F32).ap()
    p_tot = nc.dram_tensor("p_tot", [8, D], F32).ap()
    y_sp = scratch("y_sp", [NCH, 64, DI], "y" in debug)
    att_sp = scratch("att_sp", [128, DI], "att" in debug)
    vt2_sp = scratch("vt2_sp", [128, DI], "att" in debug)
    hin_sp = scratch("hin_sp", [64, DI], "att" in debug)
    sml_sp = scratch("sml_sp", [128, 1152], "att" in debug)
    hq_sp = scratch("hq_sp", [64, DI], "att" in debug)

    k = K(nc)
    sems = {}

    def sem(name):
        s = Sem(stackctx.enter_context(nc.semaphore(name)))
        sems[name] = s
        return s

    stackctx = contextlib.ExitStack()
    with stackctx:
        def sbuf(name, shape, dtype=F32):
            return stackctx.enter_context(nc.sbuf_tensor(name, shape, dtype))

        def psum(name):
            return stackctx.enter_context(nc.psum_tensor(name, [128, 512], F32))

        s_const = sem("const")
        s_u = [sem("u0"), sem("u1")]
        s_wb = [sem("wb0"), sem("wb1")]
        s_zs = [sem(f"zs{i}") for i in range(4)]
        s_mm = sem("mm")
        s_ve = sem("ve")
        s_se = sem("se")
        s_sp = sem("sp")
        s_gld = sem("gld")
        s_cc = sem("cc")
        s_vt = sem("vt")
        s_ft = sem("ft")
        s_pb = sem("pbs")
        s_c2 = sem("c2")
        s_c3 = sem("c3")
        s_c4 = sem("c4")
        s_c5 = sem("c5")
        s_m = sem("m")
        s_xnr = sem("xnr")

        ident = sbuf("identt", [128, 128])
        xn_fm = sbuf("xn_fm", [128, 16, TOKH])
        B_fm = sbuf("B_fm", [64, TOK])
        C_fm = sbuf("C_fm", [64, TOK])
        dt_fm = sbuf("dt_fm", [64, TOK])
        la_fm = sbuf("la_fm", [64, TOK])
        cs_fm = sbuf("cs_fm", [64, TOK])
        csmask = sbuf("csmask_t", [64, TOK])
        acol = sbuf("acol", [64, 1])
        dtb = sbuf("dtb", [64, 1])
        convw = sbuf("convw_t", [128, 32, 4])
        convb = sbuf("convb_t", [128, 32])
        v_fm = sbuf("v_fm", [128, 32, 512])
        wband = sbuf("wband", [128, 2, 512])
        ustage = sbuf("ustage", [128, 2, D])
        lnm = sbuf("lnm", [128, 2])
        consts1 = sbuf("consts1", [128, 1])
        lntmp = sbuf("lntmp", [128, D])
        xtmp = sbuf("xtmp", [128, 2, 515])
        zstage = sbuf("zstage", [128, 4, 512])
        h_loc = ustage.ap()[0:64].rearrange("p a b -> p (a b)")[:, 0:DI]
        xnflat = xn_fm.ap()[0:64].rearrange("p a b -> p (a b)")
        xnflat128 = xn_fm.ap().rearrange("p a b -> p (a b)")
        vt_sb = xnflat[:, 0:DI]
        m_sb = xnflat[:, DI:2 * DI]
        b_tok = sbuf("b_tok", [64, 64])
        e2s = sbuf("e2s", [64, NCH, 64])
        tmp2_fm = sbuf("tmp2_fm", [64, 64])
        cs_tok = sbuf("cs_tok", [64, NCH, 64])
        ecs_tok = sbuf("ecs_tok", [64, NCH, 64])
        drep = sbuf("drep", [64, 64])
        tcol = sbuf("tcol", [64, 1])

        pb = [psum(f"pb{i}") for i in range(6)]
        ptr = [psum("pt0"), psum("pt1")]

        k.dma("sync", ident[:], ident_in, inc=s_const)
        k.dma("sync", csmask[:], csmask_in, inc=s_const)
        k.dma("sync", acol[:], a_log, inc=s_const)
        k.dma("sync", dtb[:], dt_bias, inc=s_const)
        k.dma("sync", convw[:], conv_w, inc=s_const)
        k.dma("sync", convb[:], conv_b, inc=s_const)
        k.op("vector", lambda h: h.memset(consts1[:], 1.0),
             writes=(consts1[:],))
        k.vdrain(s_ve)
        k.act(acol[:], acol[:], AF.Exp, wait=[s_const], inc=s_se)

        umarks = []
        semarks = []
        pmarks = {}
        tposed = 0
        for t in range(9):
            rows = 128 if t < 8 else 3
            r0 = t * 128
            ub = ustage[:, t % 2]
            w = None
            if t >= 2:
                w = [(s_mm, umarks[t - 2]), (s_se, semarks[t - 2])]
            k.dma("sync", ub[0:rows, :], u[r0:r0 + rows, :], inc=s_u[t % 2],
                  wait=w)
            k.v("reduce_sum", lnm[0:rows, 0:1], ub[0:rows, :], axis=AX.X,
                wait=[s_u[t % 2]])
            k.v("tensor_scalar_mul", lnm[0:rows, 0:1], lnm[0:rows, 0:1], 1.0 / D)
            k.v("tensor_single_scalar", ub[0:rows, :], ub[0:rows, :],
                lnm[0:rows, 0:1], ALU.subtract)
            k.vdrain(s_ve)
            k.act(lntmp[0:rows, :], ub[0:rows, :], AF.Square,
                  accum_out=lnm[0:rows, 1:2], wait=[s_ve], inc=s_se)
            k.v("tensor_scalar", lnm[0:rows, 1:2], lnm[0:rows, 1:2], 1.0 / D,
                1e-5, ALU.mult, op1=ALU.add, wait=[s_se])
            k.vdrain(s_ve)
            k.act(lnm[0:rows, 1:2], lnm[0:rows, 1:2], AF.Sqrt, wait=[s_ve],
                  inc=s_se)
            k.v("reciprocal", lnm[0:rows, 1:2], lnm[0:rows, 1:2], wait=[s_se])
            k.v("tensor_single_scalar", ub[0:rows, :], ub[0:rows, :],
                lnm[0:rows, 1:2], ALU.mult)
            k.vdrain(s_ve)
            for kd in range(16):
                pslot = ptr[tposed % 2]
                tw = [s_ve] if kd == 0 else []
                if tposed >= 2:
                    tw = tw + [(s_se, pmarks[tposed - 2])]
                k.trans(pslot[:, 0:rows], ub[0:rows, kd * 128:(kd + 1) * 128],
                        ident[0:rows, 0:rows], wait=tw or None, inc=s_mm)
                k.act(xn_fm[:, kd, r0:r0 + rows], pslot[:, 0:rows], AF.Copy,
                      wait=[s_mm], inc=s_se)
                pmarks[tposed] = s_se.n
                tposed += 1
            umarks.append(s_mm.n)
            semarks.append(s_se.n)
        xn_done = (s_se, s_se.n)
        for kd in range(16):
            k.dma("sync", xn_sp[kd], xn_fm[:, kd, :], inc=s_sp, wait=[xn_done])
        xnsp_mark = (s_sp, s_sp.n)

        zmov = [(3, 0), (515, 3)]
        zslot_marks = []
        bank_free = [(s_se, s_se.n), (s_ve, s_ve.n)]
        wmarks = []

        def proj_pass(tiles, evac, xn_cols=zmov, ncols=512, bankset=None):
            nonlocal bank_free, wmarks
            nt = len(tiles)
            banks = bankset if bankset is not None else pb
            for kd in range(16):
                c0 = tiles[0][0]
                c1 = tiles[-1][0] + tiles[-1][1]
                slot = len(wmarks) % 2
                wb = wband[:, slot, 0:c1 - c0]
                lw = None
                if len(wmarks) >= 2:
                    lw = [(s_mm, wmarks[-2])]
                k.dma("sync", wb, w_in[kd * 128:(kd + 1) * 128, c0:c1],
                      inc=s_wb[slot], wait=lw)
                for mi, (wc0, wm) in enumerate(tiles):
                    lhsT = wb[:, wc0 - c0:wc0 - c0 + wm]
                    for (x0, pbase) in xn_cols:
                        mmw = []
                        if mi == 0 and (x0, pbase) == xn_cols[0]:
                            mmw.append((s_wb[slot], s_wb[slot].n))
                            if kd == 0:
                                mmw += bank_free
                        k.mm(banks[pbase + mi][0:wm, 0:ncols], lhsT,
                             xn_fm[:, kd, x0:x0 + ncols],
                             start=(kd == 0), stop=(kd == 15),
                             wait=mmw or None,
                             inc=s_mm if (mi == nt - 1 and (x0, pbase) == xn_cols[-1]) else None)
                wmarks.append(s_mm.n)
            evac()

        zsets = [[pb[0], pb[1], pb[2], pb[3]], [pb[4], pb[5], ptr[0], ptr[1]]]
        bfree2 = {0: bank_free, 1: bank_free}
        zmov2 = [(3, 0), (515, 2)]
        for gi, g0 in enumerate(range(0, 32, 2)):
            gt = [(m * 128, 128) for m in range(g0, min(g0 + 2, 32))]
            bs = zsets[gi % 2]

            def z_evac(g0=g0, gt=gt, bs=bs, gi=gi):
                nonlocal bank_free
                for mi in range(len(gt)):
                    m = g0 + mi
                    for hi in range(2):
                        slot = len(zslot_marks) % 4
                        zw = [(s_mm, s_mm.n)]
                        if len(zslot_marks) >= 4:
                            zw.append((s_zs[slot], zslot_marks[-4]))
                        k.act(zstage[:, slot], bs[hi * 2 + mi][:, 0:512],
                              AF.Copy, wait=zw, inc=s_se)
                        k.dma("sync", z_sp[m, :, hi * 512:hi * 512 + 512],
                              zstage[:, slot], inc=s_zs[slot], wait=[s_se])
                        zslot_marks.append(s_zs[slot].n)
                bfree2[gi % 2] = [(s_se, s_se.n)]
                bank_free = [(s_se, s_se.n)]

            bank_free = bfree2[gi % 2]
            proj_pass(gt, z_evac, xn_cols=zmov2, bankset=bs)
        bank_free = [(s_se, s_se.n)]

        def bcdt_evac():
            nonlocal bank_free
            for mi, (nm, dest) in enumerate((("B", B_fm), ("C", C_fm),
                                             ("dt", dt_fm))):
                for hi in range(2):
                    src = pb[hi * 3 + mi][0:64, 0:512]
                    dc = hi * 512
                    if nm in ("B", "C"):
                        k.v("tensor_relu", dest[:, dc:dc + 512], src,
                            wait=[(s_mm, s_mm.n)])
                        k.v("tensor_sub", lntmp[0:64, 0:512], src,
                            dest[:, dc:dc + 512])
                        k.vdrain(s_ve)
                        k.act(lntmp[0:64, 512:1024], lntmp[0:64, 0:512], AF.Exp,
                              wait=[s_ve], inc=s_se)
                        k.v("tensor_add", dest[:, dc:dc + 512],
                            dest[:, dc:dc + 512], lntmp[0:64, 512:1024],
                            wait=[s_se])
                        k.vdrain(s_ve)
                    else:
                        t0 = lntmp[0:64, 0:512]
                        t1 = lntmp[0:64, 512:1024]
                        k.v("tensor_scalar_add", t0, src, dtb[:, 0:1],
                            wait=[(s_mm, s_mm.n)])
                        k.v("tensor_scalar_mul", t1, t0, -1.0)
                        k.v("tensor_tensor", t1, t1, t0, ALU.min)
                        k.vdrain(s_ve)
                        k.act(t1, t1, AF.Exp, wait=[s_ve])
                        k.op("scalar", lambda h: h.drain())
                        k.act(t1, t1, AF.Ln, bias=1.0 if False else consts1[0:64, 0:1],
                              inc=s_se)
                        k.v("tensor_relu", dest[:, dc:dc + 512], t0, wait=[s_se])
                        k.v("tensor_add", dest[:, dc:dc + 512],
                            dest[:, dc:dc + 512], t1)
                        k.vdrain(s_ve)
            bank_free = [(s_se, s_se.n), (s_ve, s_ve.n)]
        proj_pass([(8192, 64), (8256, 64), (8320, 64)], bcdt_evac)

        k.v("tensor_scalar", la_fm[:], dt_fm[:], acol[:, 0:1], -1.0,
            ALU.mult, op1=ALU.mult, wait=[s_se, s_ve])
        k.op("vector", lambda h: h.tensor_tensor_scan(
            cs_fm[:], csmask[:], la_fm[:], 0.0, ALU.mult, ALU.add),
            reads=(csmask[:], la_fm[:]), writes=(cs_fm[:],))
        k.vdrain(s_ve)
        if "bcdt" in debug:
            k.dma("sync", bcdt_sp[0], B_fm[:], inc=s_sp, wait=[s_ve])
            k.dma("sync", bcdt_sp[1], C_fm[:], inc=s_sp)
            k.dma("sync", bcdt_sp[2], dt_fm[:], inc=s_sp)
            k.dma("sync", bcdt_sp[3], cs_fm[:], inc=s_sp)

        xt_marks = {}
        ptm = {}
        mqm = {}
        mmvt = {}
        ttm = {}
        ttm2 = {}
        ptm2 = {}
        for half in range(2):
            xn0 = half * 512
            if half == 1:
                xw = [(s_ve, s_ve.n), (s_se, s_se.n), (s_mm, s_mm.n),
                      (s_vt, s_vt.n), (s_m, s_m.n)]
                for kd in range(16):
                    k.dma("sync", xn_fm[:, kd, :], xn_sp[kd], inc=s_xnr,
                          wait=xw if kd == 0 else None)
            vfree = None
            if half == 1 and "v" in debug:
                vfree = (s_sp, s_sp.n)
            xsets = [[pb[0], pb[1], pb[2], pb[3]],
                     [pb[4], pb[5], ptr[0], ptr[1]]]
            xfree = {0: list(bank_free), 1: list(bank_free)}
            for gi, g0 in enumerate(range(0, 32, 2)):
                gt = [(4096 + m * 128, 128) for m in range(g0, min(g0 + 2, 32))]
                nt = len(gt)
                bs = xsets[gi % 2]
                for kd in range(16):
                    c0 = gt[0][0]
                    c1 = gt[-1][0] + gt[-1][1]
                    slot = len(wmarks) % 2
                    wb = wband[:, slot, 0:c1 - c0]
                    lw = [(s_mm, wmarks[-2])] if len(wmarks) >= 2 else None
                    k.dma("sync", wb, w_in[kd * 128:(kd + 1) * 128, c0:c1],
                          inc=s_wb[slot], wait=lw)
                    for mi in range(nt):
                        wc0, wm = gt[mi]
                        lhsT = wb[:, wc0 - c0:wc0 - c0 + wm]
                        mmw = []
                        if mi == 0:
                            mmw.append((s_wb[slot], s_wb[slot].n))
                            if kd == 0:
                                mmw += xfree[gi % 2] + [(s_xnr, s_xnr.n)]
                        k.mm(bs[mi][:, 0:512], lhsT,
                             xn_fm[:, kd, xn0:xn0 + 512],
                             start=(kd == 0), stop=(kd == 15), wait=mmw or None)
                        k.mm(bs[2 + mi][:, 0:3], lhsT,
                             xn_fm[:, kd, xn0 + 512:xn0 + 515],
                             start=(kd == 0), stop=(kd == 15),
                             inc=s_mm if (mi == nt - 1) else None)
                    wmarks.append(s_mm.n)
                for mi in range(nt):
                    m = g0 + mi
                    xt = xtmp[:, m % 2]
                    xw = [(s_mm, s_mm.n)]
                    if (m - 2) in xt_marks:
                        xw.append((s_ve, xt_marks[m - 2]))
                    k.act(xt[:, 0:512], bs[mi][:, 0:512], AF.Copy, wait=xw)
                    k.act(xt[:, 512:515], bs[2 + mi][:, 0:3], AF.Copy, inc=s_se)
                    cw = convw[:, m]
                    vdst = v_fm[:, m, :]
                    vw = [s_se]
                    if vfree is not None:
                        vw.append(vfree)
                    k.v("tensor_scalar_mul", vdst, xt[:, 0:512], cw[:, 0:1],
                        wait=vw)
                    for kk in range(1, 4):
                        k.v("scalar_tensor_tensor", vdst, xt[:, kk:kk + 512],
                            cw[:, kk:kk + 1], vdst, ALU.mult, ALU.add)
                    k.vdrain(s_ve)
                    xt_marks[m] = s_ve.n
                    k.act(vdst, vdst, AF.Silu, bias=convb[:, m:m + 1],
                          wait=[s_ve], inc=s_se)
                xfree[gi % 2] = [(s_se, s_se.n)]
                bank_free = [(s_se, s_se.n)]
                if "v" in debug:
                    for mi in range(nt):
                        m = g0 + mi
                        k.dma("sync", vfm_sp[half, m], v_fm[:, m, :],
                              inc=s_sp, wait=[(s_se, s_se.n)])

            if half == 0:
                k.op("vector", lambda h: h.memset(h_loc, 0.0),
                     writes=(h_loc,), wait=[(s_mm, s_mm.n), (s_se, s_se.n)])
                k.vdrain(s_ve)
            vconsumed = []
            for c in range(8):
                gc = half * 8 + c
                csl = slice(gc * 64, gc * 64 + 64)
                for m in range(32):
                    pslot = ptr[m % 2]
                    tw = [(s_se, s_se.n)] if m < 2 else [(s_se, ptm[m - 2])]
                    if m == 0:
                        tw = tw + [(s_mm, mmvt.get("x", 0))]
                    ew_extra = [xnsp_mark] if (m == 0 and c == 0 and half == 0) else []
                    k.trans(pslot[0:64, 0:128], v_fm[:, m, c * 64:(c + 1) * 64],
                            ident[:], wait=tw, inc=s_mm)
                    ew = [s_mm] if m > 0 else [s_mm, (s_ve, s_ve.n)]
                    k.act(vt_sb[:, m * 128:(m + 1) * 128], pslot[0:64, 0:128],
                          AF.Copy, wait=ew + ew_extra, inc=s_se)
                    ptm[m] = s_se.n
                vt_done = (s_se, s_se.n)
                vconsumed.append(s_mm.n)
                k.dma("sync", vtok_sp[gc], vt_sb, inc=s_vt, wait=[vt_done])
                k.v("tensor_scalar", tmp2_fm[:], cs_fm[:, csl],
                    cs_fm[:, gc * 64 + 63:gc * 64 + 64], -1.0, ALU.subtract,
                    op1=ALU.mult, wait=[vt_done])
                k.vdrain(s_ve)
                k.trans(ptr[0][0:64, 0:64], tmp2_fm[:], ident[0:64, 0:64],
                        wait=[s_ve], inc=s_mm)
                k.trans(ptr[1][0:64, 0:64], B_fm[:, csl], ident[0:64, 0:64],
                        inc=s_mm)
                k.trans(ptr[0][0:64, 64:128], cs_fm[:, csl], ident[0:64, 0:64],
                        inc=s_mm)
                k.act(e2s[:, gc], ptr[0][0:64, 0:64], AF.Exp, wait=[s_mm])
                k.act(b_tok[:], ptr[1][0:64, 0:64], AF.Copy)
                k.act(cs_tok[:, gc], ptr[0][0:64, 64:128], AF.Copy)
                k.act(ecs_tok[:, gc], ptr[0][0:64, 64:128], AF.Exp, inc=s_se)
                ptm[0] = s_se.n
                ptm[1] = s_se.n
                k.v("tensor_mul", drep[:], e2s[:, gc], ecs_tok[:, gc],
                    wait=[s_se])
                k.v("tensor_tensor",
                    vt_sb.rearrange("t (h p) -> t h p", h=64),
                    vt_sb.rearrange("t (h p) -> t h p", h=64),
                    e2s[:, gc][:, :, None].broadcast_to([64, 64, 64]), ALU.mult,
                    wait=[s_vt])
                k.vdrain(s_ve)
                for q in range(8):
                    k.mm(pb[q % 6][0:64, 0:512], b_tok[:],
                         vt_sb[:, q * 512:(q + 1) * 512], start=True,
                         stop=True,
                         wait=[s_ve] if q == 0 else ([(s_se, mqm[q - 6])] if q >= 6 else None),
                         inc=s_mm)
                    macw = [s_mm]
                    if q == 0:
                        macw.append((s_m, s_m.n))
                    k.act(m_sb[:, q * 512:(q + 1) * 512], pb[q % 6][0:64, 0:512],
                          AF.Copy, wait=macw, inc=s_se)
                    mqm[q] = s_se.n
                mmvt["x"] = s_mm.n
                m_done = (s_se, s_se.n)
                k.dma("sync", m_sp[gc], m_sb, inc=s_m, wait=[m_done])
                k.v("tensor_tensor",
                    h_loc.rearrange("n (h p) -> n h p", h=64),
                    h_loc.rearrange("n (h p) -> n h p", h=64),
                    drep[:, :, None].broadcast_to([64, 64, 64]), ALU.mult,
                    wait=[m_done])
                k.v("tensor_add", h_loc, h_loc, m_sb[:])
                k.vdrain(s_ve)
                mspill_scan = [(s_m, s_m.n), (s_ve, s_ve.n)]
            bank_free = bank_free + [(s_se, s_se.n), (s_ve, s_ve.n)]

        k.v("reduce_sum", tcol[:], la_fm[:], axis=AX.X, wait=[(s_se, s_se.n)])
        k.vdrain(s_ve)
        k.act(tcol[:], tcol[:], AF.Exp, wait=[s_ve], inc=s_se)
        k.dma("sync", f_bnc, h_loc, inc=s_ft, wait=[(s_ve, s_ve.n)])
        k.dma("sync", t_bnc, tcol[:], inc=s_ft, wait=[(s_se, s_se.n)])
        if "ft" in debug:
            k.dma("sync", ft_sp[0:64, :], h_loc, inc=s_sp,
                  wait=[(s_ve, s_ve.n)])
            k.dma("sync", ft_sp[64:65, 0:64], tcol[:, 0:1].rearrange("a b -> b a") if False else tcol[0:64, 0:1],
                  inc=s_sp, wait=[(s_se, s_se.n)])
        cc_ready = (s_ft, s_ft.n)
        k.op("gpsimd", lambda h: h.collective_compute(
            "AllGather", ALU.bypass, replica_groups=[list(range(NCORES))],
            ins=[f_bnc.opt()], outs=[f_gat.opt()]), wait=[cc_ready], inc=s_cc)
        k.op("gpsimd", lambda h: h.collective_compute(
            "AllGather", ALU.bypass, replica_groups=[list(range(NCORES))],
            ins=[t_bnc.opt()], outs=[t_gat.opt()]), wait=None, inc=s_cc)

        wrt_sb = sbuf("wrt_sb", [128, 16 * E])
        mg_sb = sbuf("mg_sb", [128, 8, E])
        lgt = sbuf("lgt", [128, E + 4])
        p_part = zstage.ap()[0:8].rearrange("p a b -> p (a b)")[:, 0:D]
        k.dma("sync", wrt_sb[:], wrt, inc=s_c2)
        xw = [(s_ve, s_ve.n), (s_se, s_se.n), (s_mm, s_mm.n),
              (s_vt, s_vt.n), (s_m, s_m.n)]
        for kd in range(16):
            k.dma("sync", xn_fm[:, kd, :], xn_sp[kd], inc=s_xnr,
                  wait=xw if kd == 0 else None)
        for tt in range(8):
            tc = 3 + tt * 128
            for kd in range(16):
                k.mm(pb[0][0:128, 0:E], xn_fm[:, kd, tc:tc + 128],
                     wrt_sb[:, kd * E:(kd + 1) * E],
                     start=(kd == 0), stop=(kd == 15),
                     wait=([(s_xnr, s_xnr.n), (s_c2, s_c2.n)] + bank_free +
                           [(s_se, s_se.n)]) if (tt == 0 and kd == 0) else
                          ([(s_se, ttm[tt - 1])] if kd == 0 and tt > 0 else None),
                     inc=s_mm if kd == 15 else None)
            lcw = [s_mm] + ([(s_ve, ttm2[tt - 1])] if tt > 0 else [])
            k.act(lgt[:, 0:E], pb[0][:, 0:E], AF.Copy, wait=lcw, inc=s_se)
            k.v("reduce_max", lgt[:, E:E + 1], lgt[:, 0:E], axis=AX.X,
                wait=[s_se])
            k.v("tensor_single_scalar", mg_sb[:, tt], lgt[:, 0:E],
                lgt[:, E:E + 1], ALU.is_equal)
            k.vdrain(s_ve)
            k.v("scalar_tensor_tensor", lgt[:, 0:E], mg_sb[:, tt], -1e30,
                lgt[:, 0:E], ALU.mult, ALU.add)
            k.v("reduce_max", lgt[:, E + 1:E + 2], lgt[:, 0:E], axis=AX.X)
            k.vdrain(s_ve)
            k.v("tensor_single_scalar", lgt[:, 0:E], lgt[:, 0:E],
                lgt[:, E + 1:E + 2], ALU.is_equal)
            k.v("tensor_sub", lgt[:, E + 2:E + 3], lgt[:, E:E + 1],
                lgt[:, E + 1:E + 2])
            k.vdrain(s_ve)
            k.act(lgt[:, E + 2:E + 3], lgt[:, E + 2:E + 3], AF.Sigmoid,
                  wait=[s_ve], inc=s_se)
            k.v("tensor_scalar", lgt[:, E + 3:E + 4], lgt[:, E + 2:E + 3],
                -1.0, 1.0, ALU.mult, op1=ALU.add, wait=[s_se])
            k.vdrain(s_ve)
            k.v("tensor_scalar", mg_sb[:, tt], mg_sb[:, tt],
                lgt[:, E + 2:E + 3], None, ALU.mult, wait=[s_ve])
            k.v("scalar_tensor_tensor", mg_sb[:, tt], lgt[:, 0:E],
                lgt[:, E + 3:E + 4], mg_sb[:, tt], ALU.mult, ALU.add)
            k.vdrain(s_ve)
            ttm[tt] = s_se.n
            ttm2[tt] = s_ve.n
        mg_done = (s_ve, s_ve.n)
        if "p" in debug:
            k.dma("sync", mg_sp,
                  mg_sb.ap().rearrange("p a b -> p (a b)"), inc=s_sp,
                  wait=[mg_done])

        h1v = v_fm.ap().rearrange("p a b -> p (a b)")
        h2v = xnflat128
        def ffn_fm(wsrc, mov, dest_fn, func, first_wait):
            for g0 in range(0, 16, 3):
                gt = list(range(g0, min(g0 + 3, 16)))
                nt = len(gt)
                for kd in range(16):
                    slot = len(wmarks) % 2
                    wb = wband[:, slot, 0:nt * 128]
                    lw = [(s_mm, wmarks[-2])] if len(wmarks) >= 2 else None
                    k.dma("sync", wb,
                          wsrc[kd * 128:(kd + 1) * 128,
                               g0 * 128:g0 * 128 + nt * 128],
                          inc=s_wb[slot], wait=lw)
                    for mi in range(nt):
                        lhsT = wb[:, mi * 128:(mi + 1) * 128]
                        for hi in range(2):
                            mmw = []
                            if mi == 0 and hi == 0:
                                mmw.append((s_wb[slot], s_wb[slot].n))
                                if kd == 0:
                                    mmw += bank_free2[0]
                                    if g0 == 0:
                                        mmw += first_wait
                            k.mm(pb[hi * 3 + mi][:, 0:512], lhsT,
                                 mov(kd, hi), start=(kd == 0), stop=(kd == 15),
                                 wait=mmw or None,
                                 inc=s_mm if (mi == nt - 1 and hi == 1) else None)
                    wmarks.append(s_mm.n)
                for mi in range(nt):
                    m = g0 + mi
                    for hi in range(2):
                        k.act(dest_fn(m, hi), pb[hi * 3 + mi][:, 0:512], func,
                              wait=[(s_mm, s_mm.n)], inc=s_se)
                bank_free2[0] = [(s_se, s_se.n)]

        bank_free2 = [bank_free + [mg_done, (s_se, s_se.n)]]
        ffn_fm(w1t, lambda kd, hi: xn_fm[:, kd, 3 + hi * 512:3 + hi * 512 + 512],
               lambda m, hi: h1v[:, m * 1024 + hi * 512:m * 1024 + hi * 512 + 512],
               AF.Silu, [])
        ffn_fm(w3t, lambda kd, hi: h1v[:, kd * 1024 + hi * 512:kd * 1024 + hi * 512 + 512],
               lambda m, hi: h2v[:, m * 1024 + hi * 512:m * 1024 + hi * 512 + 512],
               AF.Silu, [(s_mm, s_mm.n)])
        if "h2" in debug:
            for m in range(16):
                k.dma("sync", h2_sp[m],
                      h2v[:, m * 1024:(m + 1) * 1024], inc=s_sp,
                      wait=[(s_se, s_se.n)])
        ffn_fm(w2t, lambda kd, hi: h2v[:, kd * 1024 + hi * 512:kd * 1024 + hi * 512 + 512],
               lambda m, hi: h1v[:, m * 1024 + hi * 512:m * 1024 + hi * 512 + 512],
               AF.Copy, [(s_mm, s_mm.n)])
        htok = lntmp.ap()
        hw0 = [(s_se, s_se.n), (s_ve, s_ve.n)]
        for tt in range(8):
            for dtile in range(16):
                pslot = ptr[dtile % 2]
                tw = ([(s_se, ptm2[(tt, dtile - 2)])] if dtile >= 2
                      else [(s_se, ptm2.get((tt - 1, 14 + dtile), s_se.n))])
                if tt == 0 and dtile < 2:
                    tw = [(s_se, s_se.n)]
                if tt == 0 and dtile == 0:
                    tw += hw0
                k.trans(pslot[:, 0:128],
                        h1v[:, dtile * 1024 + tt * 128:dtile * 1024 + tt * 128 + 128],
                        ident[:], wait=tw, inc=s_mm)
                k.act(htok[:, dtile * 128:(dtile + 1) * 128], pslot[:, 0:128],
                      AF.Copy, wait=[s_mm], inc=s_se)
                ptm2[(tt, dtile)] = s_se.n
            for q in range(4):
                k.mm(pb[q][0:8, 0:512], mg_sb[:, tt],
                     htok[:, q * 512:(q + 1) * 512],
                     start=(tt == 0), stop=(tt == 7),
                     wait=[(s_se, s_se.n)] if q == 0 else None,
                     inc=s_mm if q == 3 else None)
        for q in range(4):
            k.act(p_part[:, q * 512:(q + 1) * 512], pb[q][0:8, 0:512], AF.Copy,
                  wait=[s_mm] + ([(sz, sz.n) for sz in s_zs] if q == 0 else []),
                  inc=s_se)
        if "p" in debug:
            k.dma("sync", p_sp, p_part, inc=s_sp, wait=[(s_se, s_se.n)])

        k.dma("sync", p_bnc, p_part, inc=s_pb, wait=[(s_se, s_se.n)])
        pcc_ready = (s_pb, s_pb.n)
        k.op("gpsimd", lambda h: h.collective_compute(
            "AllReduce", ALU.add, replica_groups=[list(range(NCORES))],
            ins=[p_bnc.opt()], outs=[p_tot.opt()]), wait=[pcc_ready], inc=s_cc)

        hbuf = h_loc
        SPc = 3 * DI + 2048
        trep = xnflat[:, SPc + 768:SPc + 832]
        tm_t = xnflat[:, SPc + 832:SPc + 896]
        cmask = mg_sb.ap()[0:64].rearrange("p a b -> p (a b)")[:, 32:40]
        cmaski = mg_sb.ap()[0:64].rearrange("p a b -> p (a b)")[:, 40:48]
        fbufs = [xnflat[:, 0:DI], xnflat[:, DI:2 * DI]]
        s_f = [sem("f0"), sem("f1")]
        cmw = [(s_mm, s_mm.n), (s_se, s_se.n), (s_ve, s_ve.n)]
        k.dma("sync", cmask, combmask, inc=s_c3, wait=cmw)
        k.dma("sync", cmaski, combmaskinv, inc=s_c3)
        k.op("vector", lambda h: h.memset(hbuf, 0.0), writes=(hbuf,),
             wait=[(s_mm, s_mm.n), (s_se, s_se.n), (s_sp, s_sp.n),
                   (s_ft, s_ft.n)])
        k.vdrain(s_ve)
        fjm = {}
        for j in range(NCORES - 1):
            fb = fbufs[j % 2]
            fw = [(s_cc, 2), (s_mm, s_mm.n), (s_se, s_se.n)]
            if j >= 2:
                fw.append((s_ve, fjm[j - 2]))
            k.dma("sync", fb, f_gat[j], inc=s_f[j % 2], wait=fw)
            k.dma("sync", trep,
                  t_gat[j].rearrange("h one -> one h").broadcast_to([64, 64]),
                  inc=s_f[j % 2],
                  wait=[(s_ve, fjm[j - 1])] if j >= 1 else None)
            k.v("tensor_scalar", tm_t, trep, cmask[:, j:j + 1],
                cmaski[:, j:j + 1], ALU.mult, op1=ALU.add,
                wait=[(s_f[j % 2], s_f[j % 2].n), (s_c3, s_c3.n)])
            k.vdrain(s_ve)
            k.v("tensor_tensor", hbuf.rearrange("n (h p) -> n h p", h=64),
                hbuf.rearrange("n (h p) -> n h p", h=64),
                tm_t[:, :, None].broadcast_to([64, 64, 64]), ALU.mult)
            k.v("scalar_tensor_tensor", hbuf, fb, cmask[:, j:j + 1], hbuf,
                ALU.mult, ALU.add)
            k.vdrain(s_ve)
            fjm[j] = s_ve.n
        hin_done = (s_ve, s_ve.n)

        vtk2 = xnflat128[:, 0:DI]
        mrgY = xnflat128[:, DI:2 * DI]
        mrgA = xnflat[:, DI:2 * DI]
        att2 = xnflat128[:, 2 * DI:3 * DI]
        zb2 = xnflat128[:, 3 * DI:4 * DI]
        wob = [xnflat128[:, 0:2048], xnflat128[:, 2048:4096]]
        lnf = lntmp.ap()
        tmpq = lnf[:, 0:512]
        masklv = lnf[:, 512:576]
        gm2 = lnf[:, 576:640]
        drepA = lnf[0:64, 640:704]
        drepB = lnf[0:64, 1088:1152]
        cstk2 = lnf[:, 704:768]
        ecs2 = lnf[:, 768:832]
        drpDv = lnf[:, 896:960]
        nsc = mg_sb.ap().rearrange("p a b -> p (a b)")[:, 0:32]
        rcol = mg_sb.ap().rearrange("p a b -> p (a b)")[:, 48:50]
        smw = [(s_mm, s_mm.n), (s_se, s_se.n), (s_ve, s_ve.n)]
        k.dma("sync", nsc, norm_scale, inc=s_c4, wait=smw)
        s_vl = sem("vl")
        s_ml = sem("ml")
        s_zl = sem("zl")
        s_cb = [sem("cb0"), sem("cb1")]
        s_cbr = sem("cbr")
        s_dr = sem("dr")
        s_ur = sem("ur")
        s_ot = sem("ot")
        marks = {"amm": {}, "yev": {}, "scan": {}, "tp": {}, "gy": {},
                 "zl": {}, "sq": {}, "sm": {}, "ml": {}}
        gy_fm = v_fm
        for half in range(2):
            mlw = [(s_ve, s_ve.n), (s_se, s_se.n)] if half == 0 else \
                  [(s_ot, s_ot.n), (s_ve, s_ve.n)]
            k.dma("sync", masklv, maskl_in, inc=s_c5, wait=mlw)
            k.dma("sync", drpDv, d_rep, inc=s_c5)
            mklm = (s_c5, s_c5.n)
            for p in range(4):
                ga = half * 8 + 2 * p
                gb = ga + 1
                sla = slice(ga * 64, ga * 64 + 64)
                slb = slice(gb * 64, gb * 64 + 64)
                slab = slice(ga * 64, ga * 64 + 128)
                pp = half * 4 + p
                if pp == 0:
                    vw = [hin_done, (s_mm, s_mm.n), (s_cb[0], s_cb[0].n)]
                elif p == 0:
                    vw = [(s_mm, s_mm.n), (s_ve, s_ve.n)]
                else:
                    vw = [(s_mm, marks["amm"][pp - 1]),
                          (s_ve, marks["yev"][pp - 1]),
                          (s_se, marks["sq"].get(pp - 1, 0))]
                k.dma("sync", vtk2,
                      vtok_sp.rearrange("c s d -> (c s) d")[ga * 64:ga * 64 + 128, :],
                      inc=s_vl, wait=vw)
                mw = [hin_done] if pp == 0 else \
                     [(s_ve, marks["scan"][pp - 1]), (s_mm, marks["tp"].get(pp - 1, 0)),
                      (s_se, marks["sq"].get(pp - 1, 0))]
                k.dma("sync", mrgA, m_sp[ga], inc=s_ml, wait=mw)
                zw = [(s_se, marks["zl"][pp - 1]),
                      (s_ve, marks["gy"][(pp - 1, 31)])] if pp > 0 else \
                     [(s_mm, s_mm.n), hin_done]
                k.dma("gpsimd", zb2.rearrange("p (m t) -> p m t", m=32),
                      z_sp[:, :, ga * 64:ga * 64 + 128].rearrange("m p t -> p m t"),
                      inc=s_zl, wait=zw)
                cw = [(s_cb[p % 2], s_cb[p % 2].n)]
                k.dma("gpsimd", csrow[p % 2, 0], cs_fm[:, sla],
                      inc=s_cb[p % 2], wait=cw)
                k.dma("gpsimd", csrow[p % 2, 1], cs_fm[:, slb],
                      inc=s_cb[p % 2])
                k.dma("gpsimd", totrow[p % 2, 0],
                      cs_fm[:, ga * 64 + 63:ga * 64 + 64], inc=s_cb[p % 2])
                k.dma("gpsimd", totrow[p % 2, 1],
                      cs_fm[:, gb * 64 + 63:gb * 64 + 64], inc=s_cb[p % 2])
                rw = [(s_cb[p % 2], s_cb[p % 2].n)]
                if pp > 0:
                    rw.append((s_mm, marks["amm"][pp - 1]))
                    rw.append((s_ve, marks["yev"][pp - 1]))
                k.dma("gpsimd", att2[0:64, :],
                      csrow[p % 2, 0:1].rearrange('a b -> (a b)')[None, :].broadcast_to([64, NH * S]) if False else csrow[p % 2, 0:1, :].rearrange('a f -> a f').broadcast_to([64, NH * S]),
                      inc=s_cbr, wait=rw)
                k.dma("gpsimd", att2[64:128, :],
                      csrow[p % 2, 1:2, :].broadcast_to([64, NH * S]), inc=s_cbr)
                csb_ld = (s_cbr, s_cbr.n)
                sw = [(s_ve, s_ve.n)]
                daw = [(s_cb[p % 2], s_cb[p % 2].n),
                       (s_ve, marks["scan"].get(pp - 1, 0))]
                if p == 0:
                    daw += [(s_ot, s_ot.n), (s_ve, s_ve.n), (s_ur, s_ur.n)]
                k.dma("gpsimd", drepA,
                      totrow[p % 2, 0:1, :].broadcast_to([64, NH]),
                      inc=s_dr, wait=daw)
                k.dma("gpsimd", drepB,
                      totrow[p % 2, 1:2, :].broadcast_to([64, NH]), inc=s_dr)
                tpw = sw + [(s_se, marks["sm"].get(pp - 1, 0))]
                k.trans(ptr[1][:, 0:64], cs_fm[:, slab], ident[0:64, 0:64],
                        wait=tpw, inc=s_mm)
                k.mm(pb[0][0:64, 0:64], B_fm[:, sla], C_fm[:, sla],
                     start=True, stop=True)
                k.mm(pb[2][64:128, 0:64], B_fm[:, slb], C_fm[:, slb],
                     start=True, stop=True, inc=s_mm)
                k.act(drepA, drepA, AF.Exp,
                      wait=[s_mm, (s_dr, s_dr.n), (s_ve, s_ve.n)])
                k.act(drepB, drepB, AF.Exp)
                k.act(cstk2[:], ptr[1][:, 0:64], AF.Copy)
                k.act(ecs2[:], ptr[1][:, 0:64], AF.Exp)
                k.act(gm2[0:64, :], pb[0][0:64, 0:64], AF.Copy)
                k.act(gm2[64:128, :], pb[2][64:128, 0:64], AF.Copy, inc=s_se)
                marks["sm"][pp] = s_se.n
                a3 = att2.rearrange("t (h s) -> t h s", h=NH)
                k.v("tensor_tensor", a3, a3,
                    masklv[:, None, :].broadcast_to([128, NH, S]), ALU.add,
                    wait=[csb_ld, mklm])
                k.v("tensor_tensor", a3, a3,
                    cstk2[:, :, None].broadcast_to([128, NH, S]), ALU.subtract,
                    wait=[(s_se, marks["sm"][pp])])
                k.vdrain(s_ve)
                k.act(att2, att2, AF.Exp, wait=[s_ve], inc=s_se)
                k.v("tensor_tensor", a3, a3,
                    gm2[:, None, :].broadcast_to([128, NH, S]), ALU.mult,
                    wait=[s_se])
                k.vdrain(s_ve)
                att_done = (s_ve, s_ve.n)
                if "att" in debug and pp == 0:
                    k.dma("sync", att_sp, att2, inc=s_sp, wait=[att_done])
                    k.dma("sync", vt2_sp, vtk2, inc=s_sp,
                          wait=[(s_vl, s_vl.n)])
                    k.dma("sync", hin_sp, hbuf, inc=s_sp, wait=[hin_done])
                    k.dma("sync", sml_sp, lnf[:, 0:1152], inc=s_sp,
                          wait=[(s_se, s_se.n), (s_ve, s_ve.n)])
                for q in range(4):
                    qs = q * 1024
                    for bh in range(2):
                        qq = qs + bh * 512
                        w0 = None
                        if q == 0 and bh == 0:
                            w0 = [att_done, (s_vl, s_vl.n), (s_ml, s_ml.n),
                                  (s_se, marks["sm"][pp])]
                            if pp > 0:
                                w0 = w0 + [(s_ve, marks["yev"][pp - 1])]
                        elif bh == 0 and q > 0:
                            w0 = [(s_ve, marks["yev"][(pp, q - 1)])]
                        k.mm(pb[bh][0:64, 0:512], C_fm[:, sla],
                             hbuf[:, qq:qq + 512], start=True, stop=True,
                             wait=w0)
                        for hh in range(8):
                            hid = q * 16 + bh * 8 + hh
                            k.mm(pb[4 + bh][0:64, hh * 64:hh * 64 + 64],
                                 att2[0:64, hid * 64:hid * 64 + 64],
                                 vtk2[0:64, hid * 64:hid * 64 + 64],
                                 start=(hh == 0), stop=(hh == 7))
                            k.mm(ptr[bh][64:128, hh * 64:hh * 64 + 64],
                                 att2[64:128, hid * 64:hid * 64 + 64],
                                 vtk2[64:128, hid * 64:hid * 64 + 64],
                                 start=(hh == 0), stop=(hh == 7),
                                 inc=s_mm if hh == 7 else None)
                    interE = s_mm.n
                    k.v("tensor_tensor",
                        hbuf[:, qs:qs + 1024].rearrange("n (h p) -> n h p", h=16),
                        hbuf[:, qs:qs + 1024].rearrange("n (h p) -> n h p", h=16),
                        drepA[:, q * 16:q * 16 + 16][:, :, None]
                        .broadcast_to([64, 16, 64]), ALU.mult,
                        wait=[(s_mm, interE), (s_ml, s_ml.n),
                              (s_se, marks["sm"][pp])])
                    k.v("tensor_add", hbuf[:, qs:qs + 1024],
                        hbuf[:, qs:qs + 1024], mrgA[:, qs:qs + 1024])
                    k.vdrain(s_ve)
                    scanE = s_ve.n
                    k.dma("sync", mrgA[:, qs:qs + 1024],
                          m_sp[gb, :, qs:qs + 1024], inc=s_ml,
                          wait=[(s_ve, scanE)])
                    for bh in range(2):
                        qq = qs + bh * 512
                        k.mm(pb[2 + bh][64:128, 0:512], C_fm[:, slb],
                             hbuf[:, qq:qq + 512], start=True, stop=True,
                             wait=[(s_ve, scanE)] if bh == 0 else None,
                             inc=s_mm if bh == 1 else None)
                    k.v("tensor_tensor",
                        hbuf[:, qs:qs + 1024].rearrange("n (h p) -> n h p", h=16),
                        hbuf[:, qs:qs + 1024].rearrange("n (h p) -> n h p", h=16),
                        drepB[:, q * 16:q * 16 + 16][:, :, None]
                        .broadcast_to([64, 16, 64]), ALU.mult,
                        wait=[(s_mm, s_mm.n), (s_ml, s_ml.n)])
                    k.v("tensor_add", hbuf[:, qs:qs + 1024],
                        hbuf[:, qs:qs + 1024], mrgA[:, qs:qs + 1024])
                    k.vdrain(s_ve)
                    marks["scan"][pp] = s_ve.n
                    for bh in range(2):
                        qq = qs + bh * 512
                        hsl = slice(q * 16 + bh * 8, q * 16 + bh * 8 + 8)
                        for half_rows, pi, pa in ((slice(0, 64), pb[bh], pb[4 + bh]),
                                                  (slice(64, 128), pb[2 + bh], ptr[bh])):
                            dst = mrgY[half_rows, qq:qq + 512]
                            tq = tmpq[half_rows, :]
                            k.v("tensor_tensor",
                                dst.rearrange("t (h p) -> t h p", h=8),
                                pi[half_rows, 0:512].rearrange("t (h p) -> t h p", h=8),
                                ecs2[half_rows, hsl][:, :, None]
                                .broadcast_to([64, 8, 64]), ALU.mult,
                                wait=[(s_mm, s_mm.n), (s_ve, marks["scan"][pp]),
                                      (s_c5, s_c5.n)])
                            k.v("tensor_add", dst, dst, pa[half_rows, 0:512])
                            k.vdrain(s_ve)
                    marks["yev"][(pp, q)] = s_ve.n
                k.v("tensor_tensor",
                    att2.rearrange("t (h p) -> t h p", h=64),
                    vtk2.rearrange("t (h p) -> t h p", h=64),
                    drpDv[:, :, None].broadcast_to([128, 64, 64]), ALU.mult,
                    wait=[(s_mm, s_mm.n), (s_ve, marks["yev"][(pp, 3)]),
                          (s_c5, s_c5.n)])
                k.v("tensor_add", mrgY, mrgY, att2)
                k.vdrain(s_ve)
                marks["amm"][pp] = s_mm.n
                marks["yev"][pp] = s_ve.n
                marks["ml"][pp] = s_ml.n
                if "y" in debug:
                    k.dma("sync", y_sp[ga], mrgY[0:64, :], inc=s_sp,
                          wait=[(s_ve, s_ve.n)])
                    k.dma("sync", y_sp[gb], mrgY[64:128, :], inc=s_sp)
                k.act(vtk2, mrgY, AF.Square, accum_out=rcol[:, 0:1],
                      wait=[(s_ve, marks["yev"][pp]), (s_se, s_se.n),
                            (s_cbr, s_cbr.n)], inc=s_se)
                marks["sq"][pp] = s_se.n
                k.v("tensor_scalar", rcol[:, 0:1], rcol[:, 0:1], 1.0 / DI,
                    1e-6, ALU.mult, op1=ALU.add, wait=[s_se])
                k.vdrain(s_ve)
                k.act(rcol[:, 0:1], rcol[:, 0:1], AF.Sqrt, wait=[s_ve],
                      inc=s_se)
                k.v("reciprocal", rcol[:, 0:1], rcol[:, 0:1], wait=[s_se])
                k.vdrain(s_ve)
                k.v("tensor_scalar", mrgY, mrgY, rcol[:, 0:1], None, ALU.mult)
                k.vdrain(s_ve)
                yrms = (s_ve, s_ve.n)
                k.act(zb2, zb2, AF.Silu, wait=[(s_zl, s_zl.n)], inc=s_se)
                marks["zl"][pp] = s_se.n
                for m in range(32):
                    pslot = pb[m % 2]
                    tw = [yrms, (s_se, marks["zl"][pp])]
                    if (pp, m - 2) in marks["gy"]:
                        tw.append((s_ve, marks["gy"][(pp, m - 2)]))
                    k.trans(pslot[:, 0:128], mrgY[:, m * 128:(m + 1) * 128],
                            ident[:], wait=tw, inc=s_mm)
                    k.v("scalar_tensor_tensor",
                        gy_fm[:, m, 2 * p * 64:2 * p * 64 + 128],
                        zb2.rearrange("x (m t) -> x m t", m=32)[:, m],
                        nsc[:, m:m + 1], pslot[:, 0:128], ALU.mult,
                        ALU.mult, wait=[s_mm, (s_c4, s_c4.n)])
                    k.vdrain(s_ve)
                    marks["gy"][(pp, m)] = s_ve.n
                marks["tp"][pp] = s_mm.n
            gy_done = (s_ve, s_ve.n)
            p6km = []
            p6_evac = gy_done
            for tpair in range(2):
                for kd in range(32):
                    slot = kd % 2
                    if len(p6km) >= 2:
                        bw = [(s_mm, p6km[-2])]
                    elif half == 0 and tpair == 0:
                        bw = [gy_done, (s_vl, s_vl.n)]
                    else:
                        bw = [gy_done]
                    k.dma("sync", wob[slot], w_out[kd * 128:(kd + 1) * 128, :],
                          inc=s_wb[slot], wait=bw)
                    for tp in range(2):
                        tt = tpair * 2 + tp
                        for q in range(4):
                            pk = (pb + ptr)[tp * 4 + q]
                            mmw = None
                            if tp == 0 and q == 0:
                                mmw = [(s_wb[slot], s_wb[slot].n)]
                                if kd == 0:
                                    mmw.append(p6_evac)
                            k.mm(pk[:, 0:512],
                                 gy_fm[:, kd, tt * 128:(tt + 1) * 128],
                                 wob[slot][:, q * 512:(q + 1) * 512],
                                 start=(kd == 0), stop=(kd == 31),
                                 wait=mmw,
                                 inc=s_mm if (tp == 1 and q == 3) else None)
                    p6km.append(s_mm.n)
                for tp in range(2):
                    tt = tpair * 2 + tp
                    t0 = half * 512 + tt * 128
                    uw = [(s_ot, s_ot.n)]
                    if tpair == 0 and tp == 0:
                        uw += [(s_ve, s_ve.n), (s_se, s_se.n),
                               (s_dr, s_dr.n), (s_cbr, s_cbr.n),
                               (s_c5, s_c5.n)]
                    k.dma("sync", lntmp[:], u[3 + t0:3 + t0 + 128, :],
                          inc=s_ur, wait=uw)
                    for q in range(4):
                        pk = (pb + ptr)[tp * 4 + q]
                        k.v("tensor_add", lntmp[:, q * 512:(q + 1) * 512],
                            lntmp[:, q * 512:(q + 1) * 512], pk[:, 0:512],
                            wait=[(s_mm, p6km[-1]), (s_ur, s_ur.n)]
                            if q == 0 else None)
                    k.vdrain(s_ve)
                    k.dma("sync", out[t0:t0 + 128, :], lntmp[:], inc=s_ot,
                          wait=[(s_ve, s_ve.n)])
                p6_evac = (s_ve, s_ve.n)
        t8 = lntmp.ap()[0:8, :]
        ptot_sb = ustage.ap()[0:8].rearrange("p a b -> p (a b)")[:, 0:D]
        c0v = mg_sb.ap()[0:8].rearrange("p a b -> p (a b)")[:, 50:51]
        k.dma("sync", t8, out[0:8, :], inc=s_ur,
              wait=[(s_ot, s_ot.n), (s_cc, 3)])
        k.dma("sync", ptot_sb, p_tot, inc=s_ur,
              wait=[(s_ve, s_ve.n), (s_mm, s_mm.n)])
        k.dma("sync", c0v, core0[0:8, 0:1], inc=s_ur)
        k.v("scalar_tensor_tensor", t8, ptot_sb, c0v, t8,
            ALU.mult, ALU.add, wait=[(s_ur, s_ur.n)])
        k.vdrain(s_ve)
        k.dma("sync", out[0:8, :], t8, inc=s_ot, wait=[(s_ve, s_ve.n)])
        k.op("sync", lambda h: h.nop(),
             wait=[s_ot, s_sp, s_vt, s_m, s_xnr, s_cc, s_vl, s_ml, s_zl, s_ft, s_pb, s_c2, s_c3, s_c4, s_c5, s_dr,
                   s_cbr, s_ur, s_gld] + s_zs + s_u + s_wb + s_f + s_cb +
                  [s_const])

        k.emit()
    return nc


def _prep_inputs(L0_attn_u, L0_attn_in_proj_w, L0_attn_conv_weight,
                 L0_attn_conv_bias, L0_attn_A_log, L0_attn_dt_bias,
                 L0_attn_D_param, L0_attn_norm_scale, L0_attn_out_proj_w,
                 L0_ffn_w1, L0_ffn_w3, L0_ffn_w2, L0_ffn_wr):
    u = np.ascontiguousarray(np.asarray(L0_attn_u, np.float32).reshape(8192, D))
    upad = np.concatenate([np.zeros((HALO, D), np.float32), u], 0)
    csmask = np.ones((NH, TOK), np.float32)
    csmask[:, ::S] = 0.0
    shared = {
        "w_in": np.ascontiguousarray(L0_attn_in_proj_w, np.float32),
        "conv_w": np.ascontiguousarray(np.transpose(
            np.asarray(L0_attn_conv_weight, np.float32).reshape(4, 32, 128),
            (2, 1, 0))),
        "conv_b": np.ascontiguousarray(
            np.asarray(L0_attn_conv_bias, np.float32).reshape(32, 128).T),
        "a_log": np.asarray(L0_attn_A_log, np.float32).reshape(NH, 1),
        "dt_bias": np.asarray(L0_attn_dt_bias, np.float32).reshape(NH, 1),
        "d_rep": np.tile(np.asarray(L0_attn_D_param, np.float32).reshape(1, NH),
                         (128, 1)),
        "norm_scale": np.ascontiguousarray(
            np.asarray(L0_attn_norm_scale, np.float32).reshape(32, 128).T),
        "w_out": np.ascontiguousarray(L0_attn_out_proj_w, np.float32),
        "w1t": np.ascontiguousarray(np.asarray(L0_ffn_w1, np.float32).T),
        "w3t": np.ascontiguousarray(np.asarray(L0_ffn_w3, np.float32).T),
        "w2t": np.ascontiguousarray(np.asarray(L0_ffn_w2, np.float32).T),
        "wrt": np.ascontiguousarray(np.asarray(L0_ffn_wr, np.float32).T
                                    .reshape(16, 128, E).transpose(1, 0, 2)
                                    .reshape(128, 16 * E)),
        "ident": np.eye(128, dtype=np.float32),
        "csmask": csmask,
        "maskl": np.tile(np.where(
            np.arange(S)[None, :] >= np.arange(S)[:, None],
            0.0, -1e30).astype(np.float32), (2, 1)),
    }
    in_maps = []
    for c in range(NCORES):
        m = dict(shared)
        m["u"] = np.ascontiguousarray(upad[c * TOK:c * TOK + TOKH, :])
        cmb = np.zeros((64, NCORES), np.float32)
        cmb[:, :c] = 1.0
        m["combmask"] = cmb
        m["combmaskinv"] = (1.0 - cmb)
        m["core0"] = np.full((128, 1), 1.0 if c == 0 else 0.0, np.float32)
        in_maps.append(m)
    return in_maps


def kernel(**inputs):
    in_maps = _prep_inputs(**inputs)
    nc = build_program()
    res = run_bass_kernel_spmd(nc, in_maps, core_ids=list(range(NCORES)))
    return np.concatenate([r["out"] for r in res.results], 0).reshape(1, 8192, D)


# version 11
# speedup vs baseline: 1.2223x; 1.0147x over previous
import contextlib
import numpy as np
import concourse.bass as bass
import concourse.mybir as mybir
from concourse.bass_utils import run_bass_kernel_spmd

F32 = mybir.dt.float32
AF = mybir.ActivationFunctionType
ALU = mybir.AluOpType
AX = mybir.AxisListType

NCORES = 8
TOK = 1024
HALO = 3
TOKH = TOK + HALO
D = 2048
DI = 4096
NH = 64
NST = 64
S = 64
NCH = TOK // S
DIP = 8384
E = 8

class Sem:
    def __init__(self, handle):
        self.h = handle
        self.n = 0


class K:
    ENGS = ("tensor", "vector", "scalar", "gpsimd", "sync")

    def __init__(self, nc):
        self.nc = nc
        self.q = {e: [] for e in self.ENGS}
        self.dirty = {e: set() for e in self.ENGS}

    @staticmethod
    def _tnames(aps):
        out = set()
        for a in aps:
            if a is None or isinstance(a, (int, float)):
                continue
            t = getattr(a, "tensor", None)
            if t is not None:
                out.add(t.name)
        return out

    def op(self, eng, fn, reads=(), writes=(), inc=None, inc_k=1, wait=None,
           dma_inc=None):
        import traceback
        tag = None
        for fr in reversed(traceback.extract_stack()[:-1]):
            if fr.filename.endswith("kernel.py") and fr.name not in (
                    "op", "dma", "mm", "trans", "act", "v", "vdrain"):
                tag = f"{fr.name}:{fr.lineno}"
                break
        rn = self._tnames(reads)
        wn = self._tnames(writes)
        need_drain = bool(self.dirty[eng] & (rn | wn))
        waits = []
        if wait:
            for w in wait:
                waits.append((w.h, w.n) if isinstance(w, Sem) else (w[0].h, w[1]))
        if inc is not None:
            inc.n += inc_k
        if dma_inc is not None:
            dma_inc.n += 16
        if need_drain:
            self.dirty[eng] = set()

        def emit(h, nd=need_drain, waits=waits, fn=fn, inc=inc, inc_k=inc_k,
                 dma_inc=dma_inc, tag=tag):
            if nd:
                h.drain()
            for sh, sv in waits:
                h.wait_ge(sh, sv)
            try:
                ins = fn(h)
            except Exception as e:
                raise RuntimeError(f"emit failed at record site {tag}: {e}") from e
            if ins is not None:
                if inc is not None:
                    ins.then_inc(inc.h, inc_k)
                if dma_inc is not None:
                    ins.then_inc(dma_inc.h, 16)

        self.q[eng].append(emit)
        self.dirty[eng] |= wn

    def dma(self, eng, out, in_, inc=None, wait=None):
        self.op(eng, lambda h: h.dma_start(out=out, in_=in_),
                reads=(in_,), writes=(out,), dma_inc=inc, wait=wait)

    def mm(self, out, lhsT, rhs, start, stop, inc=None, wait=None):
        self.op("tensor",
                lambda h: h.matmul(out, lhsT, rhs, start=start, stop=stop),
                reads=(lhsT, rhs), writes=(out,), inc=inc, wait=wait)

    def trans(self, out, in_, ident, inc=None, wait=None):
        self.op("tensor", lambda h: h.transpose(out, in_, ident),
                reads=(in_, ident), writes=(out,), inc=inc, wait=wait)

    def act(self, out, in_, func, bias=0.0, scale=1.0, accum_out=None, inc=None,
            wait=None):
        self.op("scalar",
                lambda h: h.activation(out, in_, func, bias=bias, scale=scale,
                                       accum_out=accum_out),
                reads=(in_, None if isinstance(bias, float) else bias,
                       None if isinstance(scale, float) else scale),
                writes=(out, accum_out), inc=inc, wait=wait)

    def v(self, method, out, *ins, inc=None, wait=None, **kw):
        def fn(h, method=method, out=out, ins=ins, kw=kw):
            return getattr(h, method)(out, *ins, **kw)
        self.op("vector", fn, reads=ins, writes=(out,), inc=inc, wait=wait)

    def vdrain(self, inc):
        self.op("vector", lambda h: h.drain(), inc=inc)
        self.dirty["vector"] = set()

    def emit(self):
        with self.nc.Block() as block:
            @block.tensor
            def _(h):
                for f in self.q["tensor"]:
                    f(h)

            @block.vector
            def _(h):
                for f in self.q["vector"]:
                    f(h)

            @block.scalar
            def _(h):
                for f in self.q["scalar"]:
                    f(h)

            @block.gpsimd
            def _(h):
                for f in self.q["gpsimd"]:
                    f(h)

            @block.sync
            def _(h):
                for f in self.q["sync"]:
                    f(h)


def build_program(debug=(), stage=1):
    nc = bass.Bass("TRN2", target_bir_lowering=False, debug=False)

    def dram_in(name, shape):
        return nc.dram_tensor(name, shape, F32, kind="ExternalInput").ap()

    u = dram_in("u", [TOKH, D])
    w_in = dram_in("w_in", [D, DIP])
    conv_w = dram_in("conv_w", [128, 32, 4])
    conv_b = dram_in("conv_b", [128, 32])
    a_log = dram_in("a_log", [NH, 1])
    dt_bias = dram_in("dt_bias", [NH, 1])
    d_rep = dram_in("d_rep", [128, NH])
    norm_scale = dram_in("norm_scale", [128, 32])
    w_out = dram_in("w_out", [DI, D])
    w1t = dram_in("w1t", [D, D])
    w3t = dram_in("w3t", [D, D])
    w2t = dram_in("w2t", [D, D])
    wrt = dram_in("wrt", [128, 16 * E])
    ident_in = dram_in("ident", [128, 128])
    maskl_in = dram_in("maskl", [128, S])
    combmaskinv = dram_in("combmaskinv", [64, NCORES])
    csmask_in = dram_in("csmask", [NH, TOK])
    combmask = dram_in("combmask", [64, NCORES])
    core0 = dram_in("core0", [128, 1])

    out = nc.dram_tensor("out", [TOK, D], F32, kind="ExternalOutput").ap()

    def scratch(name, shape, dbg=False):
        kind = "ExternalOutput" if dbg else "Internal"
        if dbg:
            return nc.dram_tensor(name, shape, F32, kind="ExternalOutput").ap()
        return nc.dram_tensor(name, shape, F32).ap()

    xn_sp = scratch("xn_sp", [16, 128, TOKH], "xn" in debug)
    z_sp = scratch("z_sp", [32, 128, TOK], "z" in debug)
    bcdt_sp = scratch("bcdt_sp", [4, 64, TOK], "bcdt" in debug)
    vfm_sp = scratch("vfm_sp", [2, 32, 128, 512], "v" in debug)
    vtok_sp = scratch("vtok_sp", [NCH, 64, DI], "vtok" in debug)
    m_sp = scratch("m_sp", [NCH, 64, DI], "m" in debug)
    ft_sp = scratch("ft_sp", [65, DI], "ft" in debug)
    p_sp = scratch("p_sp", [8, D], "p" in debug)
    mg_sp = scratch("mg_sp", [128, 64], "p" in debug)
    h2_sp = scratch("h2_sp", [16, 128, TOK], "h2" in debug)
    f_bnc = nc.dram_tensor("f_bnc", [64, DI], F32).ap()
    t_bnc = nc.dram_tensor("t_bnc", [64, 1], F32).ap()
    f_gat = nc.dram_tensor("f_gat", [NCORES, 64, DI], F32).ap()
    t_gat = nc.dram_tensor("t_gat", [NCORES, 64, 1], F32).ap()
    csrow = nc.dram_tensor("csrow", [2, 2, NH * S], F32).ap()
    totrow = nc.dram_tensor("totrow", [2, 2, NH], F32).ap()
    rrow = nc.dram_tensor("rrow", [2, 1, S], F32).ap()
    p_bnc = nc.dram_tensor("p_bnc", [8, D], F32).ap()
    p_tot = nc.dram_tensor("p_tot", [8, D], F32).ap()
    y_sp = scratch("y_sp", [NCH, 64, DI], "y" in debug)
    att_sp = scratch("att_sp", [128, DI], "att" in debug)
    vt2_sp = scratch("vt2_sp", [128, DI], "att" in debug)
    hin_sp = scratch("hin_sp", [64, DI], "att" in debug)
    sml_sp = scratch("sml_sp", [128, 1152], "att" in debug)
    hq_sp = scratch("hq_sp", [64, DI], "att" in debug)

    k = K(nc)
    sems = {}

    def sem(name):
        s = Sem(stackctx.enter_context(nc.semaphore(name)))
        sems[name] = s
        return s

    stackctx = contextlib.ExitStack()
    with stackctx:
        def sbuf(name, shape, dtype=F32):
            return stackctx.enter_context(nc.sbuf_tensor(name, shape, dtype))

        def psum(name):
            return stackctx.enter_context(nc.psum_tensor(name, [128, 512], F32))

        s_const = sem("const")
        s_u = [sem("u0"), sem("u1")]
        s_wb = [sem("wb0"), sem("wb1")]
        s_zs = [sem(f"zs{i}") for i in range(4)]
        s_mm = sem("mm")
        s_ve = sem("ve")
        s_se = sem("se")
        s_sp = sem("sp")
        s_gld = sem("gld")
        s_cc = sem("cc")
        s_vt = sem("vt")
        s_ft = sem("ft")
        s_pb = sem("pbs")
        s_c2 = sem("c2")
        s_c3 = sem("c3")
        s_c4 = sem("c4")
        s_c5 = sem("c5")
        s_m = sem("m")
        s_xnr = sem("xnr")

        ident = sbuf("identt", [128, 128])
        xn_fm = sbuf("xn_fm", [128, 16, TOKH])
        B_fm = sbuf("B_fm", [64, TOK])
        C_fm = sbuf("C_fm", [64, TOK])
        dt_fm = sbuf("dt_fm", [64, TOK])
        la_fm = sbuf("la_fm", [64, TOK])
        cs_fm = sbuf("cs_fm", [64, TOK])
        csmask = sbuf("csmask_t", [64, TOK])
        acol = sbuf("acol", [64, 1])
        dtb = sbuf("dtb", [64, 1])
        convw = sbuf("convw_t", [128, 32, 4])
        convb = sbuf("convb_t", [128, 32])
        v_fm = sbuf("v_fm", [128, 32, 512])
        wband = sbuf("wband", [128, 2, 512])
        ustage = sbuf("ustage", [128, 2, D])
        lnm = sbuf("lnm", [128, 2])
        consts1 = sbuf("consts1", [128, 1])
        lntmp = sbuf("lntmp", [128, D])
        xtmp = sbuf("xtmp", [128, 2, 515])
        zstage = sbuf("zstage", [128, 4, 512])
        h_loc = ustage.ap()[0:64].rearrange("p a b -> p (a b)")[:, 0:DI]
        xnflat = xn_fm.ap()[0:64].rearrange("p a b -> p (a b)")
        xnflat128 = xn_fm.ap().rearrange("p a b -> p (a b)")
        vt_sb = xnflat[:, 0:DI]
        m_sb = xnflat[:, DI:2 * DI]
        b_tok = sbuf("b_tok", [64, 64])
        e2s = sbuf("e2s", [64, NCH, 64])
        tmp2_fm = sbuf("tmp2_fm", [64, 64])
        cs_tok = sbuf("cs_tok", [64, NCH, 64])
        ecs_tok = sbuf("ecs_tok", [64, NCH, 64])
        drep = sbuf("drep", [64, 64])
        tcol = sbuf("tcol", [64, 1])

        pb = [psum(f"pb{i}") for i in range(6)]
        ptr = [psum("pt0"), psum("pt1")]

        k.dma("sync", ident[:], ident_in, inc=s_const)
        k.dma("sync", csmask[:], csmask_in, inc=s_const)
        k.dma("sync", acol[:], a_log, inc=s_const)
        k.dma("sync", dtb[:], dt_bias, inc=s_const)
        k.dma("sync", convw[:], conv_w, inc=s_const)
        k.dma("sync", convb[:], conv_b, inc=s_const)
        k.op("vector", lambda h: h.memset(consts1[:], 1.0),
             writes=(consts1[:],))
        k.vdrain(s_ve)
        k.act(acol[:], acol[:], AF.Exp, wait=[s_const], inc=s_se)

        umarks = []
        semarks = []
        pmarks = {}
        tposed = 0
        for t in range(9):
            rows = 128 if t < 8 else 3
            r0 = t * 128
            ub = ustage[:, t % 2]
            w = None
            if t >= 2:
                w = [(s_mm, umarks[t - 2]), (s_se, semarks[t - 2])]
            k.dma("sync", ub[0:rows, :], u[r0:r0 + rows, :], inc=s_u[t % 2],
                  wait=w)
            k.v("reduce_sum", lnm[0:rows, 0:1], ub[0:rows, :], axis=AX.X,
                wait=[s_u[t % 2]])
            k.v("tensor_scalar_mul", lnm[0:rows, 0:1], lnm[0:rows, 0:1], 1.0 / D)
            k.v("tensor_single_scalar", ub[0:rows, :], ub[0:rows, :],
                lnm[0:rows, 0:1], ALU.subtract)
            k.vdrain(s_ve)
            k.act(lntmp[0:rows, :], ub[0:rows, :], AF.Square,
                  accum_out=lnm[0:rows, 1:2], wait=[s_ve], inc=s_se)
            k.v("tensor_scalar", lnm[0:rows, 1:2], lnm[0:rows, 1:2], 1.0 / D,
                1e-5, ALU.mult, op1=ALU.add, wait=[s_se])
            k.vdrain(s_ve)
            k.act(lnm[0:rows, 1:2], lnm[0:rows, 1:2], AF.Sqrt, wait=[s_ve],
                  inc=s_se)
            k.v("reciprocal", lnm[0:rows, 1:2], lnm[0:rows, 1:2], wait=[s_se])
            k.v("tensor_single_scalar", ub[0:rows, :], ub[0:rows, :],
                lnm[0:rows, 1:2], ALU.mult)
            k.vdrain(s_ve)
            for kd in range(16):
                pslot = ptr[tposed % 2]
                tw = [s_ve] if kd == 0 else []
                if tposed >= 2:
                    tw = tw + [(s_se, pmarks[tposed - 2])]
                k.trans(pslot[:, 0:rows], ub[0:rows, kd * 128:(kd + 1) * 128],
                        ident[0:rows, 0:rows], wait=tw or None, inc=s_mm)
                k.act(xn_fm[:, kd, r0:r0 + rows], pslot[:, 0:rows], AF.Copy,
                      wait=[s_mm], inc=s_se)
                pmarks[tposed] = s_se.n
                tposed += 1
            umarks.append(s_mm.n)
            semarks.append(s_se.n)
        xn_done = (s_se, s_se.n)
        for kd in range(16):
            k.dma("sync", xn_sp[kd], xn_fm[:, kd, :], inc=s_sp, wait=[xn_done])
        xnsp_mark = (s_sp, s_sp.n)

        zmov = [(3, 0), (515, 3)]
        zslot_marks = []
        bank_free = [(s_se, s_se.n), (s_ve, s_ve.n)]
        wmarks = []

        def proj_pass(tiles, evac, xn_cols=zmov, ncols=512, bankset=None):
            nonlocal bank_free, wmarks
            nt = len(tiles)
            banks = bankset if bankset is not None else pb
            for kd in range(16):
                c0 = tiles[0][0]
                c1 = tiles[-1][0] + tiles[-1][1]
                slot = len(wmarks) % 2
                wb = wband[:, slot, 0:c1 - c0]
                lw = None
                if len(wmarks) >= 2:
                    lw = [(s_mm, wmarks[-2])]
                k.dma("sync", wb, w_in[kd * 128:(kd + 1) * 128, c0:c1],
                      inc=s_wb[slot], wait=lw)
                for mi, (wc0, wm) in enumerate(tiles):
                    lhsT = wb[:, wc0 - c0:wc0 - c0 + wm]
                    for (x0, pbase) in xn_cols:
                        mmw = []
                        if mi == 0 and (x0, pbase) == xn_cols[0]:
                            mmw.append((s_wb[slot], s_wb[slot].n))
                            if kd == 0:
                                mmw += bank_free
                        k.mm(banks[pbase + mi][0:wm, 0:ncols], lhsT,
                             xn_fm[:, kd, x0:x0 + ncols],
                             start=(kd == 0), stop=(kd == 15),
                             wait=mmw or None,
                             inc=s_mm if (mi == nt - 1 and (x0, pbase) == xn_cols[-1]) else None)
                wmarks.append(s_mm.n)
            evac()

        zsets = [[pb[0], pb[1], pb[2], pb[3]], [pb[4], pb[5], ptr[0], ptr[1]]]
        bfree2 = {0: bank_free, 1: bank_free}
        zmov2 = [(3, 0), (515, 2)]
        for gi, g0 in enumerate(range(0, 32, 2)):
            gt = [(m * 128, 128) for m in range(g0, min(g0 + 2, 32))]
            bs = zsets[gi % 2]

            def z_evac(g0=g0, gt=gt, bs=bs, gi=gi):
                nonlocal bank_free
                for mi in range(len(gt)):
                    m = g0 + mi
                    for hi in range(2):
                        slot = len(zslot_marks) % 4
                        zw = [(s_mm, s_mm.n)]
                        if len(zslot_marks) >= 4:
                            zw.append((s_zs[slot], zslot_marks[-4]))
                        k.act(zstage[:, slot], bs[hi * 2 + mi][:, 0:512],
                              AF.Copy, wait=zw, inc=s_se)
                        k.dma("sync", z_sp[m, :, hi * 512:hi * 512 + 512],
                              zstage[:, slot], inc=s_zs[slot], wait=[s_se])
                        zslot_marks.append(s_zs[slot].n)
                bfree2[gi % 2] = [(s_se, s_se.n)]
                bank_free = [(s_se, s_se.n)]

            bank_free = bfree2[gi % 2]
            proj_pass(gt, z_evac, xn_cols=zmov2, bankset=bs)
        bank_free = [(s_se, s_se.n)]

        def bcdt_evac():
            nonlocal bank_free
            for mi, (nm, dest) in enumerate((("B", B_fm), ("C", C_fm),
                                             ("dt", dt_fm))):
                for hi in range(2):
                    src = pb[hi * 3 + mi][0:64, 0:512]
                    dc = hi * 512
                    if nm in ("B", "C"):
                        k.v("tensor_relu", dest[:, dc:dc + 512], src,
                            wait=[(s_mm, s_mm.n)])
                        k.v("tensor_sub", lntmp[0:64, 0:512], src,
                            dest[:, dc:dc + 512])
                        k.vdrain(s_ve)
                        k.act(lntmp[0:64, 512:1024], lntmp[0:64, 0:512], AF.Exp,
                              wait=[s_ve], inc=s_se)
                        k.v("tensor_add", dest[:, dc:dc + 512],
                            dest[:, dc:dc + 512], lntmp[0:64, 512:1024],
                            wait=[s_se])
                        k.vdrain(s_ve)
                    else:
                        t0 = lntmp[0:64, 0:512]
                        t1 = lntmp[0:64, 512:1024]
                        k.v("tensor_scalar_add", t0, src, dtb[:, 0:1],
                            wait=[(s_mm, s_mm.n)])
                        k.v("tensor_scalar_mul", t1, t0, -1.0)
                        k.v("tensor_tensor", t1, t1, t0, ALU.min)
                        k.vdrain(s_ve)
                        k.act(t1, t1, AF.Exp, wait=[s_ve])
                        k.op("scalar", lambda h: h.drain())
                        k.act(t1, t1, AF.Ln, bias=1.0 if False else consts1[0:64, 0:1],
                              inc=s_se)
                        k.v("tensor_relu", dest[:, dc:dc + 512], t0, wait=[s_se])
                        k.v("tensor_add", dest[:, dc:dc + 512],
                            dest[:, dc:dc + 512], t1)
                        k.vdrain(s_ve)
            bank_free = [(s_se, s_se.n), (s_ve, s_ve.n)]
        proj_pass([(8192, 64), (8256, 64), (8320, 64)], bcdt_evac)

        k.v("tensor_scalar", la_fm[:], dt_fm[:], acol[:, 0:1], -1.0,
            ALU.mult, op1=ALU.mult, wait=[s_se, s_ve])
        k.op("vector", lambda h: h.tensor_tensor_scan(
            cs_fm[:], csmask[:], la_fm[:], 0.0, ALU.mult, ALU.add),
            reads=(csmask[:], la_fm[:]), writes=(cs_fm[:],))
        k.vdrain(s_ve)
        if "bcdt" in debug:
            k.dma("sync", bcdt_sp[0], B_fm[:], inc=s_sp, wait=[s_ve])
            k.dma("sync", bcdt_sp[1], C_fm[:], inc=s_sp)
            k.dma("sync", bcdt_sp[2], dt_fm[:], inc=s_sp)
            k.dma("sync", bcdt_sp[3], cs_fm[:], inc=s_sp)

        xt_marks = {}
        ptm = {}
        mqm = {}
        mmvt = {}
        ttm = {}
        ttm2 = {}
        ptm2 = {}
        for half in range(2):
            xn0 = half * 512
            if half == 1:
                xw = [(s_ve, s_ve.n), (s_se, s_se.n), (s_mm, s_mm.n),
                      (s_vt, s_vt.n), (s_m, s_m.n)]
                for kd in range(16):
                    k.dma("sync", xn_fm[:, kd, :], xn_sp[kd], inc=s_xnr,
                          wait=xw if kd == 0 else None)
            vfree = None
            if half == 1 and "v" in debug:
                vfree = (s_sp, s_sp.n)
            xsets = [[pb[0], pb[1], pb[2], pb[3]],
                     [pb[4], pb[5], ptr[0], ptr[1]]]
            xfree = {0: list(bank_free), 1: list(bank_free)}
            for gi, g0 in enumerate(range(0, 32, 2)):
                gt = [(4096 + m * 128, 128) for m in range(g0, min(g0 + 2, 32))]
                nt = len(gt)
                bs = xsets[gi % 2]
                for kd in range(16):
                    c0 = gt[0][0]
                    c1 = gt[-1][0] + gt[-1][1]
                    slot = len(wmarks) % 2
                    wb = wband[:, slot, 0:c1 - c0]
                    lw = [(s_mm, wmarks[-2])] if len(wmarks) >= 2 else None
                    k.dma("sync", wb, w_in[kd * 128:(kd + 1) * 128, c0:c1],
                          inc=s_wb[slot], wait=lw)
                    for mi in range(nt):
                        wc0, wm = gt[mi]
                        lhsT = wb[:, wc0 - c0:wc0 - c0 + wm]
                        mmw = []
                        if mi == 0:
                            mmw.append((s_wb[slot], s_wb[slot].n))
                            if kd == 0:
                                mmw += xfree[gi % 2] + [(s_xnr, s_xnr.n)]
                        k.mm(bs[mi][:, 0:512], lhsT,
                             xn_fm[:, kd, xn0:xn0 + 512],
                             start=(kd == 0), stop=(kd == 15), wait=mmw or None)
                        k.mm(bs[2 + mi][:, 0:3], lhsT,
                             xn_fm[:, kd, xn0 + 512:xn0 + 515],
                             start=(kd == 0), stop=(kd == 15),
                             inc=s_mm if (mi == nt - 1) else None)
                    wmarks.append(s_mm.n)
                for mi in range(nt):
                    m = g0 + mi
                    xt = xtmp[:, m % 2]
                    xw = [(s_mm, s_mm.n)]
                    if (m - 2) in xt_marks:
                        xw.append((s_ve, xt_marks[m - 2]))
                    k.act(xt[:, 0:512], bs[mi][:, 0:512], AF.Copy, wait=xw)
                    k.act(xt[:, 512:515], bs[2 + mi][:, 0:3], AF.Copy, inc=s_se)
                    cw = convw[:, m]
                    vdst = v_fm[:, m, :]
                    vw = [s_se]
                    if vfree is not None:
                        vw.append(vfree)
                    k.v("tensor_scalar_mul", vdst, xt[:, 0:512], cw[:, 0:1],
                        wait=vw)
                    for kk in range(1, 4):
                        k.v("scalar_tensor_tensor", vdst, xt[:, kk:kk + 512],
                            cw[:, kk:kk + 1], vdst, ALU.mult, ALU.add)
                    k.vdrain(s_ve)
                    xt_marks[m] = s_ve.n
                    k.act(vdst, vdst, AF.Silu, bias=convb[:, m:m + 1],
                          wait=[s_ve], inc=s_se)
                xfree[gi % 2] = [(s_se, s_se.n)]
                bank_free = [(s_se, s_se.n)]
                if "v" in debug:
                    for mi in range(nt):
                        m = g0 + mi
                        k.dma("sync", vfm_sp[half, m], v_fm[:, m, :],
                              inc=s_sp, wait=[(s_se, s_se.n)])

            if half == 0:
                k.op("vector", lambda h: h.memset(h_loc, 0.0),
                     writes=(h_loc,), wait=[(s_mm, s_mm.n), (s_se, s_se.n)])
                k.vdrain(s_ve)
            vconsumed = []
            for c in range(8):
                gc = half * 8 + c
                csl = slice(gc * 64, gc * 64 + 64)
                for m in range(32):
                    pslot = ptr[m % 2]
                    tw = [(s_se, s_se.n)] if m < 2 else [(s_se, ptm[m - 2])]
                    if m == 0:
                        tw = tw + [(s_mm, mmvt.get("x", 0))]
                    ew_extra = [xnsp_mark] if (m == 0 and c == 0 and half == 0) else []
                    k.trans(pslot[0:64, 0:128], v_fm[:, m, c * 64:(c + 1) * 64],
                            ident[:], wait=tw, inc=s_mm)
                    ew = [s_mm] if m > 0 else [s_mm, (s_ve, s_ve.n)]
                    k.act(vt_sb[:, m * 128:(m + 1) * 128], pslot[0:64, 0:128],
                          AF.Copy, wait=ew + ew_extra, inc=s_se)
                    ptm[m] = s_se.n
                vt_done = (s_se, s_se.n)
                vconsumed.append(s_mm.n)
                k.dma("sync", vtok_sp[gc], vt_sb, inc=s_vt, wait=[vt_done])
                k.v("tensor_scalar", tmp2_fm[:], cs_fm[:, csl],
                    cs_fm[:, gc * 64 + 63:gc * 64 + 64], -1.0, ALU.subtract,
                    op1=ALU.mult, wait=[vt_done])
                k.vdrain(s_ve)
                k.trans(ptr[0][0:64, 0:64], tmp2_fm[:], ident[0:64, 0:64],
                        wait=[s_ve], inc=s_mm)
                k.trans(ptr[1][0:64, 0:64], B_fm[:, csl], ident[0:64, 0:64],
                        inc=s_mm)
                k.trans(ptr[0][0:64, 64:128], cs_fm[:, csl], ident[0:64, 0:64],
                        inc=s_mm)
                k.act(e2s[:, gc], ptr[0][0:64, 0:64], AF.Exp, wait=[s_mm])
                k.act(b_tok[:], ptr[1][0:64, 0:64], AF.Copy)
                k.act(cs_tok[:, gc], ptr[0][0:64, 64:128], AF.Copy)
                k.act(ecs_tok[:, gc], ptr[0][0:64, 64:128], AF.Exp, inc=s_se)
                ptm[0] = s_se.n
                ptm[1] = s_se.n
                k.v("tensor_mul", drep[:], e2s[:, gc], ecs_tok[:, gc],
                    wait=[s_se])
                k.v("tensor_tensor",
                    vt_sb.rearrange("t (h p) -> t h p", h=64),
                    vt_sb.rearrange("t (h p) -> t h p", h=64),
                    e2s[:, gc][:, :, None].broadcast_to([64, 64, 64]), ALU.mult,
                    wait=[s_vt])
                k.vdrain(s_ve)
                for q in range(8):
                    k.mm(pb[q % 6][0:64, 0:512], b_tok[:],
                         vt_sb[:, q * 512:(q + 1) * 512], start=True,
                         stop=True,
                         wait=[s_ve] if q == 0 else ([(s_se, mqm[q - 6])] if q >= 6 else None),
                         inc=s_mm)
                    macw = [s_mm]
                    if q == 0:
                        macw.append((s_m, s_m.n))
                    k.act(m_sb[:, q * 512:(q + 1) * 512], pb[q % 6][0:64, 0:512],
                          AF.Copy, wait=macw, inc=s_se)
                    mqm[q] = s_se.n
                mmvt["x"] = s_mm.n
                m_done = (s_se, s_se.n)
                k.dma("sync", m_sp[gc], m_sb, inc=s_m, wait=[m_done])
                k.v("tensor_tensor",
                    h_loc.rearrange("n (h p) -> n h p", h=64),
                    h_loc.rearrange("n (h p) -> n h p", h=64),
                    drep[:, :, None].broadcast_to([64, 64, 64]), ALU.mult,
                    wait=[m_done])
                k.v("tensor_add", h_loc, h_loc, m_sb[:])
                k.vdrain(s_ve)
                mspill_scan = [(s_m, s_m.n), (s_ve, s_ve.n)]
            bank_free = bank_free + [(s_se, s_se.n), (s_ve, s_ve.n)]

        k.v("reduce_sum", tcol[:], la_fm[:], axis=AX.X, wait=[(s_se, s_se.n)])
        k.vdrain(s_ve)
        k.act(tcol[:], tcol[:], AF.Exp, wait=[s_ve], inc=s_se)
        k.dma("sync", f_bnc, h_loc, inc=s_ft, wait=[(s_ve, s_ve.n)])
        k.dma("sync", t_bnc, tcol[:], inc=s_ft, wait=[(s_se, s_se.n)])
        if "ft" in debug:
            k.dma("sync", ft_sp[0:64, :], h_loc, inc=s_sp,
                  wait=[(s_ve, s_ve.n)])
            k.dma("sync", ft_sp[64:65, 0:64], tcol[:, 0:1].rearrange("a b -> b a") if False else tcol[0:64, 0:1],
                  inc=s_sp, wait=[(s_se, s_se.n)])
        cc_ready = (s_ft, s_ft.n)
        k.op("gpsimd", lambda h: h.collective_compute(
            "AllGather", ALU.bypass, replica_groups=[list(range(NCORES))],
            ins=[f_bnc.opt()], outs=[f_gat.opt()]), wait=[cc_ready], inc=s_cc)
        k.op("gpsimd", lambda h: h.collective_compute(
            "AllGather", ALU.bypass, replica_groups=[list(range(NCORES))],
            ins=[t_bnc.opt()], outs=[t_gat.opt()]), wait=None, inc=s_cc)

        wrt_sb = sbuf("wrt_sb", [128, 16 * E])
        mg_sb = sbuf("mg_sb", [128, 8, E])
        lgt = sbuf("lgt", [128, E + 4])
        p_part = zstage.ap()[0:8].rearrange("p a b -> p (a b)")[:, 0:D]
        k.dma("sync", wrt_sb[:], wrt, inc=s_c2)
        xw = [(s_ve, s_ve.n), (s_se, s_se.n), (s_mm, s_mm.n),
              (s_vt, s_vt.n), (s_m, s_m.n)]
        for kd in range(16):
            k.dma("sync", xn_fm[:, kd, :], xn_sp[kd], inc=s_xnr,
                  wait=xw if kd == 0 else None)
        for tt in range(8):
            tc = 3 + tt * 128
            for kd in range(16):
                k.mm(pb[0][0:128, 0:E], xn_fm[:, kd, tc:tc + 128],
                     wrt_sb[:, kd * E:(kd + 1) * E],
                     start=(kd == 0), stop=(kd == 15),
                     wait=([(s_xnr, s_xnr.n), (s_c2, s_c2.n)] + bank_free +
                           [(s_se, s_se.n)]) if (tt == 0 and kd == 0) else
                          ([(s_se, ttm[tt - 1])] if kd == 0 and tt > 0 else None),
                     inc=s_mm if kd == 15 else None)
            lcw = [s_mm] + ([(s_ve, ttm2[tt - 1])] if tt > 0 else [])
            k.act(lgt[:, 0:E], pb[0][:, 0:E], AF.Copy, wait=lcw, inc=s_se)
            k.v("reduce_max", lgt[:, E:E + 1], lgt[:, 0:E], axis=AX.X,
                wait=[s_se])
            k.v("tensor_single_scalar", mg_sb[:, tt], lgt[:, 0:E],
                lgt[:, E:E + 1], ALU.is_equal)
            k.vdrain(s_ve)
            k.v("scalar_tensor_tensor", lgt[:, 0:E], mg_sb[:, tt], -1e30,
                lgt[:, 0:E], ALU.mult, ALU.add)
            k.v("reduce_max", lgt[:, E + 1:E + 2], lgt[:, 0:E], axis=AX.X)
            k.vdrain(s_ve)
            k.v("tensor_single_scalar", lgt[:, 0:E], lgt[:, 0:E],
                lgt[:, E + 1:E + 2], ALU.is_equal)
            k.v("tensor_sub", lgt[:, E + 2:E + 3], lgt[:, E:E + 1],
                lgt[:, E + 1:E + 2])
            k.vdrain(s_ve)
            k.act(lgt[:, E + 2:E + 3], lgt[:, E + 2:E + 3], AF.Sigmoid,
                  wait=[s_ve], inc=s_se)
            k.v("tensor_scalar", lgt[:, E + 3:E + 4], lgt[:, E + 2:E + 3],
                -1.0, 1.0, ALU.mult, op1=ALU.add, wait=[s_se])
            k.vdrain(s_ve)
            k.v("tensor_scalar", mg_sb[:, tt], mg_sb[:, tt],
                lgt[:, E + 2:E + 3], None, ALU.mult, wait=[s_ve])
            k.v("scalar_tensor_tensor", mg_sb[:, tt], lgt[:, 0:E],
                lgt[:, E + 3:E + 4], mg_sb[:, tt], ALU.mult, ALU.add)
            k.vdrain(s_ve)
            ttm[tt] = s_se.n
            ttm2[tt] = s_ve.n
        mg_done = (s_ve, s_ve.n)
        if "p" in debug:
            k.dma("sync", mg_sp,
                  mg_sb.ap().rearrange("p a b -> p (a b)"), inc=s_sp,
                  wait=[mg_done])

        h1v = v_fm.ap().rearrange("p a b -> p (a b)")
        h2v = xnflat128
        def ffn_fm(wsrc, mov, dest_fn, func, first_wait):
            for g0 in range(0, 16, 3):
                gt = list(range(g0, min(g0 + 3, 16)))
                nt = len(gt)
                for kd in range(16):
                    slot = len(wmarks) % 2
                    wb = wband[:, slot, 0:nt * 128]
                    lw = [(s_mm, wmarks[-2])] if len(wmarks) >= 2 else None
                    k.dma("sync", wb,
                          wsrc[kd * 128:(kd + 1) * 128,
                               g0 * 128:g0 * 128 + nt * 128],
                          inc=s_wb[slot], wait=lw)
                    for mi in range(nt):
                        lhsT = wb[:, mi * 128:(mi + 1) * 128]
                        for hi in range(2):
                            mmw = []
                            if mi == 0 and hi == 0:
                                mmw.append((s_wb[slot], s_wb[slot].n))
                                if kd == 0:
                                    mmw += bank_free2[0]
                                    if g0 == 0:
                                        mmw += first_wait
                            k.mm(pb[hi * 3 + mi][:, 0:512], lhsT,
                                 mov(kd, hi), start=(kd == 0), stop=(kd == 15),
                                 wait=mmw or None,
                                 inc=s_mm if (mi == nt - 1 and hi == 1) else None)
                    wmarks.append(s_mm.n)
                for mi in range(nt):
                    m = g0 + mi
                    for hi in range(2):
                        k.act(dest_fn(m, hi), pb[hi * 3 + mi][:, 0:512], func,
                              wait=[(s_mm, s_mm.n)], inc=s_se)
                bank_free2[0] = [(s_se, s_se.n)]

        bank_free2 = [bank_free + [mg_done, (s_se, s_se.n)]]
        ffn_fm(w1t, lambda kd, hi: xn_fm[:, kd, 3 + hi * 512:3 + hi * 512 + 512],
               lambda m, hi: h1v[:, m * 1024 + hi * 512:m * 1024 + hi * 512 + 512],
               AF.Silu, [])
        ffn_fm(w3t, lambda kd, hi: h1v[:, kd * 1024 + hi * 512:kd * 1024 + hi * 512 + 512],
               lambda m, hi: h2v[:, m * 1024 + hi * 512:m * 1024 + hi * 512 + 512],
               AF.Silu, [(s_mm, s_mm.n)])
        if "h2" in debug:
            for m in range(16):
                k.dma("sync", h2_sp[m],
                      h2v[:, m * 1024:(m + 1) * 1024], inc=s_sp,
                      wait=[(s_se, s_se.n)])
        ffn_fm(w2t, lambda kd, hi: h2v[:, kd * 1024 + hi * 512:kd * 1024 + hi * 512 + 512],
               lambda m, hi: h1v[:, m * 1024 + hi * 512:m * 1024 + hi * 512 + 512],
               AF.Copy, [(s_mm, s_mm.n)])
        htok = lntmp.ap()
        hw0 = [(s_se, s_se.n), (s_ve, s_ve.n)]
        for tt in range(8):
            for dtile in range(16):
                pslot = ptr[dtile % 2]
                tw = ([(s_se, ptm2[(tt, dtile - 2)])] if dtile >= 2
                      else [(s_se, ptm2.get((tt - 1, 14 + dtile), s_se.n))])
                if tt == 0 and dtile < 2:
                    tw = [(s_se, s_se.n)]
                if tt == 0 and dtile == 0:
                    tw += hw0
                k.trans(pslot[:, 0:128],
                        h1v[:, dtile * 1024 + tt * 128:dtile * 1024 + tt * 128 + 128],
                        ident[:], wait=tw, inc=s_mm)
                k.act(htok[:, dtile * 128:(dtile + 1) * 128], pslot[:, 0:128],
                      AF.Copy, wait=[s_mm], inc=s_se)
                ptm2[(tt, dtile)] = s_se.n
            for q in range(4):
                k.mm(pb[q][0:8, 0:512], mg_sb[:, tt],
                     htok[:, q * 512:(q + 1) * 512],
                     start=(tt == 0), stop=(tt == 7),
                     wait=[(s_se, s_se.n)] if q == 0 else None,
                     inc=s_mm if q == 3 else None)
        for q in range(4):
            k.act(p_part[:, q * 512:(q + 1) * 512], pb[q][0:8, 0:512], AF.Copy,
                  wait=[s_mm] + ([(sz, sz.n) for sz in s_zs] if q == 0 else []),
                  inc=s_se)
        if "p" in debug:
            k.dma("sync", p_sp, p_part, inc=s_sp, wait=[(s_se, s_se.n)])

        k.dma("sync", p_bnc, p_part, inc=s_pb, wait=[(s_se, s_se.n)])
        pcc_ready = (s_pb, s_pb.n)
        k.op("gpsimd", lambda h: h.collective_compute(
            "AllReduce", ALU.add, replica_groups=[list(range(NCORES))],
            ins=[p_bnc.opt()], outs=[p_tot.opt()]), wait=[pcc_ready], inc=s_cc)

        hbuf = h_loc
        SPc = 3 * DI + 2048
        trep = xnflat[:, SPc + 768:SPc + 832]
        tm_t = xnflat[:, SPc + 832:SPc + 896]
        cmask = mg_sb.ap()[0:64].rearrange("p a b -> p (a b)")[:, 32:40]
        cmaski = mg_sb.ap()[0:64].rearrange("p a b -> p (a b)")[:, 40:48]
        fbufs = [xnflat[:, 0:DI], xnflat[:, DI:2 * DI]]
        s_f = [sem("f0"), sem("f1")]
        cmw = [(s_mm, s_mm.n), (s_se, s_se.n), (s_ve, s_ve.n)]
        k.dma("sync", cmask, combmask, inc=s_c3, wait=cmw)
        k.dma("sync", cmaski, combmaskinv, inc=s_c3)
        k.op("vector", lambda h: h.memset(hbuf, 0.0), writes=(hbuf,),
             wait=[(s_mm, s_mm.n), (s_se, s_se.n), (s_sp, s_sp.n),
                   (s_ft, s_ft.n)])
        k.vdrain(s_ve)
        fjm = {}
        for j in range(NCORES - 1):
            fb = fbufs[j % 2]
            fw = [(s_cc, 2), (s_mm, s_mm.n), (s_se, s_se.n)]
            if j >= 2:
                fw.append((s_ve, fjm[j - 2]))
            k.dma("sync", fb, f_gat[j], inc=s_f[j % 2], wait=fw)
            k.dma("sync", trep,
                  t_gat[j].rearrange("h one -> one h").broadcast_to([64, 64]),
                  inc=s_f[j % 2],
                  wait=[(s_ve, fjm[j - 1])] if j >= 1 else None)
            k.v("tensor_scalar", tm_t, trep, cmask[:, j:j + 1],
                cmaski[:, j:j + 1], ALU.mult, op1=ALU.add,
                wait=[(s_f[j % 2], s_f[j % 2].n), (s_c3, s_c3.n)])
            k.vdrain(s_ve)
            k.v("tensor_tensor", hbuf.rearrange("n (h p) -> n h p", h=64),
                hbuf.rearrange("n (h p) -> n h p", h=64),
                tm_t[:, :, None].broadcast_to([64, 64, 64]), ALU.mult)
            k.v("scalar_tensor_tensor", hbuf, fb, cmask[:, j:j + 1], hbuf,
                ALU.mult, ALU.add)
            k.vdrain(s_ve)
            fjm[j] = s_ve.n
        hin_done = (s_ve, s_ve.n)

        vtk2 = xnflat128[:, 0:DI]
        mrgY = xnflat128[:, DI:2 * DI]
        mrgA = xnflat[:, DI:2 * DI]
        att2 = xnflat128[:, 2 * DI:3 * DI]
        zb2 = xnflat128[:, 3 * DI:4 * DI]
        wob = [xnflat128[:, 0:2048], xnflat128[:, 2048:4096]]
        lnf = lntmp.ap()
        tmpq = lnf[:, 0:512]
        masklv = lnf[:, 512:576]
        gm2 = lnf[:, 576:640]
        drepA = lnf[0:64, 640:704]
        drepB = lnf[0:64, 1088:1152]
        cstk2 = lnf[:, 704:768]
        ecs2 = lnf[:, 768:832]
        drpDv = lnf[:, 896:960]
        nsc = mg_sb.ap().rearrange("p a b -> p (a b)")[:, 0:32]
        rcol = mg_sb.ap().rearrange("p a b -> p (a b)")[:, 48:50]
        smw = [(s_mm, s_mm.n), (s_se, s_se.n), (s_ve, s_ve.n)]
        k.dma("sync", nsc, norm_scale, inc=s_c4, wait=smw)
        s_vl = sem("vl")
        s_ml = sem("ml")
        s_zl = sem("zl")
        s_cb = [sem("cb0"), sem("cb1")]
        s_cbr = sem("cbr")
        s_dr = sem("dr")
        s_ur = sem("ur")
        s_ot = sem("ot")
        marks = {"amm": {}, "yev": {}, "scan": {}, "tp": {}, "gy": {},
                 "zl": {}, "sq": {}, "sm": {}, "ml": {}}
        gy_fm = v_fm
        for half in range(2):
            mlw = [(s_ve, s_ve.n), (s_se, s_se.n)] if half == 0 else \
                  [(s_ot, s_ot.n), (s_ve, s_ve.n)]
            k.dma("sync", masklv, maskl_in, inc=s_c5, wait=mlw)
            k.dma("sync", drpDv, d_rep, inc=s_c5)
            mklm = (s_c5, s_c5.n)
            for p in range(4):
                ga = half * 8 + 2 * p
                gb = ga + 1
                sla = slice(ga * 64, ga * 64 + 64)
                slb = slice(gb * 64, gb * 64 + 64)
                slab = slice(ga * 64, ga * 64 + 128)
                pp = half * 4 + p
                if pp == 0:
                    vw = [hin_done, (s_mm, s_mm.n), (s_cb[0], s_cb[0].n)]
                elif p == 0:
                    vw = [(s_mm, s_mm.n), (s_ve, s_ve.n)]
                else:
                    vw = [(s_mm, marks["amm"][pp - 1]),
                          (s_ve, marks["yev"][pp - 1]),
                          (s_se, marks["sq"].get(pp - 1, 0))]
                k.dma("sync", vtk2,
                      vtok_sp.rearrange("c s d -> (c s) d")[ga * 64:ga * 64 + 128, :],
                      inc=s_vl, wait=vw)
                mw = [hin_done] if pp == 0 else \
                     [(s_ve, marks["scan"][pp - 1]), (s_mm, marks["tp"].get(pp - 1, 0)),
                      (s_se, marks["sq"].get(pp - 1, 0))]
                k.dma("sync", mrgA, m_sp[ga], inc=s_ml, wait=mw)
                zw = [(s_se, marks["zl"][pp - 1]),
                      (s_ve, marks["gy"][(pp - 1, 31)])] if pp > 0 else \
                     [(s_mm, s_mm.n), hin_done]
                k.dma("gpsimd", zb2.rearrange("p (m t) -> p m t", m=32),
                      z_sp[:, :, ga * 64:ga * 64 + 128].rearrange("m p t -> p m t"),
                      inc=s_zl, wait=zw)
                cw = [(s_cb[p % 2], s_cb[p % 2].n)]
                k.dma("gpsimd", csrow[p % 2, 0], cs_fm[:, sla],
                      inc=s_cb[p % 2], wait=cw)
                k.dma("gpsimd", csrow[p % 2, 1], cs_fm[:, slb],
                      inc=s_cb[p % 2])
                k.dma("gpsimd", totrow[p % 2, 0],
                      cs_fm[:, ga * 64 + 63:ga * 64 + 64], inc=s_cb[p % 2])
                k.dma("gpsimd", totrow[p % 2, 1],
                      cs_fm[:, gb * 64 + 63:gb * 64 + 64], inc=s_cb[p % 2])
                rw = [(s_cb[p % 2], s_cb[p % 2].n)]
                if pp > 0:
                    rw.append((s_mm, marks["amm"][pp - 1]))
                    rw.append((s_ve, marks["yev"][pp - 1]))
                k.dma("gpsimd", att2[0:64, :],
                      csrow[p % 2, 0:1].rearrange('a b -> (a b)')[None, :].broadcast_to([64, NH * S]) if False else csrow[p % 2, 0:1, :].rearrange('a f -> a f').broadcast_to([64, NH * S]),
                      inc=s_cbr, wait=rw)
                k.dma("gpsimd", att2[64:128, :],
                      csrow[p % 2, 1:2, :].broadcast_to([64, NH * S]), inc=s_cbr)
                csb_ld = (s_cbr, s_cbr.n)
                sw = [(s_ve, s_ve.n)]
                daw = [(s_cb[p % 2], s_cb[p % 2].n),
                       (s_ve, marks["scan"].get(pp - 1, 0))]
                if p == 0:
                    daw += [(s_ot, s_ot.n), (s_ve, s_ve.n), (s_ur, s_ur.n)]
                k.dma("gpsimd", drepA,
                      totrow[p % 2, 0:1, :].broadcast_to([64, NH]),
                      inc=s_dr, wait=daw)
                k.dma("gpsimd", drepB,
                      totrow[p % 2, 1:2, :].broadcast_to([64, NH]), inc=s_dr)
                tpw = sw + [(s_se, marks["sm"].get(pp - 1, 0))]
                k.trans(ptr[1][:, 0:64], cs_fm[:, slab], ident[0:64, 0:64],
                        wait=tpw, inc=s_mm)
                k.mm(pb[0][0:64, 0:64], B_fm[:, sla], C_fm[:, sla],
                     start=True, stop=True)
                k.mm(pb[2][64:128, 0:64], B_fm[:, slb], C_fm[:, slb],
                     start=True, stop=True, inc=s_mm)
                k.act(drepA, drepA, AF.Exp,
                      wait=[s_mm, (s_dr, s_dr.n), (s_ve, s_ve.n)])
                k.act(drepB, drepB, AF.Exp)
                k.act(cstk2[:], ptr[1][:, 0:64], AF.Copy)
                k.act(ecs2[:], ptr[1][:, 0:64], AF.Exp)
                k.act(gm2[0:64, :], pb[0][0:64, 0:64], AF.Copy)
                k.act(gm2[64:128, :], pb[2][64:128, 0:64], AF.Copy, inc=s_se)
                marks["sm"][pp] = s_se.n
                a3 = att2.rearrange("t (h s) -> t h s", h=NH)
                k.v("tensor_tensor", a3, a3,
                    masklv[:, None, :].broadcast_to([128, NH, S]), ALU.add,
                    wait=[csb_ld, mklm])
                k.v("tensor_tensor", a3, a3,
                    cstk2[:, :, None].broadcast_to([128, NH, S]), ALU.subtract,
                    wait=[(s_se, marks["sm"][pp])])
                k.vdrain(s_ve)
                k.act(att2, att2, AF.Exp, wait=[s_ve], inc=s_se)
                k.v("tensor_tensor", a3, a3,
                    gm2[:, None, :].broadcast_to([128, NH, S]), ALU.mult,
                    wait=[s_se])
                k.vdrain(s_ve)
                att_done = (s_ve, s_ve.n)
                if "att" in debug and pp == 0:
                    k.dma("sync", att_sp, att2, inc=s_sp, wait=[att_done])
                    k.dma("sync", vt2_sp, vtk2, inc=s_sp,
                          wait=[(s_vl, s_vl.n)])
                    k.dma("sync", hin_sp, hbuf, inc=s_sp, wait=[hin_done])
                    k.dma("sync", sml_sp, lnf[:, 0:1152], inc=s_sp,
                          wait=[(s_se, s_se.n), (s_ve, s_ve.n)])
                for q in range(4):
                    qs = q * 1024
                    for bh in range(2):
                        qq = qs + bh * 512
                        w0 = None
                        if q == 0 and bh == 0:
                            w0 = [att_done, (s_vl, s_vl.n), (s_ml, s_ml.n),
                                  (s_se, marks["sm"][pp])]
                            if pp > 0:
                                w0 = w0 + [(s_ve, marks["yev"][pp - 1])]
                        elif bh == 0 and q > 0:
                            w0 = [(s_ve, marks["yev"][(pp, q - 1)])]
                        k.mm(pb[bh][0:64, 0:512], C_fm[:, sla],
                             hbuf[:, qq:qq + 512], start=True, stop=True,
                             wait=w0, inc=s_mm if bh == 1 else None)
                    interE = s_mm.n
                    for bh in range(2):
                        for hh in range(8):
                            hid = q * 16 + bh * 8 + hh
                            k.mm(pb[4 + bh][0:64, hh * 64:hh * 64 + 64],
                                 att2[0:64, hid * 64:hid * 64 + 64],
                                 vtk2[0:64, hid * 64:hid * 64 + 64],
                                 start=(hh == 0), stop=(hh == 7))
                            k.mm(ptr[bh][64:128, hh * 64:hh * 64 + 64],
                                 att2[64:128, hid * 64:hid * 64 + 64],
                                 vtk2[64:128, hid * 64:hid * 64 + 64],
                                 start=(hh == 0), stop=(hh == 7),
                                 inc=s_mm if hh == 7 else None)
                    intraM = s_mm.n
                    k.v("tensor_tensor",
                        hbuf[:, qs:qs + 1024].rearrange("n (h p) -> n h p", h=16),
                        hbuf[:, qs:qs + 1024].rearrange("n (h p) -> n h p", h=16),
                        drepA[:, q * 16:q * 16 + 16][:, :, None]
                        .broadcast_to([64, 16, 64]), ALU.mult,
                        wait=[(s_mm, interE), (s_ml, s_ml.n),
                              (s_se, marks["sm"][pp])])
                    k.v("tensor_add", hbuf[:, qs:qs + 1024],
                        hbuf[:, qs:qs + 1024], mrgA[:, qs:qs + 1024])
                    k.vdrain(s_ve)
                    scanE = s_ve.n
                    k.dma("sync", mrgA[:, qs:qs + 1024],
                          m_sp[gb, :, qs:qs + 1024], inc=s_ml,
                          wait=[(s_ve, scanE)])
                    for bh in range(2):
                        qq = qs + bh * 512
                        k.mm(pb[2 + bh][64:128, 0:512], C_fm[:, slb],
                             hbuf[:, qq:qq + 512], start=True, stop=True,
                             wait=[(s_ve, scanE)] if bh == 0 else None,
                             inc=s_mm if bh == 1 else None)
                    k.v("tensor_tensor",
                        hbuf[:, qs:qs + 1024].rearrange("n (h p) -> n h p", h=16),
                        hbuf[:, qs:qs + 1024].rearrange("n (h p) -> n h p", h=16),
                        drepB[:, q * 16:q * 16 + 16][:, :, None]
                        .broadcast_to([64, 16, 64]), ALU.mult,
                        wait=[(s_mm, s_mm.n), (s_ml, s_ml.n)])
                    k.v("tensor_add", hbuf[:, qs:qs + 1024],
                        hbuf[:, qs:qs + 1024], mrgA[:, qs:qs + 1024])
                    k.vdrain(s_ve)
                    marks["scan"][pp] = s_ve.n
                    for bh in range(2):
                        qq = qs + bh * 512
                        hsl = slice(q * 16 + bh * 8, q * 16 + bh * 8 + 8)
                        for half_rows, pi, pa in ((slice(0, 64), pb[bh], pb[4 + bh]),
                                                  (slice(64, 128), pb[2 + bh], ptr[bh])):
                            dst = mrgY[half_rows, qq:qq + 512]
                            tq = tmpq[half_rows, :]
                            k.v("tensor_tensor",
                                dst.rearrange("t (h p) -> t h p", h=8),
                                pi[half_rows, 0:512].rearrange("t (h p) -> t h p", h=8),
                                ecs2[half_rows, hsl][:, :, None]
                                .broadcast_to([64, 8, 64]), ALU.mult,
                                wait=[(s_mm, s_mm.n), (s_ve, marks["scan"][pp]),
                                      (s_c5, s_c5.n)])
                            k.v("tensor_add", dst, dst, pa[half_rows, 0:512])
                            k.vdrain(s_ve)
                    marks["yev"][(pp, q)] = s_ve.n
                k.v("tensor_tensor",
                    att2.rearrange("t (h p) -> t h p", h=64),
                    vtk2.rearrange("t (h p) -> t h p", h=64),
                    drpDv[:, :, None].broadcast_to([128, 64, 64]), ALU.mult,
                    wait=[(s_mm, s_mm.n), (s_ve, marks["yev"][(pp, 3)]),
                          (s_c5, s_c5.n)])
                k.v("tensor_add", mrgY, mrgY, att2)
                k.vdrain(s_ve)
                marks["amm"][pp] = s_mm.n
                marks["yev"][pp] = s_ve.n
                marks["ml"][pp] = s_ml.n
                if "y" in debug:
                    k.dma("sync", y_sp[ga], mrgY[0:64, :], inc=s_sp,
                          wait=[(s_ve, s_ve.n)])
                    k.dma("sync", y_sp[gb], mrgY[64:128, :], inc=s_sp)
                k.act(vtk2, mrgY, AF.Square, accum_out=rcol[:, 0:1],
                      wait=[(s_ve, marks["yev"][pp]), (s_se, s_se.n),
                            (s_cbr, s_cbr.n)], inc=s_se)
                marks["sq"][pp] = s_se.n
                k.v("tensor_scalar", rcol[:, 0:1], rcol[:, 0:1], 1.0 / DI,
                    1e-6, ALU.mult, op1=ALU.add, wait=[s_se])
                k.vdrain(s_ve)
                k.act(rcol[:, 0:1], rcol[:, 0:1], AF.Sqrt, wait=[s_ve],
                      inc=s_se)
                k.v("reciprocal", rcol[:, 0:1], rcol[:, 0:1], wait=[s_se])
                k.vdrain(s_ve)
                k.v("tensor_scalar", mrgY, mrgY, rcol[:, 0:1], None, ALU.mult)
                k.vdrain(s_ve)
                yrms = (s_ve, s_ve.n)
                k.act(zb2, zb2, AF.Silu, wait=[(s_zl, s_zl.n)], inc=s_se)
                marks["zl"][pp] = s_se.n
                for m in range(32):
                    pslot = pb[m % 4]
                    tw = [yrms, (s_se, marks["zl"][pp])]
                    if (pp, m - 4) in marks["gy"]:
                        tw.append((s_ve, marks["gy"][(pp, m - 4)]))
                    k.trans(pslot[:, 0:128], mrgY[:, m * 128:(m + 1) * 128],
                            ident[:], wait=tw, inc=s_mm)
                    k.v("scalar_tensor_tensor",
                        gy_fm[:, m, 2 * p * 64:2 * p * 64 + 128],
                        zb2.rearrange("x (m t) -> x m t", m=32)[:, m],
                        nsc[:, m:m + 1], pslot[:, 0:128], ALU.mult,
                        ALU.mult, wait=[s_mm, (s_c4, s_c4.n)], inc=s_ve)
                    marks["gy"][(pp, m)] = s_ve.n
                marks["tp"][pp] = s_mm.n
            gy_done = (s_ve, s_ve.n)
            p6km = []
            p6_evac = gy_done
            for tpair in range(2):
                for kd in range(32):
                    slot = kd % 2
                    if len(p6km) >= 2:
                        bw = [(s_mm, p6km[-2])]
                    elif half == 0 and tpair == 0:
                        bw = [gy_done, (s_vl, s_vl.n)]
                    else:
                        bw = [gy_done]
                    k.dma("sync", wob[slot], w_out[kd * 128:(kd + 1) * 128, :],
                          inc=s_wb[slot], wait=bw)
                    for tp in range(2):
                        tt = tpair * 2 + tp
                        for q in range(4):
                            pk = (pb + ptr)[tp * 4 + q]
                            mmw = None
                            if tp == 0 and q == 0:
                                mmw = [(s_wb[slot], s_wb[slot].n)]
                                if kd == 0:
                                    mmw.append(p6_evac)
                            k.mm(pk[:, 0:512],
                                 gy_fm[:, kd, tt * 128:(tt + 1) * 128],
                                 wob[slot][:, q * 512:(q + 1) * 512],
                                 start=(kd == 0), stop=(kd == 31),
                                 wait=mmw,
                                 inc=s_mm if (tp == 1 and q == 3) else None)
                    p6km.append(s_mm.n)
                for tp in range(2):
                    tt = tpair * 2 + tp
                    t0 = half * 512 + tt * 128
                    uw = [(s_ot, s_ot.n)]
                    if tpair == 0 and tp == 0:
                        uw += [(s_ve, s_ve.n), (s_se, s_se.n),
                               (s_dr, s_dr.n), (s_cbr, s_cbr.n),
                               (s_c5, s_c5.n)]
                    k.dma("sync", lntmp[:], u[3 + t0:3 + t0 + 128, :],
                          inc=s_ur, wait=uw)
                    for q in range(4):
                        pk = (pb + ptr)[tp * 4 + q]
                        k.v("tensor_add", lntmp[:, q * 512:(q + 1) * 512],
                            lntmp[:, q * 512:(q + 1) * 512], pk[:, 0:512],
                            wait=[(s_mm, p6km[-1]), (s_ur, s_ur.n)]
                            if q == 0 else None)
                    k.vdrain(s_ve)
                    k.dma("sync", out[t0:t0 + 128, :], lntmp[:], inc=s_ot,
                          wait=[(s_ve, s_ve.n)])
                p6_evac = (s_ve, s_ve.n)
        t8 = lntmp.ap()[0:8, :]
        ptot_sb = ustage.ap()[0:8].rearrange("p a b -> p (a b)")[:, 0:D]
        c0v = mg_sb.ap()[0:8].rearrange("p a b -> p (a b)")[:, 50:51]
        k.dma("sync", t8, out[0:8, :], inc=s_ur,
              wait=[(s_ot, s_ot.n), (s_cc, 3)])
        k.dma("sync", ptot_sb, p_tot, inc=s_ur,
              wait=[(s_ve, s_ve.n), (s_mm, s_mm.n)])
        k.dma("sync", c0v, core0[0:8, 0:1], inc=s_ur)
        k.v("scalar_tensor_tensor", t8, ptot_sb, c0v, t8,
            ALU.mult, ALU.add, wait=[(s_ur, s_ur.n)])
        k.vdrain(s_ve)
        k.dma("sync", out[0:8, :], t8, inc=s_ot, wait=[(s_ve, s_ve.n)])
        k.op("sync", lambda h: h.nop(),
             wait=[s_ot, s_sp, s_vt, s_m, s_xnr, s_cc, s_vl, s_ml, s_zl, s_ft, s_pb, s_c2, s_c3, s_c4, s_c5, s_dr,
                   s_cbr, s_ur, s_gld] + s_zs + s_u + s_wb + s_f + s_cb +
                  [s_const])

        k.emit()
    return nc


def _prep_inputs(L0_attn_u, L0_attn_in_proj_w, L0_attn_conv_weight,
                 L0_attn_conv_bias, L0_attn_A_log, L0_attn_dt_bias,
                 L0_attn_D_param, L0_attn_norm_scale, L0_attn_out_proj_w,
                 L0_ffn_w1, L0_ffn_w3, L0_ffn_w2, L0_ffn_wr):
    u = np.ascontiguousarray(np.asarray(L0_attn_u, np.float32).reshape(8192, D))
    upad = np.concatenate([np.zeros((HALO, D), np.float32), u], 0)
    csmask = np.ones((NH, TOK), np.float32)
    csmask[:, ::S] = 0.0
    shared = {
        "w_in": np.ascontiguousarray(L0_attn_in_proj_w, np.float32),
        "conv_w": np.ascontiguousarray(np.transpose(
            np.asarray(L0_attn_conv_weight, np.float32).reshape(4, 32, 128),
            (2, 1, 0))),
        "conv_b": np.ascontiguousarray(
            np.asarray(L0_attn_conv_bias, np.float32).reshape(32, 128).T),
        "a_log": np.asarray(L0_attn_A_log, np.float32).reshape(NH, 1),
        "dt_bias": np.asarray(L0_attn_dt_bias, np.float32).reshape(NH, 1),
        "d_rep": np.tile(np.asarray(L0_attn_D_param, np.float32).reshape(1, NH),
                         (128, 1)),
        "norm_scale": np.ascontiguousarray(
            np.asarray(L0_attn_norm_scale, np.float32).reshape(32, 128).T),
        "w_out": np.ascontiguousarray(L0_attn_out_proj_w, np.float32),
        "w1t": np.ascontiguousarray(np.asarray(L0_ffn_w1, np.float32).T),
        "w3t": np.ascontiguousarray(np.asarray(L0_ffn_w3, np.float32).T),
        "w2t": np.ascontiguousarray(np.asarray(L0_ffn_w2, np.float32).T),
        "wrt": np.ascontiguousarray(np.asarray(L0_ffn_wr, np.float32).T
                                    .reshape(16, 128, E).transpose(1, 0, 2)
                                    .reshape(128, 16 * E)),
        "ident": np.eye(128, dtype=np.float32),
        "csmask": csmask,
        "maskl": np.tile(np.where(
            np.arange(S)[None, :] >= np.arange(S)[:, None],
            0.0, -1e30).astype(np.float32), (2, 1)),
    }
    in_maps = []
    for c in range(NCORES):
        m = dict(shared)
        m["u"] = np.ascontiguousarray(upad[c * TOK:c * TOK + TOKH, :])
        cmb = np.zeros((64, NCORES), np.float32)
        cmb[:, :c] = 1.0
        m["combmask"] = cmb
        m["combmaskinv"] = (1.0 - cmb)
        m["core0"] = np.full((128, 1), 1.0 if c == 0 else 0.0, np.float32)
        in_maps.append(m)
    return in_maps


def kernel(**inputs):
    in_maps = _prep_inputs(**inputs)
    nc = build_program()
    res = run_bass_kernel_spmd(nc, in_maps, core_ids=list(range(NCORES)))
    return np.concatenate([r["out"] for r in res.results], 0).reshape(1, 8192, D)


# version 12
# speedup vs baseline: 1.2228x; 1.0003x over previous
import contextlib
import numpy as np
import concourse.bass as bass
import concourse.mybir as mybir
from concourse.bass_utils import run_bass_kernel_spmd

F32 = mybir.dt.float32
AF = mybir.ActivationFunctionType
ALU = mybir.AluOpType
AX = mybir.AxisListType

NCORES = 8
TOK = 1024
HALO = 3
TOKH = TOK + HALO
D = 2048
DI = 4096
NH = 64
NST = 64
S = 64
NCH = TOK // S
DIP = 8384
E = 8

class Sem:
    def __init__(self, handle):
        self.h = handle
        self.n = 0


class K:
    ENGS = ("tensor", "vector", "scalar", "gpsimd", "sync")

    def __init__(self, nc):
        self.nc = nc
        self.q = {e: [] for e in self.ENGS}
        self.dirty = {e: set() for e in self.ENGS}

    @staticmethod
    def _tnames(aps):
        out = set()
        for a in aps:
            if a is None or isinstance(a, (int, float)):
                continue
            t = getattr(a, "tensor", None)
            if t is not None:
                out.add(t.name)
        return out

    def op(self, eng, fn, reads=(), writes=(), inc=None, inc_k=1, wait=None,
           dma_inc=None):
        import traceback
        tag = None
        for fr in reversed(traceback.extract_stack()[:-1]):
            if fr.filename.endswith("kernel.py") and fr.name not in (
                    "op", "dma", "mm", "trans", "act", "v", "vdrain"):
                tag = f"{fr.name}:{fr.lineno}"
                break
        rn = self._tnames(reads)
        wn = self._tnames(writes)
        need_drain = bool(self.dirty[eng] & (rn | wn))
        waits = []
        if wait:
            for w in wait:
                waits.append((w.h, w.n) if isinstance(w, Sem) else (w[0].h, w[1]))
        if inc is not None:
            inc.n += inc_k
        if dma_inc is not None:
            dma_inc.n += 16
        if need_drain:
            self.dirty[eng] = set()

        def emit(h, nd=need_drain, waits=waits, fn=fn, inc=inc, inc_k=inc_k,
                 dma_inc=dma_inc, tag=tag):
            if nd:
                h.drain()
            for sh, sv in waits:
                h.wait_ge(sh, sv)
            try:
                ins = fn(h)
            except Exception as e:
                raise RuntimeError(f"emit failed at record site {tag}: {e}") from e
            if ins is not None:
                if inc is not None:
                    ins.then_inc(inc.h, inc_k)
                if dma_inc is not None:
                    ins.then_inc(dma_inc.h, 16)

        self.q[eng].append(emit)
        self.dirty[eng] |= wn

    def dma(self, eng, out, in_, inc=None, wait=None):
        self.op(eng, lambda h: h.dma_start(out=out, in_=in_),
                reads=(in_,), writes=(out,), dma_inc=inc, wait=wait)

    def mm(self, out, lhsT, rhs, start, stop, inc=None, wait=None):
        self.op("tensor",
                lambda h: h.matmul(out, lhsT, rhs, start=start, stop=stop),
                reads=(lhsT, rhs), writes=(out,), inc=inc, wait=wait)

    def trans(self, out, in_, ident, inc=None, wait=None):
        self.op("tensor", lambda h: h.transpose(out, in_, ident),
                reads=(in_, ident), writes=(out,), inc=inc, wait=wait)

    def act(self, out, in_, func, bias=0.0, scale=1.0, accum_out=None, inc=None,
            wait=None):
        self.op("scalar",
                lambda h: h.activation(out, in_, func, bias=bias, scale=scale,
                                       accum_out=accum_out),
                reads=(in_, None if isinstance(bias, float) else bias,
                       None if isinstance(scale, float) else scale),
                writes=(out, accum_out), inc=inc, wait=wait)

    def v(self, method, out, *ins, inc=None, wait=None, **kw):
        def fn(h, method=method, out=out, ins=ins, kw=kw):
            return getattr(h, method)(out, *ins, **kw)
        self.op("vector", fn, reads=ins, writes=(out,), inc=inc, wait=wait)

    def vdrain(self, inc):
        self.op("vector", lambda h: h.drain(), inc=inc)
        self.dirty["vector"] = set()

    def emit(self):
        with self.nc.Block() as block:
            @block.tensor
            def _(h):
                for f in self.q["tensor"]:
                    f(h)

            @block.vector
            def _(h):
                for f in self.q["vector"]:
                    f(h)

            @block.scalar
            def _(h):
                for f in self.q["scalar"]:
                    f(h)

            @block.gpsimd
            def _(h):
                for f in self.q["gpsimd"]:
                    f(h)

            @block.sync
            def _(h):
                for f in self.q["sync"]:
                    f(h)


def build_program(debug=(), stage=1):
    nc = bass.Bass("TRN2", target_bir_lowering=False, debug=False)

    def dram_in(name, shape):
        return nc.dram_tensor(name, shape, F32, kind="ExternalInput").ap()

    u = dram_in("u", [TOKH, D])
    w_in = dram_in("w_in", [D, DIP])
    conv_w = dram_in("conv_w", [128, 32, 4])
    conv_b = dram_in("conv_b", [128, 32])
    a_log = dram_in("a_log", [NH, 1])
    dt_bias = dram_in("dt_bias", [NH, 1])
    d_rep = dram_in("d_rep", [128, NH])
    norm_scale = dram_in("norm_scale", [128, 32])
    w_out = dram_in("w_out", [DI, D])
    w1t = dram_in("w1t", [D, D])
    w3t = dram_in("w3t", [D, D])
    w2t = dram_in("w2t", [D, D])
    wrt = dram_in("wrt", [128, 16 * E])
    ident_in = dram_in("ident", [128, 128])
    maskl_in = dram_in("maskl", [128, S])
    combmaskinv = dram_in("combmaskinv", [64, NCORES])
    csmask_in = dram_in("csmask", [NH, TOK])
    combmask = dram_in("combmask", [64, NCORES])
    core0 = dram_in("core0", [128, 1])

    out = nc.dram_tensor("out", [TOK, D], F32, kind="ExternalOutput").ap()

    def scratch(name, shape, dbg=False):
        kind = "ExternalOutput" if dbg else "Internal"
        if dbg:
            return nc.dram_tensor(name, shape, F32, kind="ExternalOutput").ap()
        return nc.dram_tensor(name, shape, F32).ap()

    xn_sp = scratch("xn_sp", [16, 128, TOKH], "xn" in debug)
    z_sp = scratch("z_sp", [32, 128, TOK], "z" in debug)
    bcdt_sp = scratch("bcdt_sp", [4, 64, TOK], "bcdt" in debug)
    vfm_sp = scratch("vfm_sp", [2, 32, 128, 512], "v" in debug)
    vtok_sp = scratch("vtok_sp", [NCH, 64, DI], "vtok" in debug)
    m_sp = scratch("m_sp", [NCH, 64, DI], "m" in debug)
    ft_sp = scratch("ft_sp", [65, DI], "ft" in debug)
    p_sp = scratch("p_sp", [8, D], "p" in debug)
    mg_sp = scratch("mg_sp", [128, 64], "p" in debug)
    h2_sp = scratch("h2_sp", [16, 128, TOK], "h2" in debug)
    f_bnc = nc.dram_tensor("f_bnc", [64, DI], F32).ap()
    t_bnc = nc.dram_tensor("t_bnc", [64, 1], F32).ap()
    f_gat = nc.dram_tensor("f_gat", [NCORES, 64, DI], F32).ap()
    t_gat = nc.dram_tensor("t_gat", [NCORES, 64, 1], F32).ap()
    csrow = nc.dram_tensor("csrow", [2, 2, NH * S], F32).ap()
    totrow = nc.dram_tensor("totrow", [2, 2, NH], F32).ap()
    rrow = nc.dram_tensor("rrow", [2, 1, S], F32).ap()
    p_bnc = nc.dram_tensor("p_bnc", [8, D], F32).ap()
    p_tot = nc.dram_tensor("p_tot", [8, D], F32).ap()
    y_sp = scratch("y_sp", [NCH, 64, DI], "y" in debug)
    att_sp = scratch("att_sp", [128, DI], "att" in debug)
    vt2_sp = scratch("vt2_sp", [128, DI], "att" in debug)
    hin_sp = scratch("hin_sp", [64, DI], "att" in debug)
    sml_sp = scratch("sml_sp", [128, 1152], "att" in debug)
    hq_sp = scratch("hq_sp", [64, DI], "att" in debug)

    k = K(nc)
    sems = {}

    def sem(name):
        s = Sem(stackctx.enter_context(nc.semaphore(name)))
        sems[name] = s
        return s

    stackctx = contextlib.ExitStack()
    with stackctx:
        def sbuf(name, shape, dtype=F32):
            return stackctx.enter_context(nc.sbuf_tensor(name, shape, dtype))

        def psum(name):
            return stackctx.enter_context(nc.psum_tensor(name, [128, 512], F32))

        s_const = sem("const")
        s_u = [sem("u0"), sem("u1")]
        s_wb = [sem("wb0"), sem("wb1")]
        s_zs = [sem(f"zs{i}") for i in range(4)]
        s_mm = sem("mm")
        s_ve = sem("ve")
        s_se = sem("se")
        s_sp = sem("sp")
        s_gld = sem("gld")
        s_cc = sem("cc")
        s_vt = sem("vt")
        s_ft = sem("ft")
        s_pb = sem("pbs")
        s_c2 = sem("c2")
        s_c3 = sem("c3")
        s_c4 = sem("c4")
        s_c5 = sem("c5")
        s_m = sem("m")
        s_xnr = sem("xnr")

        ident = sbuf("identt", [128, 128])
        xn_fm = sbuf("xn_fm", [128, 16, TOKH])
        B_fm = sbuf("B_fm", [64, TOK])
        C_fm = sbuf("C_fm", [64, TOK])
        dt_fm = sbuf("dt_fm", [64, TOK])
        la_fm = sbuf("la_fm", [64, TOK])
        cs_fm = sbuf("cs_fm", [64, TOK])
        csmask = sbuf("csmask_t", [64, TOK])
        acol = sbuf("acol", [64, 1])
        dtb = sbuf("dtb", [64, 1])
        convw = sbuf("convw_t", [128, 32, 4])
        convb = sbuf("convb_t", [128, 32])
        v_fm = sbuf("v_fm", [128, 32, 512])
        wband = sbuf("wband", [128, 2, 512])
        ustage = sbuf("ustage", [128, 2, D])
        lnm = sbuf("lnm", [128, 2])
        consts1 = sbuf("consts1", [128, 1])
        lntmp = sbuf("lntmp", [128, D])
        xtmp = sbuf("xtmp", [128, 2, 515])
        zstage = sbuf("zstage", [128, 4, 512])
        h_loc = ustage.ap()[0:64].rearrange("p a b -> p (a b)")[:, 0:DI]
        xnflat = xn_fm.ap()[0:64].rearrange("p a b -> p (a b)")
        xnflat128 = xn_fm.ap().rearrange("p a b -> p (a b)")
        vt_sb = xnflat[:, 0:DI]
        m_sb = xnflat[:, DI:2 * DI]
        b_tok = sbuf("b_tok", [64, 64])
        e2s = sbuf("e2s", [64, NCH, 64])
        tmp2_fm = sbuf("tmp2_fm", [64, 64])
        cs_tok = sbuf("cs_tok", [64, NCH, 64])
        ecs_tok = sbuf("ecs_tok", [64, NCH, 64])
        drep = sbuf("drep", [64, 64])
        tcol = sbuf("tcol", [64, 1])

        pb = [psum(f"pb{i}") for i in range(6)]
        ptr = [psum("pt0"), psum("pt1")]

        k.dma("sync", ident[:], ident_in, inc=s_const)
        k.dma("sync", csmask[:], csmask_in, inc=s_const)
        k.dma("sync", acol[:], a_log, inc=s_const)
        k.dma("sync", dtb[:], dt_bias, inc=s_const)
        k.dma("sync", convw[:], conv_w, inc=s_const)
        k.dma("sync", convb[:], conv_b, inc=s_const)
        k.op("vector", lambda h: h.memset(consts1[:], 1.0),
             writes=(consts1[:],))
        k.vdrain(s_ve)
        k.act(acol[:], acol[:], AF.Exp, wait=[s_const], inc=s_se)

        umarks = []
        semarks = []
        pmarks = {}
        tposed = 0
        for t in range(9):
            rows = 128 if t < 8 else 3
            r0 = t * 128
            ub = ustage[:, t % 2]
            w = None
            if t >= 2:
                w = [(s_mm, umarks[t - 2]), (s_se, semarks[t - 2])]
            k.dma("sync", ub[0:rows, :], u[r0:r0 + rows, :], inc=s_u[t % 2],
                  wait=w)
            k.v("reduce_sum", lnm[0:rows, 0:1], ub[0:rows, :], axis=AX.X,
                wait=[s_u[t % 2]])
            k.v("tensor_scalar_mul", lnm[0:rows, 0:1], lnm[0:rows, 0:1], 1.0 / D)
            k.v("tensor_single_scalar", ub[0:rows, :], ub[0:rows, :],
                lnm[0:rows, 0:1], ALU.subtract)
            k.vdrain(s_ve)
            k.act(lntmp[0:rows, :], ub[0:rows, :], AF.Square,
                  accum_out=lnm[0:rows, 1:2], wait=[s_ve], inc=s_se)
            k.v("tensor_scalar", lnm[0:rows, 1:2], lnm[0:rows, 1:2], 1.0 / D,
                1e-5, ALU.mult, op1=ALU.add, wait=[s_se])
            k.vdrain(s_ve)
            k.act(lnm[0:rows, 1:2], lnm[0:rows, 1:2], AF.Sqrt, wait=[s_ve],
                  inc=s_se)
            k.v("reciprocal", lnm[0:rows, 1:2], lnm[0:rows, 1:2], wait=[s_se])
            k.v("tensor_single_scalar", ub[0:rows, :], ub[0:rows, :],
                lnm[0:rows, 1:2], ALU.mult)
            k.vdrain(s_ve)
            for kd in range(16):
                pslot = ptr[tposed % 2]
                tw = [s_ve] if kd == 0 else []
                if tposed >= 2:
                    tw = tw + [(s_se, pmarks[tposed - 2])]
                k.trans(pslot[:, 0:rows], ub[0:rows, kd * 128:(kd + 1) * 128],
                        ident[0:rows, 0:rows], wait=tw or None, inc=s_mm)
                k.act(xn_fm[:, kd, r0:r0 + rows], pslot[:, 0:rows], AF.Copy,
                      wait=[s_mm], inc=s_se)
                pmarks[tposed] = s_se.n
                tposed += 1
            umarks.append(s_mm.n)
            semarks.append(s_se.n)
        xn_done = (s_se, s_se.n)
        for kd in range(16):
            k.dma("sync", xn_sp[kd], xn_fm[:, kd, :], inc=s_sp, wait=[xn_done])
        xnsp_mark = (s_sp, s_sp.n)

        zmov = [(3, 0), (515, 3)]
        zslot_marks = []
        bank_free = [(s_se, s_se.n), (s_ve, s_ve.n)]
        wmarks = []

        def proj_pass(tiles, evac, xn_cols=zmov, ncols=512, bankset=None):
            nonlocal bank_free, wmarks
            nt = len(tiles)
            banks = bankset if bankset is not None else pb
            for kd in range(16):
                c0 = tiles[0][0]
                c1 = tiles[-1][0] + tiles[-1][1]
                slot = len(wmarks) % 2
                wb = wband[:, slot, 0:c1 - c0]
                lw = None
                if len(wmarks) >= 2:
                    lw = [(s_mm, wmarks[-2])]
                k.dma("sync", wb, w_in[kd * 128:(kd + 1) * 128, c0:c1],
                      inc=s_wb[slot], wait=lw)
                for mi, (wc0, wm) in enumerate(tiles):
                    lhsT = wb[:, wc0 - c0:wc0 - c0 + wm]
                    for (x0, pbase) in xn_cols:
                        mmw = []
                        if mi == 0 and (x0, pbase) == xn_cols[0]:
                            mmw.append((s_wb[slot], s_wb[slot].n))
                            if kd == 0:
                                mmw += bank_free
                        k.mm(banks[pbase + mi][0:wm, 0:ncols], lhsT,
                             xn_fm[:, kd, x0:x0 + ncols],
                             start=(kd == 0), stop=(kd == 15),
                             wait=mmw or None,
                             inc=s_mm if (mi == nt - 1 and (x0, pbase) == xn_cols[-1]) else None)
                wmarks.append(s_mm.n)
            evac()

        zsets = [[pb[0], pb[1], pb[2], pb[3]], [pb[4], pb[5], ptr[0], ptr[1]]]
        bfree2 = {0: bank_free, 1: bank_free}
        zmov2 = [(3, 0), (515, 2)]
        for gi, g0 in enumerate(range(0, 32, 2)):
            gt = [(m * 128, 128) for m in range(g0, min(g0 + 2, 32))]
            bs = zsets[gi % 2]

            def z_evac(g0=g0, gt=gt, bs=bs, gi=gi):
                nonlocal bank_free
                for mi in range(len(gt)):
                    m = g0 + mi
                    for hi in range(2):
                        slot = len(zslot_marks) % 4
                        zw = [(s_mm, s_mm.n)]
                        if len(zslot_marks) >= 4:
                            zw.append((s_zs[slot], zslot_marks[-4]))
                        k.act(zstage[:, slot], bs[hi * 2 + mi][:, 0:512],
                              AF.Copy, wait=zw, inc=s_se)
                        k.dma("sync", z_sp[m, :, hi * 512:hi * 512 + 512],
                              zstage[:, slot], inc=s_zs[slot], wait=[s_se])
                        zslot_marks.append(s_zs[slot].n)
                bfree2[gi % 2] = [(s_se, s_se.n)]
                bank_free = [(s_se, s_se.n)]

            bank_free = bfree2[gi % 2]
            proj_pass(gt, z_evac, xn_cols=zmov2, bankset=bs)
        bank_free = [(s_se, s_se.n)]

        def bcdt_evac():
            nonlocal bank_free
            for mi, (nm, dest) in enumerate((("B", B_fm), ("C", C_fm),
                                             ("dt", dt_fm))):
                for hi in range(2):
                    src = pb[hi * 3 + mi][0:64, 0:512]
                    dc = hi * 512
                    if nm in ("B", "C"):
                        k.v("tensor_relu", dest[:, dc:dc + 512], src,
                            wait=[(s_mm, s_mm.n)])
                        k.v("tensor_sub", lntmp[0:64, 0:512], src,
                            dest[:, dc:dc + 512])
                        k.vdrain(s_ve)
                        k.act(lntmp[0:64, 512:1024], lntmp[0:64, 0:512], AF.Exp,
                              wait=[s_ve], inc=s_se)
                        k.v("tensor_add", dest[:, dc:dc + 512],
                            dest[:, dc:dc + 512], lntmp[0:64, 512:1024],
                            wait=[s_se])
                        k.vdrain(s_ve)
                    else:
                        t0 = lntmp[0:64, 0:512]
                        t1 = lntmp[0:64, 512:1024]
                        k.v("tensor_scalar_add", t0, src, dtb[:, 0:1],
                            wait=[(s_mm, s_mm.n)])
                        k.v("tensor_scalar_mul", t1, t0, -1.0)
                        k.v("tensor_tensor", t1, t1, t0, ALU.min)
                        k.vdrain(s_ve)
                        k.act(t1, t1, AF.Exp, wait=[s_ve])
                        k.op("scalar", lambda h: h.drain())
                        k.act(t1, t1, AF.Ln, bias=1.0 if False else consts1[0:64, 0:1],
                              inc=s_se)
                        k.v("tensor_relu", dest[:, dc:dc + 512], t0, wait=[s_se])
                        k.v("tensor_add", dest[:, dc:dc + 512],
                            dest[:, dc:dc + 512], t1)
                        k.vdrain(s_ve)
            bank_free = [(s_se, s_se.n), (s_ve, s_ve.n)]
        proj_pass([(8192, 64), (8256, 64), (8320, 64)], bcdt_evac)

        k.v("tensor_scalar", la_fm[:], dt_fm[:], acol[:, 0:1], -1.0,
            ALU.mult, op1=ALU.mult, wait=[s_se, s_ve])
        k.op("vector", lambda h: h.tensor_tensor_scan(
            cs_fm[:], csmask[:], la_fm[:], 0.0, ALU.mult, ALU.add),
            reads=(csmask[:], la_fm[:]), writes=(cs_fm[:],))
        k.vdrain(s_ve)
        if "bcdt" in debug:
            k.dma("sync", bcdt_sp[0], B_fm[:], inc=s_sp, wait=[s_ve])
            k.dma("sync", bcdt_sp[1], C_fm[:], inc=s_sp)
            k.dma("sync", bcdt_sp[2], dt_fm[:], inc=s_sp)
            k.dma("sync", bcdt_sp[3], cs_fm[:], inc=s_sp)

        xt_marks = {}
        ptm = {}
        mqm = {}
        mmvt = {}
        ttm = {}
        ttm2 = {}
        ptm2 = {}
        for half in range(2):
            xn0 = half * 512
            if half == 1:
                xw = [(s_ve, s_ve.n), (s_se, s_se.n), (s_mm, s_mm.n),
                      (s_vt, s_vt.n), (s_m, s_m.n)]
                for kd in range(16):
                    k.dma("sync", xn_fm[:, kd, :], xn_sp[kd], inc=s_xnr,
                          wait=xw if kd == 0 else None)
            vfree = None
            if half == 1 and "v" in debug:
                vfree = (s_sp, s_sp.n)
            xsets = [[pb[0], pb[1], pb[2], pb[3]],
                     [pb[4], pb[5], ptr[0], ptr[1]]]
            xfree = {0: list(bank_free), 1: list(bank_free)}
            for gi, g0 in enumerate(range(0, 32, 2)):
                gt = [(4096 + m * 128, 128) for m in range(g0, min(g0 + 2, 32))]
                nt = len(gt)
                bs = xsets[gi % 2]
                for kd in range(16):
                    c0 = gt[0][0]
                    c1 = gt[-1][0] + gt[-1][1]
                    slot = len(wmarks) % 2
                    wb = wband[:, slot, 0:c1 - c0]
                    lw = [(s_mm, wmarks[-2])] if len(wmarks) >= 2 else None
                    k.dma("sync", wb, w_in[kd * 128:(kd + 1) * 128, c0:c1],
                          inc=s_wb[slot], wait=lw)
                    for mi in range(nt):
                        wc0, wm = gt[mi]
                        lhsT = wb[:, wc0 - c0:wc0 - c0 + wm]
                        mmw = []
                        if mi == 0:
                            mmw.append((s_wb[slot], s_wb[slot].n))
                            if kd == 0:
                                mmw += xfree[gi % 2] + [(s_xnr, s_xnr.n)]
                        k.mm(bs[mi][:, 0:512], lhsT,
                             xn_fm[:, kd, xn0:xn0 + 512],
                             start=(kd == 0), stop=(kd == 15), wait=mmw or None)
                        k.mm(bs[2 + mi][:, 0:3], lhsT,
                             xn_fm[:, kd, xn0 + 512:xn0 + 515],
                             start=(kd == 0), stop=(kd == 15),
                             inc=s_mm if (mi == nt - 1) else None)
                    wmarks.append(s_mm.n)
                for mi in range(nt):
                    m = g0 + mi
                    xt = xtmp[:, m % 2]
                    xw = [(s_mm, s_mm.n)]
                    if (m - 2) in xt_marks:
                        xw.append((s_ve, xt_marks[m - 2]))
                    k.act(xt[:, 0:512], bs[mi][:, 0:512], AF.Copy, wait=xw)
                    k.act(xt[:, 512:515], bs[2 + mi][:, 0:3], AF.Copy, inc=s_se)
                    cw = convw[:, m]
                    vdst = v_fm[:, m, :]
                    vw = [s_se]
                    if vfree is not None:
                        vw.append(vfree)
                    k.v("tensor_scalar_mul", vdst, xt[:, 0:512], cw[:, 0:1],
                        wait=vw)
                    for kk in range(1, 4):
                        k.v("scalar_tensor_tensor", vdst, xt[:, kk:kk + 512],
                            cw[:, kk:kk + 1], vdst, ALU.mult, ALU.add)
                    k.vdrain(s_ve)
                    xt_marks[m] = s_ve.n
                    k.act(vdst, vdst, AF.Silu, bias=convb[:, m:m + 1],
                          wait=[s_ve], inc=s_se)
                xfree[gi % 2] = [(s_se, s_se.n)]
                bank_free = [(s_se, s_se.n)]
                if "v" in debug:
                    for mi in range(nt):
                        m = g0 + mi
                        k.dma("sync", vfm_sp[half, m], v_fm[:, m, :],
                              inc=s_sp, wait=[(s_se, s_se.n)])

            if half == 0:
                k.op("vector", lambda h: h.memset(h_loc, 0.0),
                     writes=(h_loc,), wait=[(s_mm, s_mm.n), (s_se, s_se.n)])
                k.vdrain(s_ve)
            vconsumed = []
            for c in range(8):
                gc = half * 8 + c
                csl = slice(gc * 64, gc * 64 + 64)
                for m in range(32):
                    pslot = ptr[m % 2]
                    tw = [(s_se, s_se.n)] if m < 2 else [(s_se, ptm[m - 2])]
                    if m == 0:
                        tw = tw + [(s_mm, mmvt.get("x", 0))]
                    ew_extra = [xnsp_mark] if (m == 0 and c == 0 and half == 0) else []
                    k.trans(pslot[0:64, 0:128], v_fm[:, m, c * 64:(c + 1) * 64],
                            ident[:], wait=tw, inc=s_mm)
                    ew = [s_mm] if m > 0 else [s_mm, (s_ve, s_ve.n)]
                    k.act(vt_sb[:, m * 128:(m + 1) * 128], pslot[0:64, 0:128],
                          AF.Copy, wait=ew + ew_extra, inc=s_se)
                    ptm[m] = s_se.n
                vt_done = (s_se, s_se.n)
                vconsumed.append(s_mm.n)
                k.dma("sync", vtok_sp[gc], vt_sb, inc=s_vt, wait=[vt_done])
                k.v("tensor_scalar", tmp2_fm[:], cs_fm[:, csl],
                    cs_fm[:, gc * 64 + 63:gc * 64 + 64], -1.0, ALU.subtract,
                    op1=ALU.mult, wait=[vt_done])
                k.vdrain(s_ve)
                k.trans(ptr[0][0:64, 0:64], tmp2_fm[:], ident[0:64, 0:64],
                        wait=[s_ve], inc=s_mm)
                k.trans(ptr[1][0:64, 0:64], B_fm[:, csl], ident[0:64, 0:64],
                        inc=s_mm)
                k.trans(ptr[0][0:64, 64:128], cs_fm[:, csl], ident[0:64, 0:64],
                        inc=s_mm)
                k.act(e2s[:, gc], ptr[0][0:64, 0:64], AF.Exp, wait=[s_mm])
                k.act(b_tok[:], ptr[1][0:64, 0:64], AF.Copy)
                k.act(cs_tok[:, gc], ptr[0][0:64, 64:128], AF.Copy)
                k.act(ecs_tok[:, gc], ptr[0][0:64, 64:128], AF.Exp, inc=s_se)
                ptm[0] = s_se.n
                ptm[1] = s_se.n
                k.v("tensor_mul", drep[:], e2s[:, gc], ecs_tok[:, gc],
                    wait=[s_se])
                k.v("tensor_tensor",
                    vt_sb.rearrange("t (h p) -> t h p", h=64),
                    vt_sb.rearrange("t (h p) -> t h p", h=64),
                    e2s[:, gc][:, :, None].broadcast_to([64, 64, 64]), ALU.mult,
                    wait=[s_vt])
                k.vdrain(s_ve)
                for q in range(8):
                    k.mm(pb[q % 6][0:64, 0:512], b_tok[:],
                         vt_sb[:, q * 512:(q + 1) * 512], start=True,
                         stop=True,
                         wait=[s_ve] if q == 0 else ([(s_se, mqm[q - 6])] if q >= 6 else None),
                         inc=s_mm)
                    macw = [s_mm]
                    if q == 0:
                        macw.append((s_m, s_m.n))
                    k.act(m_sb[:, q * 512:(q + 1) * 512], pb[q % 6][0:64, 0:512],
                          AF.Copy, wait=macw, inc=s_se)
                    mqm[q] = s_se.n
                mmvt["x"] = s_mm.n
                m_done = (s_se, s_se.n)
                k.dma("sync", m_sp[gc], m_sb, inc=s_m, wait=[m_done])
                k.v("tensor_tensor",
                    h_loc.rearrange("n (h p) -> n h p", h=64),
                    h_loc.rearrange("n (h p) -> n h p", h=64),
                    drep[:, :, None].broadcast_to([64, 64, 64]), ALU.mult,
                    wait=[m_done])
                k.v("tensor_add", h_loc, h_loc, m_sb[:])
                k.vdrain(s_ve)
                mspill_scan = [(s_m, s_m.n), (s_ve, s_ve.n)]
            bank_free = bank_free + [(s_se, s_se.n), (s_ve, s_ve.n)]

        k.v("reduce_sum", tcol[:], la_fm[:], axis=AX.X, wait=[(s_se, s_se.n)])
        k.vdrain(s_ve)
        k.act(tcol[:], tcol[:], AF.Exp, wait=[s_ve], inc=s_se)
        k.dma("sync", f_bnc, h_loc, inc=s_ft, wait=[(s_ve, s_ve.n)])
        k.dma("sync", t_bnc, tcol[:], inc=s_ft, wait=[(s_se, s_se.n)])
        if "ft" in debug:
            k.dma("sync", ft_sp[0:64, :], h_loc, inc=s_sp,
                  wait=[(s_ve, s_ve.n)])
            k.dma("sync", ft_sp[64:65, 0:64], tcol[:, 0:1].rearrange("a b -> b a") if False else tcol[0:64, 0:1],
                  inc=s_sp, wait=[(s_se, s_se.n)])
        cc_ready = (s_ft, s_ft.n)
        k.op("gpsimd", lambda h: h.collective_compute(
            "AllGather", ALU.bypass, replica_groups=[list(range(NCORES))],
            ins=[f_bnc.opt()], outs=[f_gat.opt()]), wait=[cc_ready], inc=s_cc)
        k.op("gpsimd", lambda h: h.collective_compute(
            "AllGather", ALU.bypass, replica_groups=[list(range(NCORES))],
            ins=[t_bnc.opt()], outs=[t_gat.opt()]), wait=None, inc=s_cc)

        wrt_sb = sbuf("wrt_sb", [128, 16 * E])
        mg_sb = sbuf("mg_sb", [128, 8, E])
        lgt = sbuf("lgt", [128, E + 4])
        p_part = zstage.ap()[0:8].rearrange("p a b -> p (a b)")[:, 0:D]
        k.dma("sync", wrt_sb[:], wrt, inc=s_c2)
        xw = [(s_ve, s_ve.n), (s_se, s_se.n), (s_mm, s_mm.n),
              (s_vt, s_vt.n), (s_m, s_m.n)]
        for kd in range(16):
            k.dma("sync", xn_fm[:, kd, :], xn_sp[kd], inc=s_xnr,
                  wait=xw if kd == 0 else None)
        for tt in range(8):
            tc = 3 + tt * 128
            for kd in range(16):
                k.mm(pb[0][0:128, 0:E], xn_fm[:, kd, tc:tc + 128],
                     wrt_sb[:, kd * E:(kd + 1) * E],
                     start=(kd == 0), stop=(kd == 15),
                     wait=([(s_xnr, s_xnr.n), (s_c2, s_c2.n)] + bank_free +
                           [(s_se, s_se.n)]) if (tt == 0 and kd == 0) else
                          ([(s_se, ttm[tt - 1])] if kd == 0 and tt > 0 else None),
                     inc=s_mm if kd == 15 else None)
            lcw = [s_mm] + ([(s_ve, ttm2[tt - 1])] if tt > 0 else [])
            k.act(lgt[:, 0:E], pb[0][:, 0:E], AF.Copy, wait=lcw, inc=s_se)
            k.v("reduce_max", lgt[:, E:E + 1], lgt[:, 0:E], axis=AX.X,
                wait=[s_se])
            k.v("tensor_single_scalar", mg_sb[:, tt], lgt[:, 0:E],
                lgt[:, E:E + 1], ALU.is_equal)
            k.vdrain(s_ve)
            k.v("scalar_tensor_tensor", lgt[:, 0:E], mg_sb[:, tt], -1e30,
                lgt[:, 0:E], ALU.mult, ALU.add)
            k.v("reduce_max", lgt[:, E + 1:E + 2], lgt[:, 0:E], axis=AX.X)
            k.vdrain(s_ve)
            k.v("tensor_single_scalar", lgt[:, 0:E], lgt[:, 0:E],
                lgt[:, E + 1:E + 2], ALU.is_equal)
            k.v("tensor_sub", lgt[:, E + 2:E + 3], lgt[:, E:E + 1],
                lgt[:, E + 1:E + 2])
            k.vdrain(s_ve)
            k.act(lgt[:, E + 2:E + 3], lgt[:, E + 2:E + 3], AF.Sigmoid,
                  wait=[s_ve], inc=s_se)
            k.v("tensor_scalar", lgt[:, E + 3:E + 4], lgt[:, E + 2:E + 3],
                -1.0, 1.0, ALU.mult, op1=ALU.add, wait=[s_se])
            k.vdrain(s_ve)
            k.v("tensor_scalar", mg_sb[:, tt], mg_sb[:, tt],
                lgt[:, E + 2:E + 3], None, ALU.mult, wait=[s_ve])
            k.v("scalar_tensor_tensor", mg_sb[:, tt], lgt[:, 0:E],
                lgt[:, E + 3:E + 4], mg_sb[:, tt], ALU.mult, ALU.add)
            k.vdrain(s_ve)
            ttm[tt] = s_se.n
            ttm2[tt] = s_ve.n
        mg_done = (s_ve, s_ve.n)
        if "p" in debug:
            k.dma("sync", mg_sp,
                  mg_sb.ap().rearrange("p a b -> p (a b)"), inc=s_sp,
                  wait=[mg_done])

        h1v = v_fm.ap().rearrange("p a b -> p (a b)")
        h2v = xnflat128
        def ffn_fm(wsrc, mov, dest_fn, func, first_wait):
            for g0 in range(0, 16, 3):
                gt = list(range(g0, min(g0 + 3, 16)))
                nt = len(gt)
                for kd in range(16):
                    slot = len(wmarks) % 2
                    wb = wband[:, slot, 0:nt * 128]
                    lw = [(s_mm, wmarks[-2])] if len(wmarks) >= 2 else None
                    k.dma("sync", wb,
                          wsrc[kd * 128:(kd + 1) * 128,
                               g0 * 128:g0 * 128 + nt * 128],
                          inc=s_wb[slot], wait=lw)
                    for mi in range(nt):
                        lhsT = wb[:, mi * 128:(mi + 1) * 128]
                        for hi in range(2):
                            mmw = []
                            if mi == 0 and hi == 0:
                                mmw.append((s_wb[slot], s_wb[slot].n))
                                if kd == 0:
                                    mmw += bank_free2[0]
                                    if g0 == 0:
                                        mmw += first_wait
                            k.mm(pb[hi * 3 + mi][:, 0:512], lhsT,
                                 mov(kd, hi), start=(kd == 0), stop=(kd == 15),
                                 wait=mmw or None,
                                 inc=s_mm if (mi == nt - 1 and hi == 1) else None)
                    wmarks.append(s_mm.n)
                for mi in range(nt):
                    m = g0 + mi
                    for hi in range(2):
                        k.act(dest_fn(m, hi), pb[hi * 3 + mi][:, 0:512], func,
                              wait=[(s_mm, s_mm.n)], inc=s_se)
                bank_free2[0] = [(s_se, s_se.n)]

        bank_free2 = [bank_free + [mg_done, (s_se, s_se.n)]]
        ffn_fm(w1t, lambda kd, hi: xn_fm[:, kd, 3 + hi * 512:3 + hi * 512 + 512],
               lambda m, hi: h1v[:, m * 1024 + hi * 512:m * 1024 + hi * 512 + 512],
               AF.Silu, [])
        ffn_fm(w3t, lambda kd, hi: h1v[:, kd * 1024 + hi * 512:kd * 1024 + hi * 512 + 512],
               lambda m, hi: h2v[:, m * 1024 + hi * 512:m * 1024 + hi * 512 + 512],
               AF.Silu, [(s_mm, s_mm.n)])
        if "h2" in debug:
            for m in range(16):
                k.dma("sync", h2_sp[m],
                      h2v[:, m * 1024:(m + 1) * 1024], inc=s_sp,
                      wait=[(s_se, s_se.n)])
        ffn_fm(w2t, lambda kd, hi: h2v[:, kd * 1024 + hi * 512:kd * 1024 + hi * 512 + 512],
               lambda m, hi: h1v[:, m * 1024 + hi * 512:m * 1024 + hi * 512 + 512],
               AF.Copy, [(s_mm, s_mm.n)])
        htok = lntmp.ap()
        hw0 = [(s_se, s_se.n), (s_ve, s_ve.n)]
        for tt in range(8):
            for dtile in range(16):
                pslot = ptr[dtile % 2]
                tw = ([(s_se, ptm2[(tt, dtile - 2)])] if dtile >= 2
                      else [(s_se, ptm2.get((tt - 1, 14 + dtile), s_se.n))])
                if tt == 0 and dtile < 2:
                    tw = [(s_se, s_se.n)]
                if tt == 0 and dtile == 0:
                    tw += hw0
                k.trans(pslot[:, 0:128],
                        h1v[:, dtile * 1024 + tt * 128:dtile * 1024 + tt * 128 + 128],
                        ident[:], wait=tw, inc=s_mm)
                k.act(htok[:, dtile * 128:(dtile + 1) * 128], pslot[:, 0:128],
                      AF.Copy, wait=[s_mm], inc=s_se)
                ptm2[(tt, dtile)] = s_se.n
            for q in range(4):
                k.mm(pb[q][0:8, 0:512], mg_sb[:, tt],
                     htok[:, q * 512:(q + 1) * 512],
                     start=(tt == 0), stop=(tt == 7),
                     wait=[(s_se, s_se.n)] if q == 0 else None,
                     inc=s_mm if q == 3 else None)
        for q in range(4):
            k.act(p_part[:, q * 512:(q + 1) * 512], pb[q][0:8, 0:512], AF.Copy,
                  wait=[s_mm] + ([(sz, sz.n) for sz in s_zs] if q == 0 else []),
                  inc=s_se)
        if "p" in debug:
            k.dma("sync", p_sp, p_part, inc=s_sp, wait=[(s_se, s_se.n)])

        k.dma("sync", p_bnc, p_part, inc=s_pb, wait=[(s_se, s_se.n)])
        pcc_ready = (s_pb, s_pb.n)
        k.op("gpsimd", lambda h: h.collective_compute(
            "AllReduce", ALU.add, replica_groups=[list(range(NCORES))],
            ins=[p_bnc.opt()], outs=[p_tot.opt()]), wait=[pcc_ready], inc=s_cc)

        hbuf = h_loc
        SPc = 3 * DI + 2048
        trep = xnflat[:, SPc + 768:SPc + 832]
        tm_t = xnflat[:, SPc + 832:SPc + 896]
        cmask = mg_sb.ap()[0:64].rearrange("p a b -> p (a b)")[:, 32:40]
        cmaski = mg_sb.ap()[0:64].rearrange("p a b -> p (a b)")[:, 40:48]
        fbufs = [xnflat[:, 0:DI], xnflat[:, DI:2 * DI]]
        s_f = [sem("f0"), sem("f1")]
        cmw = [(s_mm, s_mm.n), (s_se, s_se.n), (s_ve, s_ve.n)]
        k.dma("sync", cmask, combmask, inc=s_c3, wait=cmw)
        k.dma("sync", cmaski, combmaskinv, inc=s_c3)
        k.op("vector", lambda h: h.memset(hbuf, 0.0), writes=(hbuf,),
             wait=[(s_mm, s_mm.n), (s_se, s_se.n), (s_sp, s_sp.n),
                   (s_ft, s_ft.n)])
        k.vdrain(s_ve)
        fjm = {}
        for j in range(NCORES - 1):
            fb = fbufs[j % 2]
            fw = [(s_cc, 2), (s_mm, s_mm.n), (s_se, s_se.n)]
            if j >= 2:
                fw.append((s_ve, fjm[j - 2]))
            k.dma("sync", fb, f_gat[j], inc=s_f[j % 2], wait=fw)
            k.dma("sync", trep,
                  t_gat[j].rearrange("h one -> one h").broadcast_to([64, 64]),
                  inc=s_f[j % 2],
                  wait=[(s_ve, fjm[j - 1])] if j >= 1 else None)
            k.v("tensor_scalar", tm_t, trep, cmask[:, j:j + 1],
                cmaski[:, j:j + 1], ALU.mult, op1=ALU.add,
                wait=[(s_f[j % 2], s_f[j % 2].n), (s_c3, s_c3.n)])
            k.vdrain(s_ve)
            k.v("tensor_tensor", hbuf.rearrange("n (h p) -> n h p", h=64),
                hbuf.rearrange("n (h p) -> n h p", h=64),
                tm_t[:, :, None].broadcast_to([64, 64, 64]), ALU.mult)
            k.v("scalar_tensor_tensor", hbuf, fb, cmask[:, j:j + 1], hbuf,
                ALU.mult, ALU.add)
            k.vdrain(s_ve)
            fjm[j] = s_ve.n
        hin_done = (s_ve, s_ve.n)

        vtk2 = xnflat128[:, 0:DI]
        mrgY = xnflat128[:, DI:2 * DI]
        mrgA = xnflat[:, DI:2 * DI]
        att2 = xnflat128[:, 2 * DI:3 * DI]
        zb2 = xnflat128[:, 3 * DI:4 * DI]
        wob = [xnflat128[:, 0:2048], xnflat128[:, 2048:4096]]
        lnf = lntmp.ap()
        tmpq = lnf[:, 0:512]
        masklv = lnf[:, 512:576]
        gm2 = lnf[:, 576:640]
        drepA = lnf[0:64, 640:704]
        drepB = lnf[0:64, 1088:1152]
        cstk2 = lnf[:, 704:768]
        ecs2 = lnf[:, 768:832]
        drpDv = lnf[:, 896:960]
        nsc = mg_sb.ap().rearrange("p a b -> p (a b)")[:, 0:32]
        rcol = mg_sb.ap().rearrange("p a b -> p (a b)")[:, 48:50]
        smw = [(s_mm, s_mm.n), (s_se, s_se.n), (s_ve, s_ve.n)]
        k.dma("sync", nsc, norm_scale, inc=s_c4, wait=smw)
        s_vl = sem("vl")
        s_ml = sem("ml")
        s_zl = sem("zl")
        s_cb = [sem("cb0"), sem("cb1")]
        s_cbr = sem("cbr")
        s_dr = sem("dr")
        s_ur = sem("ur")
        s_ot = sem("ot")
        marks = {"amm": {}, "yev": {}, "scan": {}, "tp": {}, "gy": {},
                 "zl": {}, "sq": {}, "sm": {}, "ml": {}}
        gy_fm = v_fm
        for half in range(2):
            mlw = [(s_ve, s_ve.n), (s_se, s_se.n)] if half == 0 else \
                  [(s_ot, s_ot.n), (s_ve, s_ve.n)]
            k.dma("sync", masklv, maskl_in, inc=s_c5, wait=mlw)
            k.dma("sync", drpDv, d_rep, inc=s_c5)
            mklm = (s_c5, s_c5.n)
            for p in range(4):
                ga = half * 8 + 2 * p
                gb = ga + 1
                sla = slice(ga * 64, ga * 64 + 64)
                slb = slice(gb * 64, gb * 64 + 64)
                slab = slice(ga * 64, ga * 64 + 128)
                pp = half * 4 + p
                if pp == 0:
                    vw = [hin_done, (s_mm, s_mm.n), (s_cb[0], s_cb[0].n)]
                elif p == 0:
                    vw = [(s_mm, s_mm.n), (s_ve, s_ve.n)]
                else:
                    vw = [(s_mm, marks["amm"][pp - 1]),
                          (s_ve, marks["yev"][pp - 1]),
                          (s_se, marks["sq"].get(pp - 1, 0))]
                k.dma("sync", vtk2,
                      vtok_sp.rearrange("c s d -> (c s) d")[ga * 64:ga * 64 + 128, :],
                      inc=s_vl, wait=vw)
                mw = [hin_done] if pp == 0 else \
                     [(s_ve, marks["scan"][pp - 1]), (s_mm, marks["tp"].get(pp - 1, 0)),
                      (s_se, marks["sq"].get(pp - 1, 0))]
                k.dma("sync", mrgA, m_sp[ga], inc=s_ml, wait=mw)
                zw = [(s_se, marks["zl"][pp - 1]),
                      (s_ve, marks["gy"][(pp - 1, 31)])] if pp > 0 else \
                     [(s_mm, s_mm.n), hin_done]
                k.dma("gpsimd", zb2.rearrange("p (m t) -> p m t", m=32),
                      z_sp[:, :, ga * 64:ga * 64 + 128].rearrange("m p t -> p m t"),
                      inc=s_zl, wait=zw)
                cw = [(s_cb[p % 2], s_cb[p % 2].n)]
                k.dma("gpsimd", csrow[p % 2, 0], cs_fm[:, sla],
                      inc=s_cb[p % 2], wait=cw)
                k.dma("gpsimd", csrow[p % 2, 1], cs_fm[:, slb],
                      inc=s_cb[p % 2])
                k.dma("gpsimd", totrow[p % 2, 0],
                      cs_fm[:, ga * 64 + 63:ga * 64 + 64], inc=s_cb[p % 2])
                k.dma("gpsimd", totrow[p % 2, 1],
                      cs_fm[:, gb * 64 + 63:gb * 64 + 64], inc=s_cb[p % 2])
                rw = [(s_cb[p % 2], s_cb[p % 2].n)]
                if pp > 0:
                    rw.append((s_mm, marks["amm"][pp - 1]))
                    rw.append((s_ve, marks["yev"][pp - 1]))
                k.dma("gpsimd", att2[0:64, :],
                      csrow[p % 2, 0:1].rearrange('a b -> (a b)')[None, :].broadcast_to([64, NH * S]) if False else csrow[p % 2, 0:1, :].rearrange('a f -> a f').broadcast_to([64, NH * S]),
                      inc=s_cbr, wait=rw)
                k.dma("gpsimd", att2[64:128, :],
                      csrow[p % 2, 1:2, :].broadcast_to([64, NH * S]), inc=s_cbr)
                csb_ld = (s_cbr, s_cbr.n)
                sw = [(s_ve, s_ve.n)]
                daw = [(s_cb[p % 2], s_cb[p % 2].n),
                       (s_ve, marks["scan"].get(pp - 1, 0))]
                if p == 0:
                    daw += [(s_ot, s_ot.n), (s_ve, s_ve.n), (s_ur, s_ur.n)]
                k.dma("gpsimd", drepA,
                      totrow[p % 2, 0:1, :].broadcast_to([64, NH]),
                      inc=s_dr, wait=daw)
                k.dma("gpsimd", drepB,
                      totrow[p % 2, 1:2, :].broadcast_to([64, NH]), inc=s_dr)
                tpw = sw + [(s_se, marks["sm"].get(pp - 1, 0))]
                k.trans(ptr[1][:, 0:64], cs_fm[:, slab], ident[0:64, 0:64],
                        wait=tpw, inc=s_mm)
                k.mm(pb[0][0:64, 0:64], B_fm[:, sla], C_fm[:, sla],
                     start=True, stop=True)
                k.mm(pb[2][64:128, 0:64], B_fm[:, slb], C_fm[:, slb],
                     start=True, stop=True, inc=s_mm)
                k.act(drepA, drepA, AF.Exp,
                      wait=[s_mm, (s_dr, s_dr.n), (s_ve, s_ve.n)])
                k.act(drepB, drepB, AF.Exp)
                k.act(cstk2[:], ptr[1][:, 0:64], AF.Copy)
                k.act(ecs2[:], ptr[1][:, 0:64], AF.Exp)
                k.act(gm2[0:64, :], pb[0][0:64, 0:64], AF.Copy)
                k.act(gm2[64:128, :], pb[2][64:128, 0:64], AF.Copy, inc=s_se)
                marks["sm"][pp] = s_se.n
                a3 = att2.rearrange("t (h s) -> t h s", h=NH)
                k.v("tensor_tensor", a3, a3,
                    masklv[:, None, :].broadcast_to([128, NH, S]), ALU.add,
                    wait=[csb_ld, mklm])
                k.v("tensor_tensor", a3, a3,
                    cstk2[:, :, None].broadcast_to([128, NH, S]), ALU.subtract,
                    wait=[(s_se, marks["sm"][pp])], inc=s_ve)
                k.act(att2, att2, AF.Exp, wait=[s_ve], inc=s_se)
                k.v("tensor_tensor", a3, a3,
                    gm2[:, None, :].broadcast_to([128, NH, S]), ALU.mult,
                    wait=[s_se], inc=s_ve)
                att_done = (s_ve, s_ve.n)
                if "att" in debug and pp == 0:
                    k.dma("sync", att_sp, att2, inc=s_sp, wait=[att_done])
                    k.dma("sync", vt2_sp, vtk2, inc=s_sp,
                          wait=[(s_vl, s_vl.n)])
                    k.dma("sync", hin_sp, hbuf, inc=s_sp, wait=[hin_done])
                    k.dma("sync", sml_sp, lnf[:, 0:1152], inc=s_sp,
                          wait=[(s_se, s_se.n), (s_ve, s_ve.n)])
                for q in range(4):
                    qs = q * 1024
                    for bh in range(2):
                        qq = qs + bh * 512
                        w0 = None
                        if q == 0 and bh == 0:
                            w0 = [att_done, (s_vl, s_vl.n), (s_ml, s_ml.n),
                                  (s_se, marks["sm"][pp])]
                            if pp > 0:
                                w0 = w0 + [(s_ve, marks["yev"][pp - 1])]
                        elif bh == 0 and q > 0:
                            w0 = [(s_ve, marks["yev"][(pp, q - 1)])]
                        k.mm(pb[bh][0:64, 0:512], C_fm[:, sla],
                             hbuf[:, qq:qq + 512], start=True, stop=True,
                             wait=w0, inc=s_mm if bh == 1 else None)
                    interE = s_mm.n
                    for bh in range(2):
                        for hh in range(8):
                            hid = q * 16 + bh * 8 + hh
                            k.mm(pb[4 + bh][0:64, hh * 64:hh * 64 + 64],
                                 att2[0:64, hid * 64:hid * 64 + 64],
                                 vtk2[0:64, hid * 64:hid * 64 + 64],
                                 start=(hh == 0), stop=(hh == 7))
                            k.mm(ptr[bh][64:128, hh * 64:hh * 64 + 64],
                                 att2[64:128, hid * 64:hid * 64 + 64],
                                 vtk2[64:128, hid * 64:hid * 64 + 64],
                                 start=(hh == 0), stop=(hh == 7),
                                 inc=s_mm if hh == 7 else None)
                    intraM = s_mm.n
                    k.v("tensor_tensor",
                        hbuf[:, qs:qs + 1024].rearrange("n (h p) -> n h p", h=16),
                        hbuf[:, qs:qs + 1024].rearrange("n (h p) -> n h p", h=16),
                        drepA[:, q * 16:q * 16 + 16][:, :, None]
                        .broadcast_to([64, 16, 64]), ALU.mult,
                        wait=[(s_mm, interE), (s_ml, s_ml.n),
                              (s_se, marks["sm"][pp])])
                    k.v("tensor_add", hbuf[:, qs:qs + 1024],
                        hbuf[:, qs:qs + 1024], mrgA[:, qs:qs + 1024],
                        inc=s_ve)
                    scanE = s_ve.n
                    k.dma("sync", mrgA[:, qs:qs + 1024],
                          m_sp[gb, :, qs:qs + 1024], inc=s_ml,
                          wait=[(s_ve, scanE)])
                    for bh in range(2):
                        qq = qs + bh * 512
                        k.mm(pb[2 + bh][64:128, 0:512], C_fm[:, slb],
                             hbuf[:, qq:qq + 512], start=True, stop=True,
                             wait=[(s_ve, scanE)] if bh == 0 else None,
                             inc=s_mm if bh == 1 else None)
                    k.v("tensor_tensor",
                        hbuf[:, qs:qs + 1024].rearrange("n (h p) -> n h p", h=16),
                        hbuf[:, qs:qs + 1024].rearrange("n (h p) -> n h p", h=16),
                        drepB[:, q * 16:q * 16 + 16][:, :, None]
                        .broadcast_to([64, 16, 64]), ALU.mult,
                        wait=[(s_mm, s_mm.n), (s_ml, s_ml.n)])
                    k.v("tensor_add", hbuf[:, qs:qs + 1024],
                        hbuf[:, qs:qs + 1024], mrgA[:, qs:qs + 1024],
                        inc=s_ve)
                    marks["scan"][pp] = s_ve.n
                    for bh in range(2):
                        qq = qs + bh * 512
                        hsl = slice(q * 16 + bh * 8, q * 16 + bh * 8 + 8)
                        for half_rows, pi, pa in ((slice(0, 64), pb[bh], pb[4 + bh]),
                                                  (slice(64, 128), pb[2 + bh], ptr[bh])):
                            dst = mrgY[half_rows, qq:qq + 512]
                            tq = tmpq[half_rows, :]
                            k.v("tensor_tensor",
                                dst.rearrange("t (h p) -> t h p", h=8),
                                pi[half_rows, 0:512].rearrange("t (h p) -> t h p", h=8),
                                ecs2[half_rows, hsl][:, :, None]
                                .broadcast_to([64, 8, 64]), ALU.mult,
                                wait=[(s_mm, s_mm.n), (s_ve, marks["scan"][pp]),
                                      (s_c5, s_c5.n)])
                            k.v("tensor_add", dst, dst, pa[half_rows, 0:512],
                                inc=s_ve)
                    marks["yev"][(pp, q)] = s_ve.n
                k.v("tensor_tensor",
                    att2.rearrange("t (h p) -> t h p", h=64),
                    vtk2.rearrange("t (h p) -> t h p", h=64),
                    drpDv[:, :, None].broadcast_to([128, 64, 64]), ALU.mult,
                    wait=[(s_mm, s_mm.n), (s_ve, marks["yev"][(pp, 3)]),
                          (s_c5, s_c5.n)])
                k.v("tensor_add", mrgY, mrgY, att2, inc=s_ve)
                marks["amm"][pp] = s_mm.n
                marks["yev"][pp] = s_ve.n
                marks["ml"][pp] = s_ml.n
                if "y" in debug:
                    k.dma("sync", y_sp[ga], mrgY[0:64, :], inc=s_sp,
                          wait=[(s_ve, s_ve.n)])
                    k.dma("sync", y_sp[gb], mrgY[64:128, :], inc=s_sp)
                k.act(vtk2, mrgY, AF.Square, accum_out=rcol[:, 0:1],
                      wait=[(s_ve, marks["yev"][pp]), (s_se, s_se.n),
                            (s_cbr, s_cbr.n)], inc=s_se)
                marks["sq"][pp] = s_se.n
                k.v("tensor_scalar", rcol[:, 0:1], rcol[:, 0:1], 1.0 / DI,
                    1e-6, ALU.mult, op1=ALU.add, wait=[s_se], inc=s_ve)
                k.act(rcol[:, 0:1], rcol[:, 0:1], AF.Sqrt, wait=[s_ve],
                      inc=s_se)
                k.v("reciprocal", rcol[:, 0:1], rcol[:, 0:1], wait=[s_se])
                k.v("tensor_scalar", mrgY, mrgY, rcol[:, 0:1], None, ALU.mult,
                    inc=s_ve)
                yrms = (s_ve, s_ve.n)
                k.act(zb2, zb2, AF.Silu, wait=[(s_zl, s_zl.n)], inc=s_se)
                marks["zl"][pp] = s_se.n
                for m in range(32):
                    pslot = pb[m % 4]
                    tw = [yrms, (s_se, marks["zl"][pp])]
                    if (pp, m - 4) in marks["gy"]:
                        tw.append((s_ve, marks["gy"][(pp, m - 4)]))
                    k.trans(pslot[:, 0:128], mrgY[:, m * 128:(m + 1) * 128],
                            ident[:], wait=tw, inc=s_mm)
                    k.v("scalar_tensor_tensor",
                        gy_fm[:, m, 2 * p * 64:2 * p * 64 + 128],
                        zb2.rearrange("x (m t) -> x m t", m=32)[:, m],
                        nsc[:, m:m + 1], pslot[:, 0:128], ALU.mult,
                        ALU.mult, wait=[s_mm, (s_c4, s_c4.n)], inc=s_ve)
                    marks["gy"][(pp, m)] = s_ve.n
                marks["tp"][pp] = s_mm.n
            gy_done = (s_ve, s_ve.n)
            p6km = []
            p6_evac = gy_done
            for tpair in range(2):
                for kd in range(32):
                    slot = kd % 2
                    if len(p6km) >= 2:
                        bw = [(s_mm, p6km[-2])]
                    elif half == 0 and tpair == 0:
                        bw = [gy_done, (s_vl, s_vl.n)]
                    else:
                        bw = [gy_done]
                    k.dma("sync", wob[slot], w_out[kd * 128:(kd + 1) * 128, :],
                          inc=s_wb[slot], wait=bw)
                    for tp in range(2):
                        tt = tpair * 2 + tp
                        for q in range(4):
                            pk = (pb + ptr)[tp * 4 + q]
                            mmw = None
                            if tp == 0 and q == 0:
                                mmw = [(s_wb[slot], s_wb[slot].n)]
                                if kd == 0:
                                    mmw.append(p6_evac)
                            k.mm(pk[:, 0:512],
                                 gy_fm[:, kd, tt * 128:(tt + 1) * 128],
                                 wob[slot][:, q * 512:(q + 1) * 512],
                                 start=(kd == 0), stop=(kd == 31),
                                 wait=mmw,
                                 inc=s_mm if (tp == 1 and q == 3) else None)
                    p6km.append(s_mm.n)
                for tp in range(2):
                    tt = tpair * 2 + tp
                    t0 = half * 512 + tt * 128
                    uw = [(s_ot, s_ot.n)]
                    if tpair == 0 and tp == 0:
                        uw += [(s_ve, s_ve.n), (s_se, s_se.n),
                               (s_dr, s_dr.n), (s_cbr, s_cbr.n),
                               (s_c5, s_c5.n)]
                    k.dma("sync", lntmp[:], u[3 + t0:3 + t0 + 128, :],
                          inc=s_ur, wait=uw)
                    for q in range(4):
                        pk = (pb + ptr)[tp * 4 + q]
                        k.v("tensor_add", lntmp[:, q * 512:(q + 1) * 512],
                            lntmp[:, q * 512:(q + 1) * 512], pk[:, 0:512],
                            wait=[(s_mm, p6km[-1]), (s_ur, s_ur.n)]
                            if q == 0 else None)
                    k.vdrain(s_ve)
                    k.dma("sync", out[t0:t0 + 128, :], lntmp[:], inc=s_ot,
                          wait=[(s_ve, s_ve.n)])
                p6_evac = (s_ve, s_ve.n)
        t8 = lntmp.ap()[0:8, :]
        ptot_sb = ustage.ap()[0:8].rearrange("p a b -> p (a b)")[:, 0:D]
        c0v = mg_sb.ap()[0:8].rearrange("p a b -> p (a b)")[:, 50:51]
        k.dma("sync", t8, out[0:8, :], inc=s_ur,
              wait=[(s_ot, s_ot.n), (s_cc, 3)])
        k.dma("sync", ptot_sb, p_tot, inc=s_ur,
              wait=[(s_ve, s_ve.n), (s_mm, s_mm.n)])
        k.dma("sync", c0v, core0[0:8, 0:1], inc=s_ur)
        k.v("scalar_tensor_tensor", t8, ptot_sb, c0v, t8,
            ALU.mult, ALU.add, wait=[(s_ur, s_ur.n)])
        k.vdrain(s_ve)
        k.dma("sync", out[0:8, :], t8, inc=s_ot, wait=[(s_ve, s_ve.n)])
        k.op("sync", lambda h: h.nop(),
             wait=[s_ot, s_sp, s_vt, s_m, s_xnr, s_cc, s_vl, s_ml, s_zl, s_ft, s_pb, s_c2, s_c3, s_c4, s_c5, s_dr,
                   s_cbr, s_ur, s_gld] + s_zs + s_u + s_wb + s_f + s_cb +
                  [s_const])

        k.emit()
    return nc


def _prep_inputs(L0_attn_u, L0_attn_in_proj_w, L0_attn_conv_weight,
                 L0_attn_conv_bias, L0_attn_A_log, L0_attn_dt_bias,
                 L0_attn_D_param, L0_attn_norm_scale, L0_attn_out_proj_w,
                 L0_ffn_w1, L0_ffn_w3, L0_ffn_w2, L0_ffn_wr):
    u = np.ascontiguousarray(np.asarray(L0_attn_u, np.float32).reshape(8192, D))
    upad = np.concatenate([np.zeros((HALO, D), np.float32), u], 0)
    csmask = np.ones((NH, TOK), np.float32)
    csmask[:, ::S] = 0.0
    shared = {
        "w_in": np.ascontiguousarray(L0_attn_in_proj_w, np.float32),
        "conv_w": np.ascontiguousarray(np.transpose(
            np.asarray(L0_attn_conv_weight, np.float32).reshape(4, 32, 128),
            (2, 1, 0))),
        "conv_b": np.ascontiguousarray(
            np.asarray(L0_attn_conv_bias, np.float32).reshape(32, 128).T),
        "a_log": np.asarray(L0_attn_A_log, np.float32).reshape(NH, 1),
        "dt_bias": np.asarray(L0_attn_dt_bias, np.float32).reshape(NH, 1),
        "d_rep": np.tile(np.asarray(L0_attn_D_param, np.float32).reshape(1, NH),
                         (128, 1)),
        "norm_scale": np.ascontiguousarray(
            np.asarray(L0_attn_norm_scale, np.float32).reshape(32, 128).T),
        "w_out": np.ascontiguousarray(L0_attn_out_proj_w, np.float32),
        "w1t": np.ascontiguousarray(np.asarray(L0_ffn_w1, np.float32).T),
        "w3t": np.ascontiguousarray(np.asarray(L0_ffn_w3, np.float32).T),
        "w2t": np.ascontiguousarray(np.asarray(L0_ffn_w2, np.float32).T),
        "wrt": np.ascontiguousarray(np.asarray(L0_ffn_wr, np.float32).T
                                    .reshape(16, 128, E).transpose(1, 0, 2)
                                    .reshape(128, 16 * E)),
        "ident": np.eye(128, dtype=np.float32),
        "csmask": csmask,
        "maskl": np.tile(np.where(
            np.arange(S)[None, :] >= np.arange(S)[:, None],
            0.0, -1e30).astype(np.float32), (2, 1)),
    }
    in_maps = []
    for c in range(NCORES):
        m = dict(shared)
        m["u"] = np.ascontiguousarray(upad[c * TOK:c * TOK + TOKH, :])
        cmb = np.zeros((64, NCORES), np.float32)
        cmb[:, :c] = 1.0
        m["combmask"] = cmb
        m["combmaskinv"] = (1.0 - cmb)
        m["core0"] = np.full((128, 1), 1.0 if c == 0 else 0.0, np.float32)
        in_maps.append(m)
    return in_maps


def kernel(**inputs):
    in_maps = _prep_inputs(**inputs)
    nc = build_program()
    res = run_bass_kernel_spmd(nc, in_maps, core_ids=list(range(NCORES)))
    return np.concatenate([r["out"] for r in res.results], 0).reshape(1, 8192, D)


# version 13 (final)
# speedup vs baseline: 1.2247x; 1.0016x over previous
import contextlib
import numpy as np
import concourse.bass as bass
import concourse.mybir as mybir
from concourse.bass_utils import run_bass_kernel_spmd

F32 = mybir.dt.float32
AF = mybir.ActivationFunctionType
ALU = mybir.AluOpType
AX = mybir.AxisListType

NCORES = 8
TOK = 1024
HALO = 3
TOKH = TOK + HALO
D = 2048
DI = 4096
NH = 64
NST = 64
S = 64
NCH = TOK // S
DIP = 8384
E = 8

class Sem:
    def __init__(self, handle):
        self.h = handle
        self.n = 0


class K:
    ENGS = ("tensor", "vector", "scalar", "gpsimd", "sync")

    def __init__(self, nc):
        self.nc = nc
        self.q = {e: [] for e in self.ENGS}
        self.dirty = {e: set() for e in self.ENGS}

    @staticmethod
    def _tnames(aps):
        out = set()
        for a in aps:
            if a is None or isinstance(a, (int, float)):
                continue
            t = getattr(a, "tensor", None)
            if t is not None:
                out.add(t.name)
        return out

    def op(self, eng, fn, reads=(), writes=(), inc=None, inc_k=1, wait=None,
           dma_inc=None):
        import traceback
        tag = None
        for fr in reversed(traceback.extract_stack()[:-1]):
            if fr.filename.endswith("kernel.py") and fr.name not in (
                    "op", "dma", "mm", "trans", "act", "v", "vdrain"):
                tag = f"{fr.name}:{fr.lineno}"
                break
        rn = self._tnames(reads)
        wn = self._tnames(writes)
        need_drain = bool(self.dirty[eng] & (rn | wn))
        waits = []
        if wait:
            for w in wait:
                waits.append((w.h, w.n) if isinstance(w, Sem) else (w[0].h, w[1]))
        if inc is not None:
            inc.n += inc_k
        if dma_inc is not None:
            dma_inc.n += 16
        if need_drain:
            self.dirty[eng] = set()

        def emit(h, nd=need_drain, waits=waits, fn=fn, inc=inc, inc_k=inc_k,
                 dma_inc=dma_inc, tag=tag):
            if nd:
                h.drain()
            for sh, sv in waits:
                h.wait_ge(sh, sv)
            try:
                ins = fn(h)
            except Exception as e:
                raise RuntimeError(f"emit failed at record site {tag}: {e}") from e
            if ins is not None:
                if inc is not None:
                    ins.then_inc(inc.h, inc_k)
                if dma_inc is not None:
                    ins.then_inc(dma_inc.h, 16)

        self.q[eng].append(emit)
        self.dirty[eng] |= wn

    def dma(self, eng, out, in_, inc=None, wait=None):
        self.op(eng, lambda h: h.dma_start(out=out, in_=in_),
                reads=(in_,), writes=(out,), dma_inc=inc, wait=wait)

    def mm(self, out, lhsT, rhs, start, stop, inc=None, wait=None):
        self.op("tensor",
                lambda h: h.matmul(out, lhsT, rhs, start=start, stop=stop),
                reads=(lhsT, rhs), writes=(out,), inc=inc, wait=wait)

    def trans(self, out, in_, ident, inc=None, wait=None):
        self.op("tensor", lambda h: h.transpose(out, in_, ident),
                reads=(in_, ident), writes=(out,), inc=inc, wait=wait)

    def act(self, out, in_, func, bias=0.0, scale=1.0, accum_out=None, inc=None,
            wait=None):
        self.op("scalar",
                lambda h: h.activation(out, in_, func, bias=bias, scale=scale,
                                       accum_out=accum_out),
                reads=(in_, None if isinstance(bias, float) else bias,
                       None if isinstance(scale, float) else scale),
                writes=(out, accum_out), inc=inc, wait=wait)

    def v(self, method, out, *ins, inc=None, wait=None, **kw):
        def fn(h, method=method, out=out, ins=ins, kw=kw):
            return getattr(h, method)(out, *ins, **kw)
        self.op("vector", fn, reads=ins, writes=(out,), inc=inc, wait=wait)

    def vdrain(self, inc):
        self.op("vector", lambda h: h.drain(), inc=inc)
        self.dirty["vector"] = set()

    def emit(self):
        with self.nc.Block() as block:
            @block.tensor
            def _(h):
                for f in self.q["tensor"]:
                    f(h)

            @block.vector
            def _(h):
                for f in self.q["vector"]:
                    f(h)

            @block.scalar
            def _(h):
                for f in self.q["scalar"]:
                    f(h)

            @block.gpsimd
            def _(h):
                for f in self.q["gpsimd"]:
                    f(h)

            @block.sync
            def _(h):
                for f in self.q["sync"]:
                    f(h)


def build_program(debug=(), stage=1):
    nc = bass.Bass("TRN2", target_bir_lowering=False, debug=False)

    def dram_in(name, shape):
        return nc.dram_tensor(name, shape, F32, kind="ExternalInput").ap()

    u = dram_in("u", [TOKH, D])
    w_in = dram_in("w_in", [D, DIP])
    conv_w = dram_in("conv_w", [128, 32, 4])
    conv_b = dram_in("conv_b", [128, 32])
    a_log = dram_in("a_log", [NH, 1])
    dt_bias = dram_in("dt_bias", [NH, 1])
    d_rep = dram_in("d_rep", [128, NH])
    norm_scale = dram_in("norm_scale", [128, 32])
    w_out = dram_in("w_out", [DI, D])
    w1t = dram_in("w1t", [D, D])
    w3t = dram_in("w3t", [D, D])
    w2t = dram_in("w2t", [D, D])
    wrt = dram_in("wrt", [128, 16 * E])
    ident_in = dram_in("ident", [128, 128])
    maskl_in = dram_in("maskl", [128, S])
    combmaskinv = dram_in("combmaskinv", [64, NCORES])
    csmask_in = dram_in("csmask", [NH, TOK])
    combmask = dram_in("combmask", [64, NCORES])
    core0 = dram_in("core0", [128, 1])

    out = nc.dram_tensor("out", [TOK, D], F32, kind="ExternalOutput").ap()

    def scratch(name, shape, dbg=False):
        kind = "ExternalOutput" if dbg else "Internal"
        if dbg:
            return nc.dram_tensor(name, shape, F32, kind="ExternalOutput").ap()
        return nc.dram_tensor(name, shape, F32).ap()

    xn_sp = scratch("xn_sp", [16, 128, TOKH], "xn" in debug)
    z_sp = scratch("z_sp", [32, 128, TOK], "z" in debug)
    bcdt_sp = scratch("bcdt_sp", [4, 64, TOK], "bcdt" in debug)
    vfm_sp = scratch("vfm_sp", [2, 32, 128, 512], "v" in debug)
    vtok_sp = scratch("vtok_sp", [NCH, 64, DI], "vtok" in debug)
    m_sp = scratch("m_sp", [NCH, 64, DI], "m" in debug)
    ft_sp = scratch("ft_sp", [65, DI], "ft" in debug)
    p_sp = scratch("p_sp", [8, D], "p" in debug)
    mg_sp = scratch("mg_sp", [128, 64], "p" in debug)
    h2_sp = scratch("h2_sp", [16, 128, TOK], "h2" in debug)
    f_bnc = nc.dram_tensor("f_bnc", [64, DI], F32).ap()
    t_bnc = nc.dram_tensor("t_bnc", [64, 1], F32).ap()
    f_gat = nc.dram_tensor("f_gat", [NCORES, 64, DI], F32).ap()
    t_gat = nc.dram_tensor("t_gat", [NCORES, 64, 1], F32).ap()
    csrow = nc.dram_tensor("csrow", [2, 2, NH * S], F32).ap()
    totrow = nc.dram_tensor("totrow", [2, 2, NH], F32).ap()
    rrow = nc.dram_tensor("rrow", [2, 1, S], F32).ap()
    p_bnc = nc.dram_tensor("p_bnc", [8, D], F32).ap()
    p_tot = nc.dram_tensor("p_tot", [8, D], F32).ap()
    y_sp = scratch("y_sp", [NCH, 64, DI], "y" in debug)
    att_sp = scratch("att_sp", [128, DI], "att" in debug)
    vt2_sp = scratch("vt2_sp", [128, DI], "att" in debug)
    hin_sp = scratch("hin_sp", [64, DI], "att" in debug)
    sml_sp = scratch("sml_sp", [128, 1152], "att" in debug)
    hq_sp = scratch("hq_sp", [64, DI], "att" in debug)

    k = K(nc)
    sems = {}

    def sem(name):
        s = Sem(stackctx.enter_context(nc.semaphore(name)))
        sems[name] = s
        return s

    stackctx = contextlib.ExitStack()
    with stackctx:
        def sbuf(name, shape, dtype=F32):
            return stackctx.enter_context(nc.sbuf_tensor(name, shape, dtype))

        def psum(name):
            return stackctx.enter_context(nc.psum_tensor(name, [128, 512], F32))

        s_const = sem("const")
        s_u = [sem("u0"), sem("u1")]
        s_wb = [sem("wb0"), sem("wb1")]
        s_zs = [sem(f"zs{i}") for i in range(4)]
        s_mm = sem("mm")
        s_ve = sem("ve")
        s_se = sem("se")
        s_sp = sem("sp")
        s_gld = sem("gld")
        s_cc = sem("cc")
        s_vt = sem("vt")
        s_ft = sem("ft")
        s_pb = sem("pbs")
        s_c2 = sem("c2")
        s_c3 = sem("c3")
        s_c4 = sem("c4")
        s_c5 = sem("c5")
        s_m = sem("m")
        s_xnr = sem("xnr")

        ident = sbuf("identt", [128, 128])
        xn_fm = sbuf("xn_fm", [128, 16, TOKH])
        B_fm = sbuf("B_fm", [64, TOK])
        C_fm = sbuf("C_fm", [64, TOK])
        dt_fm = sbuf("dt_fm", [64, TOK])
        la_fm = sbuf("la_fm", [64, TOK])
        cs_fm = sbuf("cs_fm", [64, TOK])
        csmask = sbuf("csmask_t", [64, TOK])
        acol = sbuf("acol", [64, 1])
        dtb = sbuf("dtb", [64, 1])
        convw = sbuf("convw_t", [128, 32, 4])
        convb = sbuf("convb_t", [128, 32])
        v_fm = sbuf("v_fm", [128, 32, 512])
        wband = sbuf("wband", [128, 2, 512])
        ustage = sbuf("ustage", [128, 2, D])
        lnm = sbuf("lnm", [128, 2])
        consts1 = sbuf("consts1", [128, 1])
        lntmp = sbuf("lntmp", [128, D])
        xtmp = sbuf("xtmp", [128, 2, 515])
        zstage = sbuf("zstage", [128, 4, 512])
        h_loc = ustage.ap()[0:64].rearrange("p a b -> p (a b)")[:, 0:DI]
        xnflat = xn_fm.ap()[0:64].rearrange("p a b -> p (a b)")
        xnflat128 = xn_fm.ap().rearrange("p a b -> p (a b)")
        vt_sb = xnflat[:, 0:DI]
        m_sb = xnflat[:, DI:2 * DI]
        b_tok = sbuf("b_tok", [64, 64])
        e2s = sbuf("e2s", [64, NCH, 64])
        tmp2_fm = sbuf("tmp2_fm", [64, 64])
        cs_tok = sbuf("cs_tok", [64, NCH, 64])
        ecs_tok = sbuf("ecs_tok", [64, NCH, 64])
        drep = sbuf("drep", [64, 64])
        tcol = sbuf("tcol", [64, 1])

        pb = [psum(f"pb{i}") for i in range(6)]
        ptr = [psum("pt0"), psum("pt1")]

        k.dma("sync", ident[:], ident_in, inc=s_const)
        k.dma("sync", csmask[:], csmask_in, inc=s_const)
        k.dma("sync", acol[:], a_log, inc=s_const)
        k.dma("sync", dtb[:], dt_bias, inc=s_const)
        k.dma("sync", convw[:], conv_w, inc=s_const)
        k.dma("sync", convb[:], conv_b, inc=s_const)
        k.op("vector", lambda h: h.memset(consts1[:], 1.0),
             writes=(consts1[:],))
        k.vdrain(s_ve)
        k.act(acol[:], acol[:], AF.Exp, wait=[s_const], inc=s_se)

        umarks = []
        semarks = []
        pmarks = {}
        tposed = 0
        for t in range(9):
            rows = 128 if t < 8 else 3
            r0 = t * 128
            ub = ustage[:, t % 2]
            w = None
            if t >= 2:
                w = [(s_mm, umarks[t - 2]), (s_se, semarks[t - 2])]
            k.dma("sync", ub[0:rows, :], u[r0:r0 + rows, :], inc=s_u[t % 2],
                  wait=w)
            k.v("reduce_sum", lnm[0:rows, 0:1], ub[0:rows, :], axis=AX.X,
                wait=[s_u[t % 2]])
            k.v("tensor_scalar_mul", lnm[0:rows, 0:1], lnm[0:rows, 0:1], 1.0 / D)
            k.v("tensor_single_scalar", ub[0:rows, :], ub[0:rows, :],
                lnm[0:rows, 0:1], ALU.subtract)
            k.vdrain(s_ve)
            k.act(lntmp[0:rows, :], ub[0:rows, :], AF.Square,
                  accum_out=lnm[0:rows, 1:2], wait=[s_ve], inc=s_se)
            k.v("tensor_scalar", lnm[0:rows, 1:2], lnm[0:rows, 1:2], 1.0 / D,
                1e-5, ALU.mult, op1=ALU.add, wait=[s_se])
            k.vdrain(s_ve)
            k.act(lnm[0:rows, 1:2], lnm[0:rows, 1:2], AF.Sqrt, wait=[s_ve],
                  inc=s_se)
            k.v("reciprocal", lnm[0:rows, 1:2], lnm[0:rows, 1:2], wait=[s_se])
            k.v("tensor_single_scalar", ub[0:rows, :], ub[0:rows, :],
                lnm[0:rows, 1:2], ALU.mult)
            k.vdrain(s_ve)
            for kd in range(16):
                pslot = ptr[tposed % 2]
                tw = [s_ve] if kd == 0 else []
                if tposed >= 2:
                    tw = tw + [(s_se, pmarks[tposed - 2])]
                k.trans(pslot[:, 0:rows], ub[0:rows, kd * 128:(kd + 1) * 128],
                        ident[0:rows, 0:rows], wait=tw or None, inc=s_mm)
                k.act(xn_fm[:, kd, r0:r0 + rows], pslot[:, 0:rows], AF.Copy,
                      wait=[s_mm], inc=s_se)
                pmarks[tposed] = s_se.n
                tposed += 1
            umarks.append(s_mm.n)
            semarks.append(s_se.n)
        xn_done = (s_se, s_se.n)
        for kd in range(16):
            k.dma("sync", xn_sp[kd], xn_fm[:, kd, :], inc=s_sp, wait=[xn_done])
        xnsp_mark = (s_sp, s_sp.n)

        zmov = [(3, 0), (515, 3)]
        zslot_marks = []
        bank_free = [(s_se, s_se.n), (s_ve, s_ve.n)]
        wmarks = []

        def proj_pass(tiles, evac, xn_cols=zmov, ncols=512, bankset=None):
            nonlocal bank_free, wmarks
            nt = len(tiles)
            banks = bankset if bankset is not None else pb
            for kd in range(16):
                c0 = tiles[0][0]
                c1 = tiles[-1][0] + tiles[-1][1]
                slot = len(wmarks) % 2
                wb = wband[:, slot, 0:c1 - c0]
                lw = None
                if len(wmarks) >= 2:
                    lw = [(s_mm, wmarks[-2])]
                k.dma("sync", wb, w_in[kd * 128:(kd + 1) * 128, c0:c1],
                      inc=s_wb[slot], wait=lw)
                for mi, (wc0, wm) in enumerate(tiles):
                    lhsT = wb[:, wc0 - c0:wc0 - c0 + wm]
                    for (x0, pbase) in xn_cols:
                        mmw = []
                        if mi == 0 and (x0, pbase) == xn_cols[0]:
                            mmw.append((s_wb[slot], s_wb[slot].n))
                            if kd == 0:
                                mmw += bank_free
                        k.mm(banks[pbase + mi][0:wm, 0:ncols], lhsT,
                             xn_fm[:, kd, x0:x0 + ncols],
                             start=(kd == 0), stop=(kd == 15),
                             wait=mmw or None,
                             inc=s_mm if (mi == nt - 1 and (x0, pbase) == xn_cols[-1]) else None)
                wmarks.append(s_mm.n)
            evac()

        zsets = [[pb[0], pb[1], pb[2], pb[3]], [pb[4], pb[5], ptr[0], ptr[1]]]
        bfree2 = {0: bank_free, 1: bank_free}
        zmov2 = [(3, 0), (515, 2)]
        for gi, g0 in enumerate(range(0, 32, 2)):
            gt = [(m * 128, 128) for m in range(g0, min(g0 + 2, 32))]
            bs = zsets[gi % 2]

            def z_evac(g0=g0, gt=gt, bs=bs, gi=gi):
                nonlocal bank_free
                for mi in range(len(gt)):
                    m = g0 + mi
                    for hi in range(2):
                        slot = len(zslot_marks) % 4
                        zw = [(s_mm, s_mm.n)]
                        if len(zslot_marks) >= 4:
                            zw.append((s_zs[slot], zslot_marks[-4]))
                        k.act(zstage[:, slot], bs[hi * 2 + mi][:, 0:512],
                              AF.Copy, wait=zw, inc=s_se)
                        k.dma("sync", z_sp[m, :, hi * 512:hi * 512 + 512],
                              zstage[:, slot], inc=s_zs[slot], wait=[s_se])
                        zslot_marks.append(s_zs[slot].n)
                bfree2[gi % 2] = [(s_se, s_se.n)]
                bank_free = [(s_se, s_se.n)]

            bank_free = bfree2[gi % 2]
            proj_pass(gt, z_evac, xn_cols=zmov2, bankset=bs)
        bank_free = [(s_se, s_se.n)]

        def bcdt_evac():
            nonlocal bank_free
            for mi, (nm, dest) in enumerate((("B", B_fm), ("C", C_fm),
                                             ("dt", dt_fm))):
                for hi in range(2):
                    src = pb[hi * 3 + mi][0:64, 0:512]
                    dc = hi * 512
                    if nm in ("B", "C"):
                        k.v("tensor_relu", dest[:, dc:dc + 512], src,
                            wait=[(s_mm, s_mm.n)])
                        k.v("tensor_sub", lntmp[0:64, 0:512], src,
                            dest[:, dc:dc + 512])
                        k.vdrain(s_ve)
                        k.act(lntmp[0:64, 512:1024], lntmp[0:64, 0:512], AF.Exp,
                              wait=[s_ve], inc=s_se)
                        k.v("tensor_add", dest[:, dc:dc + 512],
                            dest[:, dc:dc + 512], lntmp[0:64, 512:1024],
                            wait=[s_se])
                        k.vdrain(s_ve)
                    else:
                        t0 = lntmp[0:64, 0:512]
                        t1 = lntmp[0:64, 512:1024]
                        k.v("tensor_scalar_add", t0, src, dtb[:, 0:1],
                            wait=[(s_mm, s_mm.n)])
                        k.v("tensor_scalar_mul", t1, t0, -1.0)
                        k.v("tensor_tensor", t1, t1, t0, ALU.min)
                        k.vdrain(s_ve)
                        k.act(t1, t1, AF.Exp, wait=[s_ve])
                        k.op("scalar", lambda h: h.drain())
                        k.act(t1, t1, AF.Ln, bias=1.0 if False else consts1[0:64, 0:1],
                              inc=s_se)
                        k.v("tensor_relu", dest[:, dc:dc + 512], t0, wait=[s_se])
                        k.v("tensor_add", dest[:, dc:dc + 512],
                            dest[:, dc:dc + 512], t1)
                        k.vdrain(s_ve)
            bank_free = [(s_se, s_se.n), (s_ve, s_ve.n)]
        proj_pass([(8192, 64), (8256, 64), (8320, 64)], bcdt_evac)

        k.v("tensor_scalar", la_fm[:], dt_fm[:], acol[:, 0:1], -1.0,
            ALU.mult, op1=ALU.mult, wait=[s_se, s_ve])
        k.op("vector", lambda h: h.tensor_tensor_scan(
            cs_fm[:], csmask[:], la_fm[:], 0.0, ALU.mult, ALU.add),
            reads=(csmask[:], la_fm[:]), writes=(cs_fm[:],))
        k.vdrain(s_ve)
        if "bcdt" in debug:
            k.dma("sync", bcdt_sp[0], B_fm[:], inc=s_sp, wait=[s_ve])
            k.dma("sync", bcdt_sp[1], C_fm[:], inc=s_sp)
            k.dma("sync", bcdt_sp[2], dt_fm[:], inc=s_sp)
            k.dma("sync", bcdt_sp[3], cs_fm[:], inc=s_sp)

        xt_marks = {}
        ptm = {}
        mqm = {}
        mmvt = {}
        ttm = {}
        ttm2 = {}
        ptm2 = {}
        for half in range(2):
            xn0 = half * 512
            if half == 1:
                xw = [(s_ve, s_ve.n), (s_se, s_se.n), (s_mm, s_mm.n),
                      (s_vt, s_vt.n), (s_m, s_m.n)]
                for kd in range(16):
                    k.dma("sync", xn_fm[:, kd, :], xn_sp[kd], inc=s_xnr,
                          wait=xw if kd == 0 else None)
            vfree = None
            if half == 1 and "v" in debug:
                vfree = (s_sp, s_sp.n)
            xsets = [[pb[0], pb[1], pb[2], pb[3]],
                     [pb[4], pb[5], ptr[0], ptr[1]]]
            xfree = {0: list(bank_free), 1: list(bank_free)}
            for gi, g0 in enumerate(range(0, 32, 2)):
                gt = [(4096 + m * 128, 128) for m in range(g0, min(g0 + 2, 32))]
                nt = len(gt)
                bs = xsets[gi % 2]
                for kd in range(16):
                    c0 = gt[0][0]
                    c1 = gt[-1][0] + gt[-1][1]
                    slot = len(wmarks) % 2
                    wb = wband[:, slot, 0:c1 - c0]
                    lw = [(s_mm, wmarks[-2])] if len(wmarks) >= 2 else None
                    k.dma("sync", wb, w_in[kd * 128:(kd + 1) * 128, c0:c1],
                          inc=s_wb[slot], wait=lw)
                    for mi in range(nt):
                        wc0, wm = gt[mi]
                        lhsT = wb[:, wc0 - c0:wc0 - c0 + wm]
                        mmw = []
                        if mi == 0:
                            mmw.append((s_wb[slot], s_wb[slot].n))
                            if kd == 0:
                                mmw += xfree[gi % 2] + [(s_xnr, s_xnr.n)]
                        k.mm(bs[mi][:, 0:512], lhsT,
                             xn_fm[:, kd, xn0:xn0 + 512],
                             start=(kd == 0), stop=(kd == 15), wait=mmw or None)
                        k.mm(bs[2 + mi][:, 0:3], lhsT,
                             xn_fm[:, kd, xn0 + 512:xn0 + 515],
                             start=(kd == 0), stop=(kd == 15),
                             inc=s_mm if (mi == nt - 1) else None)
                    wmarks.append(s_mm.n)
                for mi in range(nt):
                    m = g0 + mi
                    xt = xtmp[:, m % 2]
                    xw = [(s_mm, s_mm.n)]
                    if (m - 2) in xt_marks:
                        xw.append((s_ve, xt_marks[m - 2]))
                    k.act(xt[:, 0:512], bs[mi][:, 0:512], AF.Copy, wait=xw)
                    k.act(xt[:, 512:515], bs[2 + mi][:, 0:3], AF.Copy, inc=s_se)
                    cw = convw[:, m]
                    vdst = v_fm[:, m, :]
                    vw = [s_se]
                    if vfree is not None:
                        vw.append(vfree)
                    k.v("tensor_scalar_mul", vdst, xt[:, 0:512], cw[:, 0:1],
                        wait=vw)
                    for kk in range(1, 4):
                        k.v("scalar_tensor_tensor", vdst, xt[:, kk:kk + 512],
                            cw[:, kk:kk + 1], vdst, ALU.mult, ALU.add)
                    k.vdrain(s_ve)
                    xt_marks[m] = s_ve.n
                    k.act(vdst, vdst, AF.Silu, bias=convb[:, m:m + 1],
                          wait=[s_ve], inc=s_se)
                xfree[gi % 2] = [(s_se, s_se.n)]
                bank_free = [(s_se, s_se.n)]
                if "v" in debug:
                    for mi in range(nt):
                        m = g0 + mi
                        k.dma("sync", vfm_sp[half, m], v_fm[:, m, :],
                              inc=s_sp, wait=[(s_se, s_se.n)])

            if half == 0:
                k.op("vector", lambda h: h.memset(h_loc, 0.0),
                     writes=(h_loc,), wait=[(s_mm, s_mm.n), (s_se, s_se.n)])
                k.vdrain(s_ve)
            vconsumed = []
            for c in range(8):
                gc = half * 8 + c
                csl = slice(gc * 64, gc * 64 + 64)
                for m in range(32):
                    pslot = ptr[m % 2]
                    tw = [(s_se, s_se.n)] if m < 2 else [(s_se, ptm[m - 2])]
                    if m == 0:
                        tw = tw + [(s_mm, mmvt.get("x", 0))]
                    ew_extra = [xnsp_mark] if (m == 0 and c == 0 and half == 0) else []
                    k.trans(pslot[0:64, 0:128], v_fm[:, m, c * 64:(c + 1) * 64],
                            ident[:], wait=tw, inc=s_mm)
                    ew = [s_mm] if m > 0 else [s_mm, (s_ve, s_ve.n)]
                    k.act(vt_sb[:, m * 128:(m + 1) * 128], pslot[0:64, 0:128],
                          AF.Copy, wait=ew + ew_extra, inc=s_se)
                    ptm[m] = s_se.n
                vt_done = (s_se, s_se.n)
                vconsumed.append(s_mm.n)
                k.dma("sync", vtok_sp[gc], vt_sb, inc=s_vt, wait=[vt_done])
                k.v("tensor_scalar", tmp2_fm[:], cs_fm[:, csl],
                    cs_fm[:, gc * 64 + 63:gc * 64 + 64], -1.0, ALU.subtract,
                    op1=ALU.mult, wait=[vt_done])
                k.vdrain(s_ve)
                k.trans(ptr[0][0:64, 0:64], tmp2_fm[:], ident[0:64, 0:64],
                        wait=[s_ve], inc=s_mm)
                k.trans(ptr[1][0:64, 0:64], B_fm[:, csl], ident[0:64, 0:64],
                        inc=s_mm)
                k.trans(ptr[0][0:64, 64:128], cs_fm[:, csl], ident[0:64, 0:64],
                        inc=s_mm)
                k.act(e2s[:, gc], ptr[0][0:64, 0:64], AF.Exp, wait=[s_mm])
                k.act(b_tok[:], ptr[1][0:64, 0:64], AF.Copy)
                k.act(cs_tok[:, gc], ptr[0][0:64, 64:128], AF.Copy)
                k.act(ecs_tok[:, gc], ptr[0][0:64, 64:128], AF.Exp, inc=s_se)
                ptm[0] = s_se.n
                ptm[1] = s_se.n
                k.v("tensor_mul", drep[:], e2s[:, gc], ecs_tok[:, gc],
                    wait=[s_se])
                k.v("tensor_tensor",
                    vt_sb.rearrange("t (h p) -> t h p", h=64),
                    vt_sb.rearrange("t (h p) -> t h p", h=64),
                    e2s[:, gc][:, :, None].broadcast_to([64, 64, 64]), ALU.mult,
                    wait=[s_vt])
                k.vdrain(s_ve)
                for q in range(8):
                    k.mm(pb[q % 6][0:64, 0:512], b_tok[:],
                         vt_sb[:, q * 512:(q + 1) * 512], start=True,
                         stop=True,
                         wait=[s_ve] if q == 0 else ([(s_se, mqm[q - 6])] if q >= 6 else None),
                         inc=s_mm)
                    macw = [s_mm]
                    if q == 0:
                        macw.append((s_m, s_m.n))
                    k.act(m_sb[:, q * 512:(q + 1) * 512], pb[q % 6][0:64, 0:512],
                          AF.Copy, wait=macw, inc=s_se)
                    mqm[q] = s_se.n
                mmvt["x"] = s_mm.n
                m_done = (s_se, s_se.n)
                k.dma("sync", m_sp[gc], m_sb, inc=s_m, wait=[m_done])
                k.v("tensor_tensor",
                    h_loc.rearrange("n (h p) -> n h p", h=64),
                    h_loc.rearrange("n (h p) -> n h p", h=64),
                    drep[:, :, None].broadcast_to([64, 64, 64]), ALU.mult,
                    wait=[m_done])
                k.v("tensor_add", h_loc, h_loc, m_sb[:])
                k.vdrain(s_ve)
                mspill_scan = [(s_m, s_m.n), (s_ve, s_ve.n)]
            bank_free = bank_free + [(s_se, s_se.n), (s_ve, s_ve.n)]

        k.v("reduce_sum", tcol[:], la_fm[:], axis=AX.X, wait=[(s_se, s_se.n)])
        k.vdrain(s_ve)
        k.act(tcol[:], tcol[:], AF.Exp, wait=[s_ve], inc=s_se)
        k.dma("sync", f_bnc, h_loc, inc=s_ft, wait=[(s_ve, s_ve.n)])
        k.dma("sync", t_bnc, tcol[:], inc=s_ft, wait=[(s_se, s_se.n)])
        if "ft" in debug:
            k.dma("sync", ft_sp[0:64, :], h_loc, inc=s_sp,
                  wait=[(s_ve, s_ve.n)])
            k.dma("sync", ft_sp[64:65, 0:64], tcol[:, 0:1].rearrange("a b -> b a") if False else tcol[0:64, 0:1],
                  inc=s_sp, wait=[(s_se, s_se.n)])
        cc_ready = (s_ft, s_ft.n)
        k.op("gpsimd", lambda h: h.collective_compute(
            "AllGather", ALU.bypass, replica_groups=[list(range(NCORES))],
            ins=[f_bnc.opt()], outs=[f_gat.opt()]), wait=[cc_ready], inc=s_cc)
        k.op("gpsimd", lambda h: h.collective_compute(
            "AllGather", ALU.bypass, replica_groups=[list(range(NCORES))],
            ins=[t_bnc.opt()], outs=[t_gat.opt()]), wait=None, inc=s_cc)

        wrt_sb = sbuf("wrt_sb", [128, 16 * E])
        mg_sb = sbuf("mg_sb", [128, 8, E])
        lgt = sbuf("lgt", [128, E + 4])
        p_part = zstage.ap()[0:8].rearrange("p a b -> p (a b)")[:, 0:D]
        k.dma("sync", wrt_sb[:], wrt, inc=s_c2)
        xw = [(s_ve, s_ve.n), (s_se, s_se.n), (s_mm, s_mm.n),
              (s_vt, s_vt.n), (s_m, s_m.n)]
        for kd in range(16):
            k.dma("sync", xn_fm[:, kd, :], xn_sp[kd], inc=s_xnr,
                  wait=xw if kd == 0 else None)
        for tt in range(8):
            tc = 3 + tt * 128
            for kd in range(16):
                k.mm(pb[0][0:128, 0:E], xn_fm[:, kd, tc:tc + 128],
                     wrt_sb[:, kd * E:(kd + 1) * E],
                     start=(kd == 0), stop=(kd == 15),
                     wait=([(s_xnr, s_xnr.n), (s_c2, s_c2.n)] + bank_free +
                           [(s_se, s_se.n)]) if (tt == 0 and kd == 0) else
                          ([(s_se, ttm[tt - 1])] if kd == 0 and tt > 0 else None),
                     inc=s_mm if kd == 15 else None)
            lcw = [s_mm] + ([(s_ve, ttm2[tt - 1])] if tt > 0 else [])
            k.act(lgt[:, 0:E], pb[0][:, 0:E], AF.Copy, wait=lcw, inc=s_se)
            k.v("reduce_max", lgt[:, E:E + 1], lgt[:, 0:E], axis=AX.X,
                wait=[s_se])
            k.v("tensor_single_scalar", mg_sb[:, tt], lgt[:, 0:E],
                lgt[:, E:E + 1], ALU.is_equal)
            k.vdrain(s_ve)
            k.v("scalar_tensor_tensor", lgt[:, 0:E], mg_sb[:, tt], -1e30,
                lgt[:, 0:E], ALU.mult, ALU.add)
            k.v("reduce_max", lgt[:, E + 1:E + 2], lgt[:, 0:E], axis=AX.X)
            k.vdrain(s_ve)
            k.v("tensor_single_scalar", lgt[:, 0:E], lgt[:, 0:E],
                lgt[:, E + 1:E + 2], ALU.is_equal)
            k.v("tensor_sub", lgt[:, E + 2:E + 3], lgt[:, E:E + 1],
                lgt[:, E + 1:E + 2])
            k.vdrain(s_ve)
            k.act(lgt[:, E + 2:E + 3], lgt[:, E + 2:E + 3], AF.Sigmoid,
                  wait=[s_ve], inc=s_se)
            k.v("tensor_scalar", lgt[:, E + 3:E + 4], lgt[:, E + 2:E + 3],
                -1.0, 1.0, ALU.mult, op1=ALU.add, wait=[s_se])
            k.vdrain(s_ve)
            k.v("tensor_scalar", mg_sb[:, tt], mg_sb[:, tt],
                lgt[:, E + 2:E + 3], None, ALU.mult, wait=[s_ve])
            k.v("scalar_tensor_tensor", mg_sb[:, tt], lgt[:, 0:E],
                lgt[:, E + 3:E + 4], mg_sb[:, tt], ALU.mult, ALU.add)
            k.vdrain(s_ve)
            ttm[tt] = s_se.n
            ttm2[tt] = s_ve.n
        mg_done = (s_ve, s_ve.n)
        if "p" in debug:
            k.dma("sync", mg_sp,
                  mg_sb.ap().rearrange("p a b -> p (a b)"), inc=s_sp,
                  wait=[mg_done])

        h1v = v_fm.ap().rearrange("p a b -> p (a b)")
        h2v = xnflat128
        def ffn_fm(wsrc, mov, dest_fn, func, first_wait):
            for g0 in range(0, 16, 3):
                gt = list(range(g0, min(g0 + 3, 16)))
                nt = len(gt)
                for kd in range(16):
                    slot = len(wmarks) % 2
                    wb = wband[:, slot, 0:nt * 128]
                    lw = [(s_mm, wmarks[-2])] if len(wmarks) >= 2 else None
                    k.dma("sync", wb,
                          wsrc[kd * 128:(kd + 1) * 128,
                               g0 * 128:g0 * 128 + nt * 128],
                          inc=s_wb[slot], wait=lw)
                    for mi in range(nt):
                        lhsT = wb[:, mi * 128:(mi + 1) * 128]
                        for hi in range(2):
                            mmw = []
                            if mi == 0 and hi == 0:
                                mmw.append((s_wb[slot], s_wb[slot].n))
                                if kd == 0:
                                    mmw += bank_free2[0]
                                    if g0 == 0:
                                        mmw += first_wait
                            k.mm(pb[hi * 3 + mi][:, 0:512], lhsT,
                                 mov(kd, hi), start=(kd == 0), stop=(kd == 15),
                                 wait=mmw or None,
                                 inc=s_mm if (mi == nt - 1 and hi == 1) else None)
                    wmarks.append(s_mm.n)
                for mi in range(nt):
                    m = g0 + mi
                    for hi in range(2):
                        k.act(dest_fn(m, hi), pb[hi * 3 + mi][:, 0:512], func,
                              wait=[(s_mm, s_mm.n)], inc=s_se)
                bank_free2[0] = [(s_se, s_se.n)]

        bank_free2 = [bank_free + [mg_done, (s_se, s_se.n)]]
        ffn_fm(w1t, lambda kd, hi: xn_fm[:, kd, 3 + hi * 512:3 + hi * 512 + 512],
               lambda m, hi: h1v[:, m * 1024 + hi * 512:m * 1024 + hi * 512 + 512],
               AF.Silu, [])
        ffn_fm(w3t, lambda kd, hi: h1v[:, kd * 1024 + hi * 512:kd * 1024 + hi * 512 + 512],
               lambda m, hi: h2v[:, m * 1024 + hi * 512:m * 1024 + hi * 512 + 512],
               AF.Silu, [(s_mm, s_mm.n)])
        if "h2" in debug:
            for m in range(16):
                k.dma("sync", h2_sp[m],
                      h2v[:, m * 1024:(m + 1) * 1024], inc=s_sp,
                      wait=[(s_se, s_se.n)])
        ffn_fm(w2t, lambda kd, hi: h2v[:, kd * 1024 + hi * 512:kd * 1024 + hi * 512 + 512],
               lambda m, hi: h1v[:, m * 1024 + hi * 512:m * 1024 + hi * 512 + 512],
               AF.Copy, [(s_mm, s_mm.n)])
        htok = lntmp.ap()
        hw0 = [(s_se, s_se.n), (s_ve, s_ve.n)]
        for tt in range(8):
            for dtile in range(16):
                pslot = ptr[dtile % 2]
                tw = ([(s_se, ptm2[(tt, dtile - 2)])] if dtile >= 2
                      else [(s_se, ptm2.get((tt - 1, 14 + dtile), s_se.n))])
                if tt == 0 and dtile < 2:
                    tw = [(s_se, s_se.n)]
                if tt == 0 and dtile == 0:
                    tw += hw0
                k.trans(pslot[:, 0:128],
                        h1v[:, dtile * 1024 + tt * 128:dtile * 1024 + tt * 128 + 128],
                        ident[:], wait=tw, inc=s_mm)
                k.act(htok[:, dtile * 128:(dtile + 1) * 128], pslot[:, 0:128],
                      AF.Copy, wait=[s_mm], inc=s_se)
                ptm2[(tt, dtile)] = s_se.n
            for q in range(4):
                k.mm(pb[q][0:8, 0:512], mg_sb[:, tt],
                     htok[:, q * 512:(q + 1) * 512],
                     start=(tt == 0), stop=(tt == 7),
                     wait=[(s_se, s_se.n)] if q == 0 else None,
                     inc=s_mm if q == 3 else None)
        for q in range(4):
            k.act(p_part[:, q * 512:(q + 1) * 512], pb[q][0:8, 0:512], AF.Copy,
                  wait=[s_mm] + ([(sz, sz.n) for sz in s_zs] if q == 0 else []),
                  inc=s_se)
        if "p" in debug:
            k.dma("sync", p_sp, p_part, inc=s_sp, wait=[(s_se, s_se.n)])

        k.dma("sync", p_bnc, p_part, inc=s_pb, wait=[(s_se, s_se.n)])
        pcc_ready = (s_pb, s_pb.n)
        k.op("gpsimd", lambda h: h.collective_compute(
            "AllReduce", ALU.add, replica_groups=[list(range(NCORES))],
            ins=[p_bnc.opt()], outs=[p_tot.opt()]), wait=[pcc_ready], inc=s_cc)

        hbuf = h_loc
        SPc = 3 * DI + 2048
        trep = xnflat[:, SPc + 768:SPc + 832]
        tm_t = xnflat[:, SPc + 832:SPc + 896]
        cmask = mg_sb.ap()[0:64].rearrange("p a b -> p (a b)")[:, 32:40]
        cmaski = mg_sb.ap()[0:64].rearrange("p a b -> p (a b)")[:, 40:48]
        fbufs = [xnflat[:, 0:DI], xnflat[:, DI:2 * DI]]
        s_f = [sem("f0"), sem("f1")]
        cmw = [(s_mm, s_mm.n), (s_se, s_se.n), (s_ve, s_ve.n)]
        k.dma("sync", cmask, combmask, inc=s_c3, wait=cmw)
        k.dma("sync", cmaski, combmaskinv, inc=s_c3)
        k.op("vector", lambda h: h.memset(hbuf, 0.0), writes=(hbuf,),
             wait=[(s_mm, s_mm.n), (s_se, s_se.n), (s_sp, s_sp.n),
                   (s_ft, s_ft.n)])
        k.vdrain(s_ve)
        fjm = {}
        for j in range(NCORES - 1):
            fb = fbufs[j % 2]
            fw = [(s_cc, 2), (s_mm, s_mm.n), (s_se, s_se.n)]
            if j >= 2:
                fw.append((s_ve, fjm[j - 2]))
            k.dma("sync", fb, f_gat[j], inc=s_f[j % 2], wait=fw)
            k.dma("sync", trep,
                  t_gat[j].rearrange("h one -> one h").broadcast_to([64, 64]),
                  inc=s_f[j % 2],
                  wait=[(s_ve, fjm[j - 1])] if j >= 1 else None)
            k.v("tensor_scalar", tm_t, trep, cmask[:, j:j + 1],
                cmaski[:, j:j + 1], ALU.mult, op1=ALU.add,
                wait=[(s_f[j % 2], s_f[j % 2].n), (s_c3, s_c3.n)])
            k.vdrain(s_ve)
            k.v("tensor_tensor", hbuf.rearrange("n (h p) -> n h p", h=64),
                hbuf.rearrange("n (h p) -> n h p", h=64),
                tm_t[:, :, None].broadcast_to([64, 64, 64]), ALU.mult)
            k.v("scalar_tensor_tensor", hbuf, fb, cmask[:, j:j + 1], hbuf,
                ALU.mult, ALU.add)
            k.vdrain(s_ve)
            fjm[j] = s_ve.n
        hin_done = (s_ve, s_ve.n)

        vtk2 = xnflat128[:, 0:DI]
        mrgY = xnflat128[:, DI:2 * DI]
        mrgA = xnflat[:, DI:2 * DI]
        att2 = xnflat128[:, 2 * DI:3 * DI]
        zb2 = xnflat128[:, 3 * DI:4 * DI]
        wob = [xnflat128[:, 0:2048], xnflat128[:, 2048:4096]]
        lnf = lntmp.ap()
        tmpq = lnf[:, 0:512]
        masklv = lnf[:, 512:576]
        gm2 = lnf[:, 576:640]
        drepA = lnf[0:64, 640:704]
        drepB = lnf[0:64, 1088:1152]
        cstk2 = lnf[:, 704:768]
        ecs2 = lnf[:, 768:832]
        drpDv = lnf[:, 896:960]
        nsc = mg_sb.ap().rearrange("p a b -> p (a b)")[:, 0:32]
        rcol = mg_sb.ap().rearrange("p a b -> p (a b)")[:, 48:50]
        smw = [(s_mm, s_mm.n), (s_se, s_se.n), (s_ve, s_ve.n)]
        k.dma("sync", nsc, norm_scale, inc=s_c4, wait=smw)
        s_vl = sem("vl")
        s_ml = sem("ml")
        s_zl = sem("zl")
        s_cb = [sem("cb0"), sem("cb1")]
        s_cbr = sem("cbr")
        s_dr = sem("dr")
        s_ur = sem("ur")
        s_ot = sem("ot")
        marks = {"amm": {}, "yev": {}, "scan": {}, "tp": {}, "gy": {},
                 "zl": {}, "sq": {}, "sm": {}, "ml": {}}
        gy_fm = v_fm
        for half in range(2):
            mlw = [(s_ve, s_ve.n), (s_se, s_se.n)] if half == 0 else \
                  [(s_ot, s_ot.n), (s_ve, s_ve.n)]
            k.dma("sync", masklv, maskl_in, inc=s_c5, wait=mlw)
            k.dma("sync", drpDv, d_rep, inc=s_c5)
            mklm = (s_c5, s_c5.n)
            for p in range(4):
                ga = half * 8 + 2 * p
                gb = ga + 1
                sla = slice(ga * 64, ga * 64 + 64)
                slb = slice(gb * 64, gb * 64 + 64)
                slab = slice(ga * 64, ga * 64 + 128)
                pp = half * 4 + p
                if pp == 0:
                    vw = [hin_done, (s_mm, s_mm.n), (s_cb[0], s_cb[0].n)]
                elif p == 0:
                    vw = [(s_mm, s_mm.n), (s_ve, s_ve.n)]
                else:
                    vw = [(s_mm, marks["amm"][pp - 1]),
                          (s_ve, marks["yev"][pp - 1]),
                          (s_se, marks["sq"].get(pp - 1, 0))]
                k.dma("sync", vtk2,
                      vtok_sp.rearrange("c s d -> (c s) d")[ga * 64:ga * 64 + 128, :],
                      inc=s_vl, wait=vw)
                mw = [hin_done] if pp == 0 else \
                     [(s_ve, marks["scan"][pp - 1]), (s_mm, marks["tp"].get(pp - 1, 0)),
                      (s_se, marks["sq"].get(pp - 1, 0))]
                k.dma("sync", mrgA, m_sp[ga], inc=s_ml, wait=mw)
                zw = [(s_se, marks["zl"][pp - 1]),
                      (s_ve, marks["gy"][(pp - 1, 31)])] if pp > 0 else \
                     [(s_mm, s_mm.n), hin_done]
                k.dma("gpsimd", zb2.rearrange("p (m t) -> p m t", m=32),
                      z_sp[:, :, ga * 64:ga * 64 + 128].rearrange("m p t -> p m t"),
                      inc=s_zl, wait=zw)
                cw = [(s_cb[p % 2], s_cb[p % 2].n)]
                k.dma("gpsimd", csrow[p % 2, 0], cs_fm[:, sla],
                      inc=s_cb[p % 2], wait=cw)
                k.dma("gpsimd", csrow[p % 2, 1], cs_fm[:, slb],
                      inc=s_cb[p % 2])
                k.dma("gpsimd", totrow[p % 2, 0],
                      cs_fm[:, ga * 64 + 63:ga * 64 + 64], inc=s_cb[p % 2])
                k.dma("gpsimd", totrow[p % 2, 1],
                      cs_fm[:, gb * 64 + 63:gb * 64 + 64], inc=s_cb[p % 2])
                rw = [(s_cb[p % 2], s_cb[p % 2].n)]
                if pp > 0:
                    rw.append((s_mm, marks["amm"][pp - 1]))
                    rw.append((s_ve, marks["yev"][pp - 1]))
                k.dma("gpsimd", att2[0:64, :],
                      csrow[p % 2, 0:1].rearrange('a b -> (a b)')[None, :].broadcast_to([64, NH * S]) if False else csrow[p % 2, 0:1, :].rearrange('a f -> a f').broadcast_to([64, NH * S]),
                      inc=s_cbr, wait=rw)
                k.dma("gpsimd", att2[64:128, :],
                      csrow[p % 2, 1:2, :].broadcast_to([64, NH * S]), inc=s_cbr)
                csb_ld = (s_cbr, s_cbr.n)
                sw = [(s_ve, s_ve.n)]
                daw = [(s_cb[p % 2], s_cb[p % 2].n),
                       (s_ve, marks["scan"].get(pp - 1, 0))]
                if p == 0:
                    daw += [(s_ot, s_ot.n), (s_ve, s_ve.n), (s_ur, s_ur.n)]
                k.dma("gpsimd", drepA,
                      totrow[p % 2, 0:1, :].broadcast_to([64, NH]),
                      inc=s_dr, wait=daw)
                k.dma("gpsimd", drepB,
                      totrow[p % 2, 1:2, :].broadcast_to([64, NH]), inc=s_dr)
                tpw = sw + [(s_se, marks["sm"].get(pp - 1, 0))]
                k.trans(ptr[1][:, 0:64], cs_fm[:, slab], ident[0:64, 0:64],
                        wait=tpw, inc=s_mm)
                k.mm(pb[0][0:64, 0:64], B_fm[:, sla], C_fm[:, sla],
                     start=True, stop=True)
                k.mm(pb[2][64:128, 0:64], B_fm[:, slb], C_fm[:, slb],
                     start=True, stop=True, inc=s_mm)
                k.act(drepA, drepA, AF.Exp,
                      wait=[s_mm, (s_dr, s_dr.n), (s_ve, s_ve.n)])
                k.act(drepB, drepB, AF.Exp)
                k.act(cstk2[:], ptr[1][:, 0:64], AF.Copy)
                k.act(ecs2[:], ptr[1][:, 0:64], AF.Exp)
                k.act(gm2[0:64, :], pb[0][0:64, 0:64], AF.Copy)
                k.act(gm2[64:128, :], pb[2][64:128, 0:64], AF.Copy, inc=s_se)
                marks["sm"][pp] = s_se.n
                a3 = att2.rearrange("t (h s) -> t h s", h=NH)
                k.v("tensor_tensor", a3, a3,
                    masklv[:, None, :].broadcast_to([128, NH, S]), ALU.add,
                    wait=[csb_ld, mklm])
                k.v("tensor_tensor", a3, a3,
                    cstk2[:, :, None].broadcast_to([128, NH, S]), ALU.subtract,
                    wait=[(s_se, marks["sm"][pp])])
                k.vdrain(s_ve)
                k.act(att2, att2, AF.Exp, wait=[s_ve], inc=s_se)
                k.v("tensor_tensor", a3, a3,
                    gm2[:, None, :].broadcast_to([128, NH, S]), ALU.mult,
                    wait=[s_se])
                k.vdrain(s_ve)
                att_done = (s_ve, s_ve.n)
                if "att" in debug and pp == 0:
                    k.dma("sync", att_sp, att2, inc=s_sp, wait=[att_done])
                    k.dma("sync", vt2_sp, vtk2, inc=s_sp,
                          wait=[(s_vl, s_vl.n)])
                    k.dma("sync", hin_sp, hbuf, inc=s_sp, wait=[hin_done])
                    k.dma("sync", sml_sp, lnf[:, 0:1152], inc=s_sp,
                          wait=[(s_se, s_se.n), (s_ve, s_ve.n)])
                for q in range(4):
                    qs = q * 1024
                    for bh in range(2):
                        qq = qs + bh * 512
                        w0 = None
                        if q == 0 and bh == 0:
                            w0 = [att_done, (s_vl, s_vl.n), (s_ml, s_ml.n),
                                  (s_se, marks["sm"][pp])]
                            if pp > 0:
                                w0 = w0 + [(s_ve, marks["yev"][pp - 1])]
                        elif bh == 0 and q > 0:
                            w0 = [(s_ve, marks["yev"][(pp, q - 1)])]
                        k.mm(pb[bh][0:64, 0:512], C_fm[:, sla],
                             hbuf[:, qq:qq + 512], start=True, stop=True,
                             wait=w0, inc=s_mm if bh == 1 else None)
                    interE = s_mm.n
                    for bh in range(2):
                        for hh in range(8):
                            hid = q * 16 + bh * 8 + hh
                            k.mm(pb[4 + bh][0:64, hh * 64:hh * 64 + 64],
                                 att2[0:64, hid * 64:hid * 64 + 64],
                                 vtk2[0:64, hid * 64:hid * 64 + 64],
                                 start=(hh == 0), stop=(hh == 7))
                            k.mm(ptr[bh][64:128, hh * 64:hh * 64 + 64],
                                 att2[64:128, hid * 64:hid * 64 + 64],
                                 vtk2[64:128, hid * 64:hid * 64 + 64],
                                 start=(hh == 0), stop=(hh == 7),
                                 inc=s_mm if hh == 7 else None)
                    intraM = s_mm.n
                    k.v("tensor_tensor",
                        hbuf[:, qs:qs + 1024].rearrange("n (h p) -> n h p", h=16),
                        hbuf[:, qs:qs + 1024].rearrange("n (h p) -> n h p", h=16),
                        drepA[:, q * 16:q * 16 + 16][:, :, None]
                        .broadcast_to([64, 16, 64]), ALU.mult,
                        wait=[(s_mm, interE), (s_ml, s_ml.n),
                              (s_se, marks["sm"][pp])])
                    k.v("tensor_add", hbuf[:, qs:qs + 1024],
                        hbuf[:, qs:qs + 1024], mrgA[:, qs:qs + 1024])
                    k.vdrain(s_ve)
                    scanE = s_ve.n
                    k.dma("sync", mrgA[:, qs:qs + 1024],
                          m_sp[gb, :, qs:qs + 1024], inc=s_ml,
                          wait=[(s_ve, scanE)])
                    for bh in range(2):
                        qq = qs + bh * 512
                        k.mm(pb[2 + bh][64:128, 0:512], C_fm[:, slb],
                             hbuf[:, qq:qq + 512], start=True, stop=True,
                             wait=[(s_ve, scanE)] if bh == 0 else None,
                             inc=s_mm if bh == 1 else None)
                    k.v("tensor_tensor",
                        hbuf[:, qs:qs + 1024].rearrange("n (h p) -> n h p", h=16),
                        hbuf[:, qs:qs + 1024].rearrange("n (h p) -> n h p", h=16),
                        drepB[:, q * 16:q * 16 + 16][:, :, None]
                        .broadcast_to([64, 16, 64]), ALU.mult,
                        wait=[(s_mm, s_mm.n), (s_ml, s_ml.n)])
                    k.v("tensor_add", hbuf[:, qs:qs + 1024],
                        hbuf[:, qs:qs + 1024], mrgA[:, qs:qs + 1024])
                    k.vdrain(s_ve)
                    marks["scan"][pp] = s_ve.n
                    for bh in range(2):
                        qq = qs + bh * 512
                        hsl = slice(q * 16 + bh * 8, q * 16 + bh * 8 + 8)
                        for half_rows, pi, pa in ((slice(0, 64), pb[bh], pb[4 + bh]),
                                                  (slice(64, 128), pb[2 + bh], ptr[bh])):
                            dst = mrgY[half_rows, qq:qq + 512]
                            tq = tmpq[half_rows, :]
                            k.v("tensor_tensor",
                                dst.rearrange("t (h p) -> t h p", h=8),
                                pi[half_rows, 0:512].rearrange("t (h p) -> t h p", h=8),
                                ecs2[half_rows, hsl][:, :, None]
                                .broadcast_to([64, 8, 64]), ALU.mult,
                                wait=[(s_mm, s_mm.n), (s_ve, marks["scan"][pp]),
                                      (s_c5, s_c5.n)])
                            k.v("tensor_add", dst, dst, pa[half_rows, 0:512],
                                inc=s_ve)
                    marks["yev"][(pp, q)] = s_ve.n
                k.v("tensor_tensor",
                    att2.rearrange("t (h p) -> t h p", h=64),
                    vtk2.rearrange("t (h p) -> t h p", h=64),
                    drpDv[:, :, None].broadcast_to([128, 64, 64]), ALU.mult,
                    wait=[(s_mm, s_mm.n), (s_ve, marks["yev"][(pp, 3)]),
                          (s_c5, s_c5.n)])
                k.v("tensor_add", mrgY, mrgY, att2)
                k.vdrain(s_ve)
                marks["amm"][pp] = s_mm.n
                marks["yev"][pp] = s_ve.n
                marks["ml"][pp] = s_ml.n
                if "y" in debug:
                    k.dma("sync", y_sp[ga], mrgY[0:64, :], inc=s_sp,
                          wait=[(s_ve, s_ve.n)])
                    k.dma("sync", y_sp[gb], mrgY[64:128, :], inc=s_sp)
                k.act(vtk2, mrgY, AF.Square, accum_out=rcol[:, 0:1],
                      wait=[(s_ve, marks["yev"][pp]), (s_se, s_se.n),
                            (s_cbr, s_cbr.n)], inc=s_se)
                marks["sq"][pp] = s_se.n
                k.v("tensor_scalar", rcol[:, 0:1], rcol[:, 0:1], 1.0 / DI,
                    1e-6, ALU.mult, op1=ALU.add, wait=[s_se])
                k.vdrain(s_ve)
                k.act(rcol[:, 0:1], rcol[:, 0:1], AF.Sqrt, wait=[s_ve],
                      inc=s_se)
                k.v("reciprocal", rcol[:, 0:1], rcol[:, 0:1], wait=[s_se])
                k.vdrain(s_ve)
                k.v("tensor_scalar", mrgY, mrgY, rcol[:, 0:1], None, ALU.mult)
                k.vdrain(s_ve)
                yrms = (s_ve, s_ve.n)
                k.act(zb2, zb2, AF.Silu, wait=[(s_zl, s_zl.n)], inc=s_se)
                marks["zl"][pp] = s_se.n
                for m in range(32):
                    pslot = pb[m % 4]
                    tw = [yrms, (s_se, marks["zl"][pp])]
                    if (pp, m - 4) in marks["gy"]:
                        tw.append((s_ve, marks["gy"][(pp, m - 4)]))
                    k.trans(pslot[:, 0:128], mrgY[:, m * 128:(m + 1) * 128],
                            ident[:], wait=tw, inc=s_mm)
                    k.v("scalar_tensor_tensor",
                        gy_fm[:, m, 2 * p * 64:2 * p * 64 + 128],
                        zb2.rearrange("x (m t) -> x m t", m=32)[:, m],
                        nsc[:, m:m + 1], pslot[:, 0:128], ALU.mult,
                        ALU.mult, wait=[s_mm, (s_c4, s_c4.n)], inc=s_ve)
                    marks["gy"][(pp, m)] = s_ve.n
                marks["tp"][pp] = s_mm.n
            gy_done = (s_ve, s_ve.n)
            p6km = []
            p6_evac = gy_done
            for tpair in range(2):
                for kd in range(32):
                    slot = kd % 2
                    if len(p6km) >= 2:
                        bw = [(s_mm, p6km[-2])]
                    elif half == 0 and tpair == 0:
                        bw = [gy_done, (s_vl, s_vl.n)]
                    else:
                        bw = [gy_done]
                    k.dma("sync", wob[slot], w_out[kd * 128:(kd + 1) * 128, :],
                          inc=s_wb[slot], wait=bw)
                    for tp in range(2):
                        tt = tpair * 2 + tp
                        for q in range(4):
                            pk = (pb + ptr)[tp * 4 + q]
                            mmw = None
                            if tp == 0 and q == 0:
                                mmw = [(s_wb[slot], s_wb[slot].n)]
                                if kd == 0:
                                    mmw.append(p6_evac)
                            k.mm(pk[:, 0:512],
                                 gy_fm[:, kd, tt * 128:(tt + 1) * 128],
                                 wob[slot][:, q * 512:(q + 1) * 512],
                                 start=(kd == 0), stop=(kd == 31),
                                 wait=mmw,
                                 inc=s_mm if (tp == 1 and q == 3) else None)
                    p6km.append(s_mm.n)
                for tp in range(2):
                    tt = tpair * 2 + tp
                    t0 = half * 512 + tt * 128
                    uw = [(s_ot, s_ot.n)]
                    if tpair == 0 and tp == 0:
                        uw += [(s_ve, s_ve.n), (s_se, s_se.n),
                               (s_dr, s_dr.n), (s_cbr, s_cbr.n),
                               (s_c5, s_c5.n)]
                    k.dma("sync", lntmp[:], u[3 + t0:3 + t0 + 128, :],
                          inc=s_ur, wait=uw)
                    for q in range(4):
                        pk = (pb + ptr)[tp * 4 + q]
                        k.v("tensor_add", lntmp[:, q * 512:(q + 1) * 512],
                            lntmp[:, q * 512:(q + 1) * 512], pk[:, 0:512],
                            wait=[(s_mm, p6km[-1]), (s_ur, s_ur.n)]
                            if q == 0 else None)
                    k.vdrain(s_ve)
                    k.dma("sync", out[t0:t0 + 128, :], lntmp[:], inc=s_ot,
                          wait=[(s_ve, s_ve.n)])
                p6_evac = (s_ve, s_ve.n)
        t8 = lntmp.ap()[0:8, :]
        ptot_sb = ustage.ap()[0:8].rearrange("p a b -> p (a b)")[:, 0:D]
        c0v = mg_sb.ap()[0:8].rearrange("p a b -> p (a b)")[:, 50:51]
        k.dma("sync", t8, out[0:8, :], inc=s_ur,
              wait=[(s_ot, s_ot.n), (s_cc, 3)])
        k.dma("sync", ptot_sb, p_tot, inc=s_ur,
              wait=[(s_ve, s_ve.n), (s_mm, s_mm.n)])
        k.dma("sync", c0v, core0[0:8, 0:1], inc=s_ur)
        k.v("scalar_tensor_tensor", t8, ptot_sb, c0v, t8,
            ALU.mult, ALU.add, wait=[(s_ur, s_ur.n)])
        k.vdrain(s_ve)
        k.dma("sync", out[0:8, :], t8, inc=s_ot, wait=[(s_ve, s_ve.n)])
        k.op("sync", lambda h: h.nop(),
             wait=[s_ot, s_sp, s_vt, s_m, s_xnr, s_cc, s_vl, s_ml, s_zl, s_ft, s_pb, s_c2, s_c3, s_c4, s_c5, s_dr,
                   s_cbr, s_ur, s_gld] + s_zs + s_u + s_wb + s_f + s_cb +
                  [s_const])

        k.emit()
    return nc


def _prep_inputs(L0_attn_u, L0_attn_in_proj_w, L0_attn_conv_weight,
                 L0_attn_conv_bias, L0_attn_A_log, L0_attn_dt_bias,
                 L0_attn_D_param, L0_attn_norm_scale, L0_attn_out_proj_w,
                 L0_ffn_w1, L0_ffn_w3, L0_ffn_w2, L0_ffn_wr):
    u = np.ascontiguousarray(np.asarray(L0_attn_u, np.float32).reshape(8192, D))
    upad = np.concatenate([np.zeros((HALO, D), np.float32), u], 0)
    csmask = np.ones((NH, TOK), np.float32)
    csmask[:, ::S] = 0.0
    shared = {
        "w_in": np.ascontiguousarray(L0_attn_in_proj_w, np.float32),
        "conv_w": np.ascontiguousarray(np.transpose(
            np.asarray(L0_attn_conv_weight, np.float32).reshape(4, 32, 128),
            (2, 1, 0))),
        "conv_b": np.ascontiguousarray(
            np.asarray(L0_attn_conv_bias, np.float32).reshape(32, 128).T),
        "a_log": np.asarray(L0_attn_A_log, np.float32).reshape(NH, 1),
        "dt_bias": np.asarray(L0_attn_dt_bias, np.float32).reshape(NH, 1),
        "d_rep": np.tile(np.asarray(L0_attn_D_param, np.float32).reshape(1, NH),
                         (128, 1)),
        "norm_scale": np.ascontiguousarray(
            np.asarray(L0_attn_norm_scale, np.float32).reshape(32, 128).T),
        "w_out": np.ascontiguousarray(L0_attn_out_proj_w, np.float32),
        "w1t": np.ascontiguousarray(np.asarray(L0_ffn_w1, np.float32).T),
        "w3t": np.ascontiguousarray(np.asarray(L0_ffn_w3, np.float32).T),
        "w2t": np.ascontiguousarray(np.asarray(L0_ffn_w2, np.float32).T),
        "wrt": np.ascontiguousarray(np.asarray(L0_ffn_wr, np.float32).T
                                    .reshape(16, 128, E).transpose(1, 0, 2)
                                    .reshape(128, 16 * E)),
        "ident": np.eye(128, dtype=np.float32),
        "csmask": csmask,
        "maskl": np.tile(np.where(
            np.arange(S)[None, :] >= np.arange(S)[:, None],
            0.0, -1e30).astype(np.float32), (2, 1)),
    }
    in_maps = []
    for c in range(NCORES):
        m = dict(shared)
        m["u"] = np.ascontiguousarray(upad[c * TOK:c * TOK + TOKH, :])
        cmb = np.zeros((64, NCORES), np.float32)
        cmb[:, :c] = 1.0
        m["combmask"] = cmb
        m["combmaskinv"] = (1.0 - cmb)
        m["core0"] = np.full((128, 1), 1.0 if c == 0 else 0.0, np.float32)
        in_maps.append(m)
    return in_maps


def kernel(**inputs):
    in_maps = _prep_inputs(**inputs)
    nc = build_program()
    res = run_bass_kernel_spmd(nc, in_maps, core_ids=list(range(NCORES)))
    return np.concatenate([r["out"] for r in res.results], 0).reshape(1, 8192, D)
